# version 08
# baseline (speedup 1.0000x reference)
import numpy as np
import ml_dtypes
from contextlib import ExitStack

import concourse.bass as bass
import concourse.mybir as mybir
from concourse.bass_utils import run_bass_kernel_spmd

F32 = mybir.dt.float32
BF16 = mybir.dt.bfloat16
AF = mybir.ActivationFunctionType
ALU = mybir.AluOpType
AX = mybir.AxisListType
BFD = ml_dtypes.bfloat16

P = 128
DM = 1024
NCH = 8
EXT = 2176
TC = 2048
QB = [(i * 112, 112) for i in range(18)] + [(2016, 32)]
NQB = len(QB)
NTD = 16
NTE = 8
NEXP = 10
LN_EPS = 1e-5


def build_nc(stage="full", nqb=NQB, bstop=99, actload=True, bcmemset=False):
    nc = bass.Bass("TRN2", target_bir_lowering=False)

    xin = nc.dram_tensor("xin", [DM, EXT], F32, kind="ExternalInput")
    qkvT_d = nc.dram_tensor("qkvT", [DM, 3 * DM], F32, kind="ExternalInput")
    outwT_d = nc.dram_tensor("outwT", [DM, DM], F32, kind="ExternalInput")
    qkvb_d = nc.dram_tensor("qkvb", [1, 3 * DM], F32, kind="ExternalInput")
    outb_d = nc.dram_tensor("outb", [1, DM], F32, kind="ExternalInput")
    gwT_d = nc.dram_tensor("gwT", [DM, 8], F32, kind="ExternalInput")
    lbbc_d = nc.dram_tensor("lbbc", [P, 8], F32, kind="ExternalInput")
    egwT_d = nc.dram_tensor("egwT", [8, DM, 512], F32, kind="ExternalInput")
    euwT_d = nc.dram_tensor("euwT", [8, DM, 512], F32, kind="ExternalInput")
    edwT_d = nc.dram_tensor("edwT", [8, 512, DM], F32, kind="ExternalInput")
    sgwT_d = nc.dram_tensor("sgwT", [DM, DM], F32, kind="ExternalInput")
    suwT_d = nc.dram_tensor("suwT", [DM, DM], F32, kind="ExternalInput")
    sdwT_d = nc.dram_tensor("sdwT", [DM, DM], F32, kind="ExternalInput")
    idbf_d = nc.dram_tensor("idbf", [P, P], BF16, kind="ExternalInput")
    idf_d = nc.dram_tensor("idf", [P, P], F32, kind="ExternalInput")
    selpk_d = nc.dram_tensor("selpk", [P, P], BF16, kind="ExternalInput")
    mfirst_d = nc.dram_tensor("mfirst", [112, P], BF16, kind="ExternalInput")
    mmid_d = nc.dram_tensor("mmid", [112, P], BF16, kind="ExternalInput")
    mlast_d = nc.dram_tensor("mlast", [32, P], BF16, kind="ExternalInput")

    y_d = nc.dram_tensor("y", [DM, TC], F32, kind="ExternalOutput")
    ydbg_d = nc.dram_tensor("ydbg", [P, 18048], F32, kind="ExternalOutput")
    ydbgb_d = nc.dram_tensor("ydbgb", [P, NCH * EXT], BF16, kind="ExternalOutput")

    attout_d = nc.dram_tensor("attout", [DM, TC], F32, kind="Internal")
    rnscr_d = nc.dram_tensor("rnscr", [4, 16, P], BF16, kind="Internal")
    abscr_d = nc.dram_tensor("abscr", [2, 2, 512], BF16, kind="Internal")
    coefscr_d = nc.dram_tensor("coefscr", [8, TC], BF16, kind="Internal")
    ccin_d = nc.dram_tensor("ccin", [16, 1], F32, kind="Internal")
    ccout_d = nc.dram_tensor("ccout", [16, 1], F32, kind="Internal")

    st0 = ExitStack()
    with st0:
        E0 = st0.enter_context

        sems = {}
        cnt = {}
        for s in ("dS", "dG", "sP", "sV", "sA", "sG", "cc",
                  "xA0", "xA1", "bq", "bk", "xd0", "xd1", "bc0", "bc1", "rw", "wst"):
            sems[s] = E0(nc.semaphore(s))
            cnt[s] = 0

        def inc(instr, s, n=1):
            if instr is not None:
                instr.then_inc(sems[s], n)
            cnt[s] += n
            return cnt[s]

        def wt(eng, s, v):
            if v > 0:
                eng.wait_ge(sems[s], v)

        class _DummyInstr:
            def then_inc(self, *a, **k):
                return self

        class _Dummy:
            def __getattr__(self, name):
                def f(*a, **k):
                    return None
                return f

        _DUM = _Dummy()

        class Sched:

            ORDER = ("sync", "gpsimd", "scalar", "tensor", "vector")

            def __init__(self):
                self.fns = {n: [] for n in self.ORDER}
                self.snap = dict(cnt)

            def add(self, eng, fn):
                self.fns[eng].append(fn)
                fn(_DUM)

            def flush(self, blk):
                end = dict(cnt)
                for k, v in self.snap.items():
                    cnt[k] = v
                for eng in self.ORDER:
                    fns = self.fns[eng]
                    if fns:
                        getattr(blk, eng)(lambda e, fns=fns: [f(e) for f in fns])
                for k, v in end.items():
                    assert cnt[k] == v, (k, cnt[k], v)

        def act_load(se, set_name):
            from concourse.hw_specs import get_activation_tables
            for idx, nm in enumerate(get_activation_tables(nc.m.arch)):
                if nm == set_name:
                    se.add_instruction(mybir.InstLoadActFuncSet(
                        name=nc.get_next_instruction_name(),
                        act_func_set_id=idx, ins=[], outs=[]))
                    return
            raise ValueError(set_name)

        xln = E0(nc.sbuf_tensor("xln", [P, NCH * EXT], BF16))
        wbuf = E0(nc.sbuf_tensor("wbuf", [P, 24576], BF16))
        wstage = E0(nc.sbuf_tensor("wstage", [P, 4096], F32))
        gw_sb = E0(nc.sbuf_tensor("gw_sb", [P, 64], BF16))
        lb_bc = E0(nc.sbuf_tensor("lb_bc", [P, 8], F32))
        id_bf = E0(nc.sbuf_tensor("id_bf", [P, P], BF16))
        id_f = E0(nc.sbuf_tensor("id_f", [P, P], F32))
        selpk = E0(nc.sbuf_tensor("selpk_s", [P, P], BF16))
        m_first = E0(nc.sbuf_tensor("mfirst_s", [112, P], BF16))
        m_mid = E0(nc.sbuf_tensor("mmid_s", [112, P], BF16))
        m_last = E0(nc.sbuf_tensor("mlast_s", [32, P], BF16))
        ones_rb = E0(nc.sbuf_tensor("ones_rb", [1, 512], BF16))
        ones_cb = E0(nc.sbuf_tensor("ones_cb", [P, 1], BF16))
        ones_cf = E0(nc.sbuf_tensor("ones_cf", [P, 1], F32))
        eps_c = E0(nc.sbuf_tensor("eps_c", [P, 1], F32))

        def xslc(kc, t0, n):
            return xln[:, kc * EXT + t0: kc * EXT + t0 + n]

        with ExitStack() as stA:
            EA = stA.enter_context
            xst = EA(nc.sbuf_tensor("xst", [P, 2 * 4096], F32))
            xbt = EA(nc.sbuf_tensor("xbt", [P, 2 * 4096], BF16))
            sqt = EA(nc.sbuf_tensor("sqt", [P, 4096], BF16))
            tmpA = EA(nc.sbuf_tensor("tmpA", [P, 4096], BF16))
            a_bc = EA(nc.sbuf_tensor("a_bc", [P, 2 * 512], BF16))
            b_bc = EA(nc.sbuf_tensor("b_bc", [P, 2 * 512], BF16))
            mrow = EA(nc.sbuf_tensor("mrow", [1, 512], F32))
            msq = EA(nc.sbuf_tensor("msq", [1, 512], F32))
            varr = EA(nc.sbuf_tensor("varr", [1, 512], F32))
            lnr = EA(nc.sbuf_tensor("lnr", [1, 512], F32))
            arow = EA(nc.sbuf_tensor("arow", [1, 512], BF16))
            brow = EA(nc.sbuf_tensor("brow", [1, 512], BF16))
            sum_ps = EA(nc.psum_tensor("sum_ps", [1, 512], F32))
            sq_ps = EA(nc.psum_tensor("sq_ps", [1, 512], F32))
            ab_ps = EA(nc.psum_tensor("ab_ps", [P, 512], F32))
            bb_ps = EA(nc.psum_tensor("bb_ps", [P, 512], F32))

            tilesA = [(0, 512), (512, 512), (1024, 512), (1536, 512), (2048, 128)]
            M = {}

            with nc.Block("phA", no_gpsimd_drain=True) as blk:
                SC = Sched()
                actload and SC.add("scalar", lambda se: act_load(se, "natural_log_exp_and_others"))
                def gp_w(gp):
                    for dst, src in ((id_bf, idbf_d), (id_f, idf_d), (selpk, selpk_d),
                                     (m_first, mfirst_d), (m_mid, mmid_d), (m_last, mlast_d),
                                     (lb_bc, lbbc_d)):
                        inc(gp.dma_start(out=dst[:, :], in_=src[:, :]), "dG", 16)
                    for kc in range(NCH):
                        d = inc(gp.dma_start(out=wstage[:, 0:3072],
                                             in_=qkvT_d[kc * P:(kc + 1) * P, :]), "dG", 16)
                        gp.wait_ge(sems["dG"], d)
                        inc(gp.tensor_copy(wbuf[:, kc * 3072:(kc + 1) * 3072],
                                           wstage[:, 0:3072]), "sG")
                    for kc in range(NCH):
                        d = inc(gp.dma_start(out=wstage[:, kc * 8: kc * 8 + 8],
                                             in_=gwT_d[kc * P:(kc + 1) * P, :]), "dG", 16)
                    gp.wait_ge(sems["dG"], d)
                    M["gwcast"] = inc(gp.tensor_copy(gw_sb[:, :], wstage[:, 0:64]), "sG")
                SC.add("gpsimd", gp_w)

                def ve_c(ve):
                    inc(ve.memset(ones_rb[:, :], 1.0), "sV")
                    inc(ve.memset(ones_cb[:, :], 1.0), "sV")
                    inc(ve.memset(ones_cf[:, :], 1.0), "sV")
                    M["ones"] = inc(ve.memset(eps_c[:, :], LN_EPS), "sV")
                SC.add("vector", ve_c)

                for it, (c0, nw) in enumerate(tilesA):
                    par = it % 2
                    po = par * 4096
                    ao = par * 512

                    def sy_x(sy, it=it, c0=c0, nw=nw, po=po):
                        if it >= 2:
                            wt(sy, "sV", M[("xln", it - 2)])
                        sx = "xA%d" % (it % 2)
                        for kc in range(NCH):
                            inc(sy.dma_start(out=xst[:, po + kc * 512: po + kc * 512 + nw],
                                             in_=xin[kc * P:(kc + 1) * P, c0:c0 + nw]), sx, 16)
                        M[("x", it)] = cnt[sx]
                    SC.add("sync", sy_x)

                    def ve_cast(ve, it=it, nw=nw, po=po):
                        wt(ve, "xA%d" % (it % 2), M[("x", it)])
                        if it >= 2:
                            wt(ve, "sP", M[("stat", it - 2)])
                        src = xst[:, po:po + 4096].rearrange("p (c n) -> p c n", c=NCH)[:, :, 0:nw]
                        dst = xbt[:, po:po + 4096].rearrange("p (c n) -> p c n", c=NCH)[:, :, 0:nw]
                        M[("cast", it)] = inc(ve.tensor_copy(dst, src), "sV")
                    SC.add("vector", ve_cast)

                    def se_sq(se, it=it, nw=nw, po=po):
                        wt(se, "sV", M[("cast", it)])
                        if it > 0:
                            wt(se, "sP", M[("stat", it - 1)])
                        src = xbt[:, po:po + 4096].rearrange("p (c n) -> p c n", c=NCH)[:, :, 0:nw]
                        dst = sqt[:, :].rearrange("p (c n) -> p c n", c=NCH)[:, :, 0:nw]
                        M[("sq", it)] = inc(se.activation(dst, src, AF.Square), "sA")
                    SC.add("scalar", se_sq)

                    def pe_stat(pe, it=it, nw=nw, po=po):
                        wt(pe, "sV", max(M[("cast", it)], M["ones"]))
                        wt(pe, "sA", M[("sq", it)])
                        if it > 0:
                            wt(pe, "sA", M[("rowm", it - 1)])
                            wt(pe, "sV", M[("rowv", it - 1)])
                        for kc in range(NCH):
                            pe.matmul(sum_ps[0:1, 0:nw], ones_cb[:, :],
                                      xbt[:, po + kc * 512: po + kc * 512 + nw],
                                      start=(kc == 0), stop=(kc == NCH - 1))
                        for kc in range(NCH):
                            mm = pe.matmul(sq_ps[0:1, 0:nw], ones_cb[:, :],
                                           sqt[:, kc * 512: kc * 512 + nw],
                                           start=(kc == 0), stop=(kc == NCH - 1))
                        M[("stat", it)] = inc(mm, "sP")
                    SC.add("tensor", pe_stat)

                    def se_row(se, it=it, nw=nw):
                        wt(se, "sP", M[("stat", it)])
                        if it > 0:
                            wt(se, "sP", M[("abmm", it - 1)])
                            wt(se, "sV", M[("rowb", it - 1)])
                        a1 = inc(se.activation(mrow[0:1, 0:nw], sum_ps[0:1, 0:nw],
                                               AF.Copy, bias=0.0, scale=1.0 / DM), "sA")
                        se.wait_ge(sems["sA"], a1)
                        M[("rowm", it)] = inc(se.activation(msq[0:1, 0:nw], mrow[0:1, 0:nw],
                                                            AF.Square), "sA")
                    SC.add("scalar", se_row)

                    def ve_row(ve, it=it, nw=nw):
                        wt(ve, "sA", M[("rowm", it)])
                        M[("rowv", it)] = inc(
                            ve.scalar_tensor_tensor(varr[0:1, 0:nw], sq_ps[0:1, 0:nw],
                                                    1.0 / DM, msq[0:1, 0:nw],
                                                    ALU.mult, ALU.subtract), "sV")
                    SC.add("vector", ve_row)

                    def se_row2(se, it=it, nw=nw):
                        wt(se, "sV", M[("rowv", it)])
                        a1 = inc(se.activation(lnr[0:1, 0:nw], varr[0:1, 0:nw],
                                               AF.Ln, bias=eps_c[0:1, 0:1], scale=1.0), "sA")
                        se.wait_ge(sems["sA"], a1)
                        M[("rowa", it)] = inc(se.activation(arow[0:1, 0:nw], lnr[0:1, 0:nw],
                                                            AF.Exp, bias=0.0, scale=-0.5), "sA")
                    SC.add("scalar", se_row2)

                    def ve_row2(ve, it=it, nw=nw):
                        wt(ve, "sA", M[("rowa", it)])
                        M[("rowb", it)] = inc(
                            ve.scalar_tensor_tensor(brow[0:1, 0:nw], mrow[0:1, 0:nw],
                                                    -1.0, arow[0:1, 0:nw],
                                                    ALU.mult, ALU.mult), "sV")
                    SC.add("vector", ve_row2)

                    def pe_ab(pe, it=it, nw=nw, par=par):
                        wt(pe, "sA", M[("rowa", it)])
                        wt(pe, "sV", M[("rowb", it)])
                        if it > 0:
                            wt(pe, "sA", M[("abev", it - 1)])
                        pe.matmul(ab_ps[:, 0:nw], ones_rb[0:1, 0:P], arow[0:1, 0:nw],
                                  start=True, stop=True)
                        mm = pe.matmul(bb_ps[:, 0:nw], ones_rb[0:1, 0:P], brow[0:1, 0:nw],
                                       start=True, stop=True)
                        M[("abmm", it)] = inc(mm, "sP")
                    SC.add("tensor", pe_ab)

                    def se_abev(se, it=it, nw=nw, par=par):
                        wt(se, "sP", M[("abmm", it)])
                        if it >= 2:
                            wt(se, "sV", M[("xln", it - 2)])
                        se.copy(a_bc[:, par * 512: par * 512 + nw], ab_ps[:, 0:nw])
                        M[("abev", it)] = inc(
                            se.copy(b_bc[:, par * 512: par * 512 + nw], bb_ps[:, 0:nw]), "sA")
                    SC.add("scalar", se_abev)

                    def ve_xln(ve, it=it, c0=c0, nw=nw, po=po, par=par):
                        wt(ve, "sA", M[("abev", it)])
                        xb3 = xbt[:, po:po + 4096].rearrange("p (c n) -> p c n", c=NCH)[:, :, 0:nw]
                        a3 = a_bc[:, par * 512: par * 512 + nw][:, None, :].broadcast_to([P, NCH, nw])
                        t3 = tmpA[:, :].rearrange("p (c n) -> p c n", c=NCH)[:, :, 0:nw]
                        v1 = inc(ve.tensor_tensor(t3, xb3, a3, ALU.mult), "sV")
                        ve.wait_ge(sems["sV"], v1)
                        o3 = xln[:, :].rearrange("p (c n) -> p c n", c=NCH)[:, :, c0:c0 + nw]
                        b3 = b_bc[:, par * 512: par * 512 + nw][:, None, :].broadcast_to([P, NCH, nw])
                        M[("xln", it)] = inc(ve.tensor_tensor(o3, t3, b3, ALU.add), "sV")
                    SC.add("vector", ve_xln)

                def sy_endA(sy):
                    wt(sy, "dS", cnt["dS"])
                SC.add("sync", sy_endA)

                def gp_endA(gp):
                    wt(gp, "dG", cnt["dG"])
                SC.add("gpsimd", gp_endA)
                SC.flush(blk)

        if stage == "xln":
            with ExitStack() as stX:
                xf = stX.enter_context(nc.sbuf_tensor("xf_dbg", [P, NCH * EXT], F32))
                with nc.Block("phDbg") as blk:
                    def ve_d(ve):
                        inc(ve.tensor_copy(xf[:, :], xln[:, :]), "sV")
                    SC.add("vector", ve_d)
                    def sy_d(sy):
                        wt(sy, "sV", cnt["sV"])
                        d = inc(sy.dma_start(out=ydbg_d[:, 0:NCH * EXT], in_=xf[:, :]), "dS", 16)
                        sy.wait_ge(sems["dS"], d)
                    SC.add("sync", sy_d)
            return nc

        if stage == "ln":
            with nc.Block("phOutL") as blk:
                def sy_l(sy):
                    d = inc(sy.dma_start(out=y_d[:, :], in_=attout_d[:, :]), "dS", 16)
                    sy.wait_ge(sems["dS"], d)
                SC.add("sync", sy_l)
            return nc

        with ExitStack() as stB:
            EB = stB.enter_context
            outwT_bf = EB(nc.sbuf_tensor("outwT_bf", [P, 8192], BF16))
            qkvb_bf = EB(nc.sbuf_tensor("qkvb_bf", [1, 3 * DM], BF16))
            outb_bf = EB(nc.sbuf_tensor("outb_bf", [1, DM], BF16))
            q_t = EB(nc.sbuf_tensor("q_t", [64, 2 * 1792], BF16))
            k_t = EB(nc.sbuf_tensor("k_t", [64, 2 * 2048], BF16))
            v_sb = EB(nc.sbuf_tensor("v_sb", [P, 2 * 1024], BF16))
            sqb = EB(nc.sbuf_tensor("sqb", [P, 1024], BF16))
            rn_ln = EB(nc.sbuf_tensor("rn_ln", [16, P], F32))
            rn_sb = EB(nc.sbuf_tensor("rn_sb", [16, P], BF16))
            bcrn_q = EB(nc.sbuf_tensor("bcrn_q", [64, 1792], BF16))
            bcrn_k = EB(nc.sbuf_tensor("bcrn_k", [64, 2048], BF16))
            e_p = EB(nc.sbuf_tensor("e_p", [112, 1024], BF16))
            p_m = EB(nc.sbuf_tensor("p_m", [112, 1024], BF16))
            p_n = EB(nc.sbuf_tensor("p_n", [112, 1024], BF16))
            sums = EB(nc.sbuf_tensor("sums", [112, 16], F32))
            rinv = EB(nc.sbuf_tensor("rinv", [112, 16], F32))
            pT_sb = EB(nc.sbuf_tensor("pT_sb", [P, 1792], BF16))
            att = EB(nc.sbuf_tensor("att", [P, 896], BF16))
            opj = EB(nc.sbuf_tensor("opj", [P, 1024], F32))
            s_av = EB(nc.psum_tensor("s_av", [P, 1024], F32))
            psB = EB(nc.psum_tensor("psB", [P, 1024], F32))
            pT_ps = EB(nc.psum_tensor("pT_ps", [P, 1024], BF16))
            av_ps = EB(nc.psum_tensor("av_ps", [64, 1024], F32))
            op_ps = EB(nc.psum_tensor("op_ps", [P, 512], F32))

            M = {("unit", -1, 0): 0, ("unit", -1, 1): 0}

            with nc.Block("phB", no_gpsimd_drain=True) as blk:
                SC = Sched()
                actload and SC.add("scalar", lambda se: act_load(se, "natural_log_exp_and_others"))
                def gp_ow(gp):
                    d = inc(gp.dma_start(out=wstage[0:1, 0:3 * DM], in_=qkvb_d[:, :]), "dG", 16)
                    gp.wait_ge(sems["dG"], d)
                    M["qkvb"] = inc(gp.tensor_copy(qkvb_bf[:, :], wstage[0:1, 0:3 * DM]), "sG")
                    d = inc(gp.dma_start(out=wstage[0:1, 0:DM], in_=outb_d[:, :]), "dG", 16)
                    gp.wait_ge(sems["dG"], d)
                    M["outbb"] = inc(gp.tensor_copy(outb_bf[:, :], wstage[0:1, 0:DM]), "sG")
                    for kc in range(NCH):
                        d = inc(gp.dma_start(out=wstage[:, 0:DM],
                                             in_=outwT_d[kc * P:(kc + 1) * P, :]), "dG", 16)
                        gp.wait_ge(sems["dG"], d)
                        inc(gp.tensor_copy(outwT_bf[:, kc * DM:(kc + 1) * DM],
                                           wstage[:, 0:DM]), "sG")
                SC.add("gpsimd", gp_ow)
                M["outw"] = cnt["sG"]

                def emit_build(qb, bsub=9):
                    s_k, nq = QB[qb]
                    qc0 = 8 + s_k
                    sl = qb % 2
                    qo = sl * 1792
                    ko = sl * 2048

                    def pe_q(pe):
                        if qb > 0:
                            wt(pe, "sV", M[("vev", qb - 1)])
                        else:
                            wt(pe, "sG", M["outbb"])
                        for pc in range(NCH):
                            for kc in range(NCH):
                                pe.matmul(psB[:, pc * P: pc * P + nq],
                                          wbuf[:, kc * 3072 + pc * P: kc * 3072 + (pc + 1) * P],
                                          xslc(kc, qc0, nq), start=(kc == 0), stop=False)
                            mm = pe.matmul(psB[:, pc * P: pc * P + nq],
                                           qkvb_bf[0:1, pc * P:(pc + 1) * P],
                                           ones_rb[0:1, 0:nq], start=False, stop=True)
                        M[("qmm", qb)] = inc(mm, "sP")
                    SC.add("tensor", pe_q)

                    def se_qsq(se):
                        wt(se, "sP", M[("qmm", qb)])
                        if qb > 0:
                            wt(se, "sP", M[("kssq", qb - 1)])
                        M[("qsq", qb)] = inc(
                            se.activation(sqb[:, 0:1024], psB[:, 0:1024], AF.Square), "sA")
                    SC.add("scalar", se_qsq)

                    def ve_qev(ve):
                        wt(ve, "sP", M[("qmm", qb)])
                        wt(ve, "sA", M[("qsq", qb)])
                        if qb > 1:
                            wt(ve, "sP", M[("s", qb - 2, 1)])
                        src0 = psB[0:64, 0:1024].rearrange("p (c n) -> p c n", c=NCH)[:, :, 0:nq]
                        src1 = psB[64:128, 0:1024].rearrange("p (c n) -> p c n", c=NCH)[:, :, 0:nq]
                        dst0 = q_t[0:64, qo:qo + 896].rearrange("p (c n) -> p c n", c=NCH)[:, :, 0:nq]
                        dst1 = q_t[0:64, qo + 896:qo + 1792].rearrange("p (c n) -> p c n", c=NCH)[:, :, 0:nq]
                        inc(ve.tensor_copy(dst0, src0), "sV")
                        M[("qev", qb)] = inc(ve.tensor_copy(dst1, src1), "sV")
                    SC.add("vector", ve_qev)
                    if bsub < 2:
                        return

                    def pe_qssq(pe):
                        wt(pe, "sA", M[("qsq", qb)])
                        wt(pe, "sV", M[("qev", qb)])
                        for pc in range(NCH):
                            mm = pe.matmul(psB[0:16, 0:nq], selpk[:, pc * 16:(pc + 1) * 16],
                                           sqb[:, pc * P: pc * P + nq],
                                           start=(pc == 0), stop=(pc == NCH - 1))
                        M[("qssq", qb)] = inc(mm, "sP")
                    SC.add("tensor", pe_qssq)

                    def se_qrn(se):
                        wt(se, "sP", M[("qssq", qb)])
                        if qb > 0:
                            wt(se, "rw", M[("krnw", qb - 1)])
                        a1 = inc(se.activation(rn_ln[0:16, 0:nq], psB[0:16, 0:nq], AF.Ln), "sA")
                        se.wait_ge(sems["sA"], a1)
                        M[("qrn", qb)] = inc(
                            se.activation(rn_sb[0:16, 0:nq], rn_ln[0:16, 0:nq],
                                          AF.Exp, bias=0.0, scale=-0.5), "sA")
                    SC.add("scalar", se_qrn)

                    def sy_qrn(sy):
                        idx = sl * 2
                        wt(sy, "sA", M[("qrn", qb)])
                        if qb > 0:
                            wt(sy, "sV", M[("qn", qb - 1)])
                        if qb > 1:
                            wt(sy, "bq", M[("qrnb", qb - 2)])
                        d1 = inc(sy.dma_start(out=rnscr_d[idx, 0:16, 0:nq],
                                              in_=rn_sb[0:16, 0:nq]), "rw", 16)
                        M[("qrnw", qb)] = d1
                        sy.wait_ge(sems["rw"], d1)
                        srcq = rnscr_d[idx].rearrange("(pc two) k -> two pc k", two=2)
                        for rh in range(2):
                            bqv = bcrn_q[:, rh * 896: rh * 896 + 896].rearrange(
                                "p (c n) -> p c n", c=NCH)[:, :, 0:nq]
                            mq = inc(sy.dma_start(
                                out=bqv, in_=srcq[rh][:, 0:nq][None].broadcast_to([64, NCH, nq])), "bq", 16)
                        M[("qrnb", qb)] = mq
                    SC.add("sync", sy_qrn)

                    def ve_qn(ve):
                        wt(ve, "bq", M[("qrnb", qb)])
                        M[("qn", qb)] = inc(
                            ve.tensor_tensor(q_t[0:64, qo:qo + 1792], q_t[0:64, qo:qo + 1792],
                                             bcrn_q[0:64, :], ALU.mult), "sV")
                    SC.add("vector", ve_qn)
                    if bsub < 3:
                        return

                    def pe_k(pe):
                        wt(pe, "sA", M[("qrn", qb)])
                        wt(pe, "sV", M[("qev", qb)])
                        for pc in range(NCH):
                            for kc in range(NCH):
                                pe.matmul(psB[:, pc * P:(pc + 1) * P],
                                          wbuf[:, kc * 3072 + DM + pc * P: kc * 3072 + DM + (pc + 1) * P],
                                          xslc(kc, s_k, P), start=(kc == 0), stop=False)
                            mm = pe.matmul(psB[:, pc * P:(pc + 1) * P],
                                           qkvb_bf[0:1, DM + pc * P: DM + (pc + 1) * P],
                                           ones_rb[0:1, 0:P], start=False, stop=True)
                        M[("kmm", qb)] = inc(mm, "sP")
                    SC.add("tensor", pe_k)

                    def se_ksq(se):
                        wt(se, "sP", M[("kmm", qb)])
                        wt(se, "sP", M[("qssq", qb)])
                        M[("ksq", qb)] = inc(
                            se.activation(sqb[:, 0:1024], psB[:, 0:1024], AF.Square), "sA")
                    SC.add("scalar", se_ksq)

                    def ve_kev(ve):
                        wt(ve, "sP", M[("kmm", qb)])
                        wt(ve, "sA", M[("ksq", qb)])
                        if qb > 1:
                            wt(ve, "sP", M[("s", qb - 2, 1)])
                        inc(ve.tensor_copy(k_t[0:64, ko:ko + 1024], psB[0:64, 0:1024]), "sV")
                        M[("kev", qb)] = inc(
                            ve.tensor_copy(k_t[0:64, ko + 1024:ko + 2048], psB[64:128, 0:1024]), "sV")
                    SC.add("vector", ve_kev)
                    if bsub < 4:
                        return

                    def pe_kssq(pe):
                        wt(pe, "sA", M[("ksq", qb)])
                        wt(pe, "sV", M[("kev", qb)])
                        for pc in range(NCH):
                            mm = pe.matmul(psB[0:16, 0:P], selpk[:, pc * 16:(pc + 1) * 16],
                                           sqb[:, pc * P:(pc + 1) * P],
                                           start=(pc == 0), stop=(pc == NCH - 1))
                        M[("kssq", qb)] = inc(mm, "sP")
                    SC.add("tensor", pe_kssq)

                    def se_krn(se):
                        wt(se, "sP", M[("kssq", qb)])
                        wt(se, "rw", M[("qrnw", qb)])
                        a1 = inc(se.activation(rn_ln[0:16, 0:P], psB[0:16, 0:P], AF.Ln), "sA")
                        se.wait_ge(sems["sA"], a1)
                        M[("krn", qb)] = inc(
                            se.activation(rn_sb[0:16, 0:P], rn_ln[0:16, 0:P],
                                          AF.Exp, bias=0.0, scale=-0.5), "sA")
                    SC.add("scalar", se_krn)

                    def sy_krn(sy):
                        idx = sl * 2 + 1
                        wt(sy, "sA", M[("krn", qb)])
                        if qb > 0:
                            wt(sy, "sV", M[("kn", qb - 1)])
                        if qb > 1:
                            wt(sy, "bk", M[("krnb", qb - 2)])
                        d1 = inc(sy.dma_start(out=rnscr_d[idx, 0:16, 0:P],
                                              in_=rn_sb[0:16, 0:P]), "rw", 16)
                        M[("krnw", qb)] = d1
                        sy.wait_ge(sems["rw"], d1)
                        srck = rnscr_d[idx].rearrange("(pc two) k -> two pc k", two=2)
                        for rh in range(2):
                            bkv = bcrn_k[:, rh * 1024: rh * 1024 + 1024].rearrange(
                                "p (c n) -> p c n", c=NCH)
                            mk = inc(sy.dma_start(
                                out=bkv, in_=srck[rh][None].broadcast_to([64, NCH, P])), "bk", 16)
                        M[("krnb", qb)] = mk
                    SC.add("sync", sy_krn)

                    def ve_kn(ve):
                        wt(ve, "bk", M[("krnb", qb)])
                        M[("kn", qb)] = inc(
                            ve.tensor_tensor(k_t[0:64, ko:ko + 2048], k_t[0:64, ko:ko + 2048],
                                             bcrn_k[0:64, :], ALU.mult), "sV")
                    SC.add("vector", ve_kn)
                    if bsub < 5:
                        return

                    def pe_v(pe):
                        wt(pe, "sA", M[("krn", qb)])
                        wt(pe, "sV", M[("kev", qb)])
                        if qb > 1:
                            wt(pe, "sP", M[("av", qb - 2, 1)])
                        for vh in range(2):
                            for kc in range(NCH):
                                pe.matmul(psB[:, vh * 512:(vh + 1) * 512],
                                          xslc(kc, s_k, P),
                                          wbuf[:, kc * 3072 + 2048 + vh * 512: kc * 3072 + 2048 + (vh + 1) * 512],
                                          start=(kc == 0), stop=False)
                            mm = pe.matmul(psB[:, vh * 512:(vh + 1) * 512],
                                           ones_rb[0:1, 0:P],
                                           qkvb_bf[0:1, 2048 + vh * 512: 2048 + (vh + 1) * 512],
                                           start=False, stop=True)
                        M[("vmm", qb)] = inc(mm, "sP")
                    SC.add("tensor", pe_v)

                    def ve_vev(ve):
                        vo = sl * 1024
                        wt(ve, "sP", M[("vmm", qb)])
                        M[("vev", qb)] = inc(
                            ve.tensor_copy(v_sb[:, vo:vo + 1024], psB[:, 0:1024]), "sV")
                    SC.add("vector", ve_vev)

                def emit_scores(qb, half):
                    s_k, nq = QB[qb]
                    sl = qb % 2
                    qo = sl * 1792
                    ko = sl * 2048

                    def pe_s(pe):
                        wt(pe, "sV", M[("qn", qb)])
                        wt(pe, "sV", M[("kn", qb)])
                        prev = (qb, 0) if half == 1 else (qb - 1, 1)
                        wt(pe, "sA", M.get(("exp", prev[0], prev[1]), 0))
                        for hh in range(8):
                            h = half * 8 + hh
                            pc, rh = h // 2, h % 2
                            mm = pe.matmul(s_av[0:nq, hh * P:(hh + 1) * P],
                                           q_t[0:64, qo + rh * 896 + pc * 112: qo + rh * 896 + pc * 112 + nq],
                                           k_t[0:64, ko + rh * 1024 + pc * P: ko + rh * 1024 + (pc + 1) * P],
                                           start=True, stop=True)
                        M[("s", qb, half)] = inc(mm, "sP")
                    SC.add("tensor", pe_s)

                def emit_exp(qb, half):
                    s_k, nq = QB[qb]

                    def se_exp(se):
                        wt(se, "sP", M[("s", qb, half)])
                        prev = (qb, 0) if half == 1 else (qb - 1, 1)
                        wt(se, "sV", M.get(("pm", prev[0], prev[1]), 0))
                        M[("exp", qb, half)] = inc(
                            se.activation(e_p[0:nq, 0:1024], s_av[0:nq, 0:1024], AF.Exp), "sA")
                    SC.add("scalar", se_exp)

                def emit_softmax(qb, half, sub=6):
                    s_k, nq = QB[qb]
                    mask = m_first if qb == 0 else (m_last if qb == NQB - 1 else m_mid)
                    if sub < 2:
                        return

                    def ve_soft(ve):
                        wt(ve, "sA", M[("exp", qb, half)])
                        e3 = e_p[0:nq, :].rearrange("p (h j) -> p h j", h=8)
                        mk3 = mask[0:nq, :][:, None, :].broadcast_to([nq, 8, P])
                        pm3 = p_m[0:nq, :].rearrange("p (h j) -> p h j", h=8)
                        v1 = inc(ve.tensor_tensor(pm3, e3, mk3, ALU.mult), "sV")
                        M[("pm", qb, half)] = v1
                        ve.wait_ge(sems["sV"], v1)
                        v2 = inc(ve.tensor_reduce(
                            sums[0:nq, half * 8:(half + 1) * 8][:, :, None],
                            p_m[0:nq, :].rearrange("p (h j) -> p h j", h=8),
                            AX.X, ALU.add), "sV")
                        ve.wait_ge(sems["sV"], v2)
                        v3 = inc(ve.reciprocal(rinv[0:nq, half * 8:(half + 1) * 8],
                                               sums[0:nq, half * 8:(half + 1) * 8]), "sV")
                        ve.wait_ge(sems["sV"], v3)
                        ri3 = rinv[0:nq, half * 8:(half + 1) * 8][:, :, None].broadcast_to([nq, 8, P])
                        pn3 = p_n[0:nq, :].rearrange("p (h j) -> p h j", h=8)
                        M[("pn", qb, half)] = inc(ve.tensor_tensor(pn3, pm3, ri3, ALU.mult), "sV")
                    SC.add("vector", ve_soft)
                    if sub < 3:
                        return

                    def pe_tr(pe):
                        wt(pe, "sV", M[("pn", qb, half)])
                        prevu = (qb, 0) if half == 1 else (qb - 1, 1)
                        wt(pe, "sV", M.get(("pTe", prevu[0], prevu[1]), 0))
                        for hh in range(8):
                            h = half * 8 + hh
                            mm = pe.transpose(pT_ps[0:P, hh * 112: hh * 112 + nq],
                                              p_n[0:nq, hh * P:(hh + 1) * P],
                                              id_bf[0:nq, 0:nq])
                        M[("tr", qb, half)] = inc(mm, "sP")
                    SC.add("tensor", pe_tr)
                    if sub < 4:
                        return

                    def ve_pTe(ve):
                        wt(ve, "sP", M[("tr", qb, half)])
                        wt(ve, "sP", M.get(("av", qb - 1, half), 0))
                        src = pT_ps[:, 0:896].rearrange(
                            "p (h j) -> p h j", h=8)[:, :, 0:nq]
                        dst = pT_sb[:, half * 896: half * 896 + 896].rearrange(
                            "p (h j) -> p h j", h=8)[:, :, 0:nq]
                        M[("pTe", qb, half)] = inc(ve.tensor_copy(dst, src), "sV")
                    SC.add("vector", ve_pTe)
                    if sub < 5:
                        return

                    def pe_av(pe):
                        sl = qb % 2
                        ko = sl * 1024
                        wt(pe, "sV", M[("pTe", qb, half)])
                        prevu = (qb, 0) if half == 1 else (qb - 1, 1)
                        wt(pe, "sV", M[("unit", prevu[0], prevu[1])])
                        for hh in range(8):
                            h = half * 8 + hh
                            mm = pe.matmul(av_ps[0:64, hh * P: hh * P + nq],
                                           v_sb[:, ko + h * 64: ko + (h + 1) * 64],
                                           pT_sb[:, half * 896 + hh * 112: half * 896 + hh * 112 + nq],
                                           start=True, stop=True)
                        M[("av", qb, half)] = inc(mm, "sP")
                    SC.add("tensor", pe_av)
                    if sub < 6:
                        return

                    def ve_att(ve):
                        wt(ve, "sP", M[("av", qb, half)])
                        wt(ve, "sP", M.get(("op", qb - 1), 0))
                        v1 = 0
                        for par_i in range(2):
                            src = av_ps[0:64, 0:1024].rearrange("p (h j) -> p h j", j=128)[:, par_i::2, 0:nq]
                            dst = att[par_i * 64:(par_i + 1) * 64,
                                      half * 448: half * 448 + 448].rearrange(
                                "p (c j) -> p c j", c=4)[:, :, 0:nq]
                            v1 = inc(ve.tensor_copy(dst, src), "sV")
                        M[("unit", qb, half)] = v1
                    SC.add("vector", ve_att)

                def emit_outproj(qb):
                    s_k, nq = QB[qb]

                    for rnd in range(2):
                        def pe_op(pe, rnd=rnd):
                            if rnd == 0:
                                wt(pe, "sV", M[("unit", qb, 1)])
                                wt(pe, "sA", M.get(("ope", qb - 1, 1), 0))
                                wt(pe, "sG", M["outw"])
                            else:
                                wt(pe, "sA", M[("ope", qb, 0)])
                            for oc4 in range(4):
                                oc = rnd * 4 + oc4
                                for pc in range(NCH):
                                    pe.matmul(op_ps[:, oc4 * P: oc4 * P + nq],
                                              outwT_bf[:, pc * DM + oc * P: pc * DM + (oc + 1) * P],
                                              att[:, pc * 112: pc * 112 + nq],
                                              start=(pc == 0), stop=False)
                                mm = pe.matmul(op_ps[:, oc4 * P: oc4 * P + nq],
                                               outb_bf[0:1, oc * P:(oc + 1) * P],
                                               ones_rb[0:1, 0:nq], start=False, stop=True)
                            M[("op", qb, rnd)] = inc(mm, "sP")
                        SC.add("tensor", pe_op)

                        def se_ope(se, rnd=rnd):
                            wt(se, "sP", M[("op", qb, rnd)])
                            if rnd == 0:
                                wt(se, "dS", M.get(("aod", qb - 1), 0))
                            M[("ope", qb, rnd)] = inc(
                                se.copy(opj[:, rnd * 512:(rnd + 1) * 512], op_ps[:, 0:512]), "sA")
                        SC.add("scalar", se_ope)

                    def sy_ao(sy):
                        wt(sy, "sA", M[("ope", qb, 1)])
                        for oc in range(NCH):
                            inc(sy.dma_start(out=attout_d[oc * P:(oc + 1) * P, s_k:s_k + nq],
                                             in_=opj[:, oc * P: oc * P + nq]), "dS", 16)
                        M[("aod", qb)] = cnt["dS"]
                    SC.add("sync", sy_ao)

                if bstop < 1:
                    emit_build(0, bsub=int(round(bstop * 10)))
                else:
                    emit_build(0)
                for qb in range(nqb if bstop >= 1 else 0):
                    if bstop >= 2:
                        emit_scores(qb, 0)
                    if qb + 1 < nqb:
                        emit_build(qb + 1)
                    if bstop >= 3:
                        emit_exp(qb, 0)
                    if bstop >= 9:
                        emit_scores(qb, 1)
                    if bstop >= 3:
                        emit_softmax(qb, 0, sub=bstop - 2)
                    if bstop >= 9:
                        emit_exp(qb, 1)
                        emit_softmax(qb, 1)
                    if bstop >= 10:
                        emit_outproj(qb)

                if stage == "qkv0":
                    qkvdbg = EB(nc.sbuf_tensor("qkvdbg", [P, 4864], F32))

                    def ve_dbg(ve):
                        wt(ve, "sV", M[("qn", 0)])
                        wt(ve, "sV", M[("kn", 0)])
                        wt(ve, "sV", M[("vev", 0)])
                        inc(ve.tensor_copy(qkvdbg[0:64, 0:1792], q_t[0:64, 0:1792]), "sV")
                        inc(ve.tensor_copy(qkvdbg[0:64, 1792:3840], k_t[0:64, 0:2048]), "sV")
                        inc(ve.tensor_copy(qkvdbg[:, 3840:4864], v_sb[:, 0:1024]), "sV")
                    SC.add("vector", ve_dbg)

                    def sy_dbg(sy):
                        wt(sy, "sV", cnt["sV"])
                        d = inc(sy.dma_start(out=ydbg_d[:, 0:4864], in_=qkvdbg[:, :]), "dS", 16)
                        sy.wait_ge(sems["dS"], d)
                    SC.add("sync", sy_dbg)

                def sy_endB(sy):
                    wt(sy, "dS", cnt["dS"])
                SC.add("sync", sy_endB)

                def gp_endB(gp):
                    wt(gp, "dG", cnt["dG"])
                SC.add("gpsimd", gp_endB)
                SC.flush(blk)

        if stage == "attn":
            with nc.Block("phOutA") as blk:
                def sy_o(sy):
                    d = inc(sy.dma_start(out=y_d[:, :], in_=attout_d[:, :]), "dS", 16)
                    sy.wait_ge(sems["dS"], d)
                blk.sync(sy_o)
            return nc

        with ExitStack() as stDE:
            ED = stDE.enter_context
            acc = ED(nc.sbuf_tensor("acc", [P, NCH * TC], F32))

            def acc3(t0, n):
                return acc[:, :].rearrange("p (c n) -> p c n", c=NCH)[:, :, t0:t0 + n]

            def we_slot(e):
                return (e % 2) * 12288

            MW = {}
            _wstfree = [0]

            def emit_wexp(SCref, e, wsem, wval):
                parts = []
                for part in range(3):
                    def srcs(part=part, e=e):
                        out = []
                        if part == 0:
                            for kc in range(NCH):
                                s = egwT_d[e, kc * P:(kc + 1) * P, :] if e < 8 else                                     sgwT_d[kc * P:(kc + 1) * P, (e - 8) * 512:(e - 7) * 512]
                                out.append((wstage[:, kc * 512:(kc + 1) * 512], s))
                        elif part == 1:
                            for kc in range(NCH):
                                s = euwT_d[e, kc * P:(kc + 1) * P, :] if e < 8 else                                     suwT_d[kc * P:(kc + 1) * P, (e - 8) * 512:(e - 7) * 512]
                                out.append((wstage[:, kc * 512:(kc + 1) * 512], s))
                        else:
                            for ic in range(4):
                                s = edwT_d[e, ic * P:(ic + 1) * P, :] if e < 8 else                                     sdwT_d[(e - 8) * 512 + ic * P: (e - 8) * 512 + (ic + 1) * P, :]
                                out.append((wstage[:, ic * DM:(ic + 1) * DM], s))
                        return out

                    def sy_st(sy, part=part, e=e, wsem=wsem, wval=wval, _wf=_wstfree[0]):
                        if part == 0 and wval > 0:
                            wt(sy, wsem, wval)
                        wt(sy, "sG", _wf)
                        for dst, s in srcs(part):
                            inc(sy.dma_start(out=dst, in_=s), "wst", 16)
                        M[("wst", e, part)] = cnt["wst"]
                    SCref.add("sync", sy_st)

                    def gp_ct(gp, part=part, e=e):
                        wt(gp, "wst", M[("wst", e, part)])
                        off = we_slot(e) + part * 4096
                        M[("wct", e, part)] = inc(
                            gp.tensor_copy(wbuf[:, off: off + 4096], wstage[:, :]), "sG")
                    SCref.add("gpsimd", gp_ct)
                    _wstfree[0] = M[("wct", e, part)]
                MW[e] = M[("wct", e, 2)]

            with ExitStack() as stD:
                DD = stD.enter_context
                xstD = DD(nc.sbuf_tensor("xstD", [P, 2 * 1024], F32))
                aostD = DD(nc.sbuf_tensor("aostD", [P, 2 * 1024], F32))
                xb2 = DD(nc.sbuf_tensor("xb2", [P, 1024], BF16))
                sq2 = DD(nc.sbuf_tensor("sq2", [P, 1024], BF16))
                tmpD = DD(nc.sbuf_tensor("tmpD", [P, 1024], BF16))
                a_bcD = DD(nc.sbuf_tensor("a_bcD", [P, 2 * P], BF16))
                b_bcD = DD(nc.sbuf_tensor("b_bcD", [P, 2 * P], BF16))
                mrowD = DD(nc.sbuf_tensor("mrowD", [1, P], F32))
                msqD = DD(nc.sbuf_tensor("msqD", [1, P], F32))
                varrD = DD(nc.sbuf_tensor("varrD", [1, P], F32))
                lnrD = DD(nc.sbuf_tensor("lnrD", [1, P], F32))
                arowD = DD(nc.sbuf_tensor("arowD", [1, P], BF16))
                browD = DD(nc.sbuf_tensor("browD", [1, P], BF16))
                sc0 = DD(nc.sbuf_tensor("sc0", [P, 8], F32))
                sce = DD(nc.sbuf_tensor("sce", [P, 8], F32))
                scs = DD(nc.sbuf_tensor("scs", [P, 8], F32))
                gm1 = DD(nc.sbuf_tensor("gm1", [P, 1], F32))
                gm2 = DD(nc.sbuf_tensor("gm2", [P, 1], F32))
                geq = DD(nc.sbuf_tensor("geq", [P, 8], F32))
                gs2 = DD(nc.sbuf_tensor("gs2", [P, 8], F32))
                sel01 = DD(nc.sbuf_tensor("sel01", [P, 8], F32))
                gden = DD(nc.sbuf_tensor("gden", [P, 1], F32))
                gwin = DD(nc.sbuf_tensor("gwin", [P, 1], F32))
                wful = DD(nc.sbuf_tensor("wful", [P, 8], F32))
                selT = DD(nc.sbuf_tensor("selT", [8, TC], F32))
                cwsb = DD(nc.sbuf_tensor("cwsb", [8, 2], F32))
                cntg = DD(nc.sbuf_tensor("cntg", [8, 1], F32))
                wsumg = DD(nc.sbuf_tensor("wsumg", [8, 1], F32))
                wmaxg = DD(nc.sbuf_tensor("wmaxg", [8, 1], F32))
                wmean = DD(nc.sbuf_tensor("wmean", [8, 1], F32))
                coefT = DD(nc.sbuf_tensor("coefT", [8, TC], BF16))
                sumD = DD(nc.psum_tensor("sumD", [1, P], F32))
                sqD = DD(nc.psum_tensor("sqD", [1, P], F32))
                abD_ps = DD(nc.psum_tensor("abD_ps", [P, P], F32))
                bbD_ps = DD(nc.psum_tensor("bbD_ps", [P, P], F32))
                glog = DD(nc.psum_tensor("glog", [P, 8], F32))
                selT_ps = DD(nc.psum_tensor("selT_ps", [8, P], F32))
                cnt_ps = DD(nc.psum_tensor("cnt_ps", [8, 1], F32))
                wsum_ps = DD(nc.psum_tensor("wsum_ps", [8, 1], F32))

                M = {}

                with nc.Block("phD", no_gpsimd_drain=True) as blk:
                    SC = Sched()
                    actload and SC.add("scalar", lambda se: act_load(se, "natural_log_exp_and_others"))
                    emit_wexp(SC, 0, "dG", 0)
                    emit_wexp(SC, 1, "dG", 0)

                    for it in range(NTD):
                        t0 = it * P
                        par = it % 2
                        po = par * 1024
                        ao = par * P

                        def sy_xd(sy, it=it, t0=t0, po=po):
                            if it >= 2:
                                wt(sy, "sV", M[("xb2c", it - 2)])
                            sx = "xd%d" % (it % 2)
                            for kc in range(NCH):
                                inc(sy.dma_start(out=xstD[:, po + kc * P: po + (kc + 1) * P],
                                                 in_=xin[kc * P:(kc + 1) * P, 8 + t0: 8 + t0 + P]), sx, 16)
                            for kc in range(NCH):
                                inc(sy.dma_start(out=aostD[:, po + kc * P: po + (kc + 1) * P],
                                                 in_=attout_d[kc * P:(kc + 1) * P, t0: t0 + P]), sx, 16)
                            M[("xd", it)] = cnt[sx]
                        SC.add("sync", sy_xd)

                        def ve_x2(ve, it=it, t0=t0, po=po):
                            wt(ve, "xd%d" % (it % 2), M[("xd", it)])
                            x3 = xstD[:, po:po + 1024].rearrange("p (c n) -> p c n", c=NCH)
                            a3 = aostD[:, po:po + 1024].rearrange("p (c n) -> p c n", c=NCH)
                            v1 = inc(ve.tensor_tensor(acc3(t0, P), x3, a3, ALU.add), "sV")
                            ve.wait_ge(sems["sV"], v1)
                            if it > 0:
                                wt(ve, "sP", M[("statD", it - 1)])
                                wt(ve, "sP", M[("gl", it - 1)])
                            v2 = inc(ve.tensor_copy(
                                xb2[:, :].rearrange("p (c n) -> p c n", c=NCH), acc3(t0, P)), "sV")
                            M[("xb2c", it)] = v2
                        SC.add("vector", ve_x2)

                        def se_sq2(se, it=it):
                            wt(se, "sV", M[("xb2c", it)])
                            if it > 0:
                                wt(se, "sP", M[("statD", it - 1)])
                            M[("sq2", it)] = inc(se.activation(sq2[:, :], xb2[:, :], AF.Square), "sA")
                        SC.add("scalar", se_sq2)

                        def pe_statD(pe, it=it):
                            wt(pe, "sA", M[("sq2", it)])
                            if it > 0:
                                wt(pe, "sA", M[("rowmD", it - 1)])
                                wt(pe, "sV", M[("rowvD", it - 1)])
                            for kc in range(NCH):
                                pe.matmul(sumD[0:1, :], ones_cb[:, :],
                                          xb2[:, kc * P:(kc + 1) * P],
                                          start=(kc == 0), stop=(kc == NCH - 1))
                            for kc in range(NCH):
                                mm = pe.matmul(sqD[0:1, :], ones_cb[:, :],
                                               sq2[:, kc * P:(kc + 1) * P],
                                               start=(kc == 0), stop=(kc == NCH - 1))
                            M[("statD", it)] = inc(mm, "sP")
                        SC.add("tensor", pe_statD)

                        def se_rowD(se, it=it):
                            wt(se, "sP", M[("statD", it)])
                            if it > 0:
                                wt(se, "sP", M[("abmmD", it - 1)])
                                wt(se, "sV", M[("rowbD", it - 1)])
                            a1 = inc(se.activation(mrowD[0:1, :], sumD[0:1, :],
                                                   AF.Copy, bias=0.0, scale=1.0 / DM), "sA")
                            se.wait_ge(sems["sA"], a1)
                            M[("rowmD", it)] = inc(se.activation(msqD[0:1, :], mrowD[0:1, :],
                                                                 AF.Square), "sA")
                        SC.add("scalar", se_rowD)

                        def ve_rowD(ve, it=it):
                            wt(ve, "sA", M[("rowmD", it)])
                            M[("rowvD", it)] = inc(
                                ve.scalar_tensor_tensor(varrD[0:1, :], sqD[0:1, :], 1.0 / DM,
                                                        msqD[0:1, :], ALU.mult, ALU.subtract), "sV")
                        SC.add("vector", ve_rowD)

                        def se_rowD2(se, it=it):
                            wt(se, "sV", M[("rowvD", it)])
                            a1 = inc(se.activation(lnrD[0:1, :], varrD[0:1, :],
                                                   AF.Ln, bias=eps_c[0:1, 0:1], scale=1.0), "sA")
                            se.wait_ge(sems["sA"], a1)
                            M[("rowaD", it)] = inc(se.activation(arowD[0:1, :], lnrD[0:1, :],
                                                                 AF.Exp, bias=0.0, scale=-0.5), "sA")
                        SC.add("scalar", se_rowD2)

                        def ve_rowD2(ve, it=it):
                            wt(ve, "sA", M[("rowaD", it)])
                            M[("rowbD", it)] = inc(
                                ve.scalar_tensor_tensor(browD[0:1, :], mrowD[0:1, :], -1.0,
                                                        arowD[0:1, :], ALU.mult, ALU.mult), "sV")
                        SC.add("vector", ve_rowD2)

                        def pe_abD(pe, it=it, par=par):
                            wt(pe, "sA", M[("rowaD", it)])
                            wt(pe, "sV", M[("rowbD", it)])
                            if it > 0:
                                wt(pe, "sA", M[("abevD", it - 1)])
                            pe.matmul(abD_ps[:, :], ones_rb[0:1, 0:P], arowD[0:1, :],
                                      start=True, stop=True)
                            mm = pe.matmul(bbD_ps[:, :], ones_rb[0:1, 0:P], browD[0:1, :],
                                           start=True, stop=True)
                            M[("abmmD", it)] = inc(mm, "sP")
                        SC.add("tensor", pe_abD)

                        def se_abevD(se, it=it, ao=ao):
                            wt(se, "sP", M[("abmmD", it)])
                            if it >= 2:
                                wt(se, "sV", M[("xln2", it - 2)])
                            se.copy(a_bcD[:, ao:ao + P], abD_ps[:, :])
                            M[("abevD", it)] = inc(se.copy(b_bcD[:, ao:ao + P], bbD_ps[:, :]), "sA")
                        SC.add("scalar", se_abevD)

                        def ve_xln2(ve, it=it, t0=t0, ao=ao):
                            wt(ve, "sA", M[("abevD", it)])
                            xb3 = xb2[:, :].rearrange("p (c n) -> p c n", c=NCH)
                            a3 = a_bcD[:, ao:ao + P][:, None, :].broadcast_to([P, NCH, P])
                            t3 = tmpD[:, :].rearrange("p (c n) -> p c n", c=NCH)
                            v1 = inc(ve.tensor_tensor(t3, xb3, a3, ALU.mult), "sV")
                            ve.wait_ge(sems["sV"], v1)
                            o3 = xln[:, :].rearrange("p (c n) -> p c n", c=NCH)[:, :, t0:t0 + P]
                            b3 = b_bcD[:, ao:ao + P][:, None, :].broadcast_to([P, NCH, P])
                            M[("xln2", it)] = inc(ve.tensor_tensor(o3, t3, b3, ALU.add), "sV")
                        SC.add("vector", ve_xln2)

                        def pe_gl(pe, it=it, t0=t0):
                            wt(pe, "sV", M[("xln2", it)])
                            if it > 0:
                                wt(pe, "sV", M[("gl_read", it - 1)])
                            for kc in range(NCH):
                                mm = pe.matmul(glog[:, :], xslc(kc, t0, P),
                                               gw_sb[:, kc * 8:(kc + 1) * 8],
                                               start=(kc == 0), stop=(kc == NCH - 1))
                            M[("gl", it)] = inc(mm, "sP")
                        SC.add("tensor", pe_gl)

                        def ve_glread(ve, it=it):
                            wt(ve, "sP", M[("gl", it)])
                            M[("gl_read", it)] = inc(
                                ve.tensor_tensor(sc0[:, :], glog[:, :], lb_bc[:, :], ALU.add), "sV")
                        SC.add("vector", ve_glread)

                        def se_sig(se, it=it):
                            wt(se, "sV", M[("gl_read", it)])
                            M[("sce", it)] = inc(se.activation(sce[:, :], sc0[:, :],
                                                               AF.Exp, bias=0.0, scale=-1.0), "sA")
                        SC.add("scalar", se_sig)

                        def ve_top2(ve, it=it):
                            wt(ve, "sA", M[("sce", it)])
                            v = inc(ve.tensor_scalar_add(sce[:, :], sce[:, :], 1.0), "sV")
                            ve.wait_ge(sems["sV"], v)
                            v = inc(ve.reciprocal(scs[:, :], sce[:, :]), "sV")
                            ve.wait_ge(sems["sV"], v)
                            v = inc(ve.reduce_max(gm1[:, :], scs[:, :], AX.X), "sV")
                            ve.wait_ge(sems["sV"], v)
                            v = inc(ve.tensor_single_scalar(geq[:, :], scs[:, :], gm1[:, :],
                                                            ALU.is_equal), "sV")
                            ve.wait_ge(sems["sV"], v)
                            v = inc(ve.scalar_tensor_tensor(gs2[:, :], geq[:, :], 1e9, scs[:, :],
                                                            ALU.mult, ALU.subtract), "sV")
                            ve.wait_ge(sems["sV"], v)
                            v = inc(ve.tensor_reduce(gm2[:, :], gs2[:, :], AX.X, ALU.min,
                                                     negate=True), "sV")
                            ve.wait_ge(sems["sV"], v)
                            v = inc(ve.tensor_single_scalar(sel01[:, :], scs[:, :], gm2[:, :],
                                                            ALU.is_ge), "sV")
                            ve.wait_ge(sems["sV"], v)
                            v = inc(ve.tensor_scalar(gden[:, :], gm1[:, :], gm2[:, :], 1e-8,
                                                     ALU.add, ALU.add), "sV")
                            ve.wait_ge(sems["sV"], v)
                            v = inc(ve.reciprocal(gwin[:, :], gden[:, :]), "sV")
                            ve.wait_ge(sems["sV"], v)
                            v = inc(ve.scalar_tensor_tensor(wful[:, :], scs[:, :], gwin[:, :],
                                                            sel01[:, :], ALU.mult, ALU.mult), "sV")
                            M[("top2", it)] = v
                        SC.add("vector", ve_top2)

                        def pe_gstat(pe, it=it):
                            wt(pe, "sV", M[("top2", it)])
                            pe.matmul(cnt_ps[0:8, 0:1], sel01[:, :], ones_cf[:, :],
                                      start=(it == 0), stop=(it == NTD - 1))
                            pe.matmul(wsum_ps[0:8, 0:1], wful[:, :], ones_cf[:, :],
                                      start=(it == 0), stop=(it == NTD - 1))
                            if it > 0:
                                wt(pe, "sA", M[("selTe", it - 1)])
                            mm = pe.transpose(selT_ps[0:8, 0:P], sel01[0:P, 0:8], id_f[0:P, 0:P])
                            M[("gstat", it)] = inc(mm, "sP")
                        SC.add("tensor", pe_gstat)

                        def se_selT(se, it=it):
                            wt(se, "sP", M[("gstat", it)])
                            M[("selTe", it)] = inc(
                                se.copy(selT[:, it * P:(it + 1) * P], selT_ps[0:8, :]), "sA")
                        SC.add("scalar", se_selT)

                    def se_cw(se):
                        wt(se, "sP", M[("gstat", NTD - 1)])
                        inc(se.copy(cwsb[:, 0:1], cnt_ps[0:8, :]), "sA")
                        M["cw"] = inc(se.copy(cwsb[:, 1:2], wsum_ps[0:8, :]), "sA")
                    SC.add("scalar", se_cw)

                    def gp_cc(gp):
                        wt(gp, "sA", M["cw"])
                        inc(gp.dma_start(out=ccin_d[0:8, :], in_=cwsb[:, 0:1]), "dG", 16)
                        d = inc(gp.dma_start(out=ccin_d[8:16, :], in_=cwsb[:, 1:2]), "dG", 16)
                        gp.wait_ge(sems["dG"], d)
                        c = inc(gp.collective_compute(
                            "AllReduce", ALU.add, replica_groups=[list(range(8))],
                            ins=[ccin_d[:, :].opt()], outs=[ccout_d[:, :].opt()]), "cc")
                        gp.wait_ge(sems["cc"], c)
                        inc(gp.dma_start(out=cntg[:, :], in_=ccout_d[0:8, :]), "dG", 16)
                        M["ccd"] = inc(gp.dma_start(out=wsumg[:, :], in_=ccout_d[8:16, :]), "dG", 16)
                    SC.add("gpsimd", gp_cc)

                    def ve_wm(ve):
                        wt(ve, "dG", M["ccd"])
                        v = inc(ve.tensor_scalar_max(wmaxg[:, :], cntg[:, :], 1.0), "sV")
                        ve.wait_ge(sems["sV"], v)
                        v = inc(ve.reciprocal(wmaxg[:, :], wmaxg[:, :]), "sV")
                        ve.wait_ge(sems["sV"], v)
                        v = inc(ve.tensor_tensor(wmean[:, :], wsumg[:, :], wmaxg[:, :], ALU.mult), "sV")
                        ve.wait_ge(sems["sV"], v)
                        wt(ve, "sA", M[("selTe", NTD - 1)])
                        M["coef"] = inc(ve.tensor_scalar_mul(coefT[:, :], selT[:, :],
                                                             wmean[:, :]), "sV")
                    SC.add("vector", ve_wm)

                    def gp_coef(gp):
                        wt(gp, "sV", M["coef"])
                        MW["coefd"] = inc(gp.dma_start(out=coefscr_d[:, :], in_=coefT[:, :]), "dG", 16)
                    SC.add("gpsimd", gp_coef)

                    def sy_endD(sy):
                        wt(sy, "dS", cnt["dS"])
                    SC.add("sync", sy_endD)

                    def gp_endD(gp):
                        wt(gp, "dG", cnt["dG"])
                    SC.add("gpsimd", gp_endD)
                    SC.flush(blk)

            if stage == "xln2":
                with nc.Block("phDbg2") as blk:
                    def sy_d3(sy):
                        d = inc(sy.dma_start(out=ydbgb_d[:, :], in_=xln[:, :]), "dS", 16)
                        sy.wait_ge(sems["dS"], d)
                    blk.sync(sy_d3)
                return nc

            if stage == "d":
                with nc.Block("phOutD") as blk:
                    def sy_d2(sy):
                        for hc in range(NCH):
                            inc(sy.dma_start(out=y_d[hc * P:(hc + 1) * P, :],
                                             in_=acc[:, hc * TC:(hc + 1) * TC]), "dS", 16)
                        wt(sy, "dS", cnt["dS"])
                    blk.sync(sy_d2)
                return nc

            with ExitStack() as stE:
                EE = stE.enter_context
                NTE2 = 4
                h_p = EE(nc.sbuf_tensor("h_p", [P, 2 * 4 * 512], BF16))
                sil = EE(nc.sbuf_tensor("sil", [P, 2 * 512], BF16))
                sp_ = EE(nc.sbuf_tensor("sp_", [P, 2 * 512], BF16))
                bc = EE(nc.sbuf_tensor("bc", [P, 2 * 512], BF16))
                g_ps = EE(nc.psum_tensor("g_ps", [P, 1024], F32))
                u_ps = EE(nc.psum_tensor("u_ps", [P, 1024], F32))
                down_ps = EE(nc.psum_tensor("down_ps", [P, 2048], F32))

                M = {}
                last_g = [0, 0]
                last_u = [0, 0]
                last_sil = [0, 0]
                last_sp = [0, 0]
                last_bc = [0, 0]
                last_dn_ic = [0, 0, 0, 0]
                last_acc = 0

                with nc.Block("phE", no_gpsimd_drain=True) as blk:
                    SC = Sched()
                    actload and SC.add("scalar", lambda se: act_load(se, "silu_and_others"))

                    NG = NEXP * NTE2

                    def emit_bc(g):
                        e, it = g // NTE2, g % NTE2
                        t0 = it * 512
                        par = it % 2
                        if e >= 8:
                            return

                        def sy_bc(sy, e=e, it=it, t0=t0, par=par, _lbc=last_bc[par]):
                            wt(sy, "dG", MW["coefd"])
                            wt(sy, "sV", _lbc)
                            sb = "bc%d" % par
                            inc(sy.dma_start(
                                out=bc[:, par * 512:(par + 1) * 512],
                                in_=coefscr_d[e, t0:t0 + 512][None, :].broadcast_to([P, 512])), sb, 16)
                            M[("bcd", e, it)] = cnt[sb]
                        SC.add("sync", sy_bc)

                    def emit_gu(g):
                        e, it = g // NTE2, g % NTE2
                        t0 = it * 512
                        par = it % 2
                        so = we_slot(e)
                        for ic in range(4):
                            sl = ic % 2

                            def pe_gu(pe, e=e, it=it, t0=t0, ic=ic, so=so, sl=sl,
                                      _lg=last_g[sl], _lu=last_u[sl]):
                                if it == 0 and ic == 0:
                                    wt(pe, "sG", MW[e])
                                wt(pe, "sA", _lg)
                                wt(pe, "sV", _lu)
                                for kc in range(NCH):
                                    pe.matmul(g_ps[:, sl * 512:(sl + 1) * 512],
                                              wbuf[:, so + kc * 512 + ic * P: so + kc * 512 + (ic + 1) * P],
                                              xslc(kc, t0, 512), start=(kc == 0), stop=(kc == NCH - 1))
                                for kc in range(NCH):
                                    mm = pe.matmul(u_ps[:, sl * 512:(sl + 1) * 512],
                                                   wbuf[:, so + 4096 + kc * 512 + ic * P: so + 4096 + kc * 512 + (ic + 1) * P],
                                                   xslc(kc, t0, 512), start=(kc == 0), stop=(kc == NCH - 1))
                                M[("gu", e, it, ic)] = inc(mm, "sP")
                            SC.add("tensor", pe_gu)

                            def se_sil(se, e=e, it=it, ic=ic, sl=sl, _ls=last_sil[sl]):
                                wt(se, "sP", M[("gu", e, it, ic)])
                                wt(se, "sV", _ls)
                                M[("sil", e, it, ic)] = inc(
                                    se.activation(sil[:, sl * 512:(sl + 1) * 512],
                                                  g_ps[:, sl * 512:(sl + 1) * 512], AF.Silu), "sA")
                            SC.add("scalar", se_sil)
                            last_g[sl] = M[("sil", e, it, ic)]

                            def ve_h(ve, e=e, it=it, ic=ic, par=par, sl=sl,
                                     _lsp=last_sp[sl], _ldn=last_dn_ic[ic]):
                                wt(ve, "sA", M[("sil", e, it, ic)])
                                if e < 8:
                                    wt(ve, "bc%d" % par, M[("bcd", e, it)])
                                    wt(ve, "sV", _lsp)
                                    v = inc(ve.tensor_tensor(sp_[:, sl * 512:(sl + 1) * 512],
                                                             sil[:, sl * 512:(sl + 1) * 512],
                                                             bc[:, par * 512:(par + 1) * 512],
                                                             ALU.mult), "sV")
                                    M[("sp", e, it, ic)] = v
                                    ve.wait_ge(sems["sV"], v)
                                    srcm = sp_[:, sl * 512:(sl + 1) * 512]
                                else:
                                    srcm = sil[:, sl * 512:(sl + 1) * 512]
                                wt(ve, "sP", _ldn)
                                ho = (it % 2) * 2048
                                v = inc(ve.tensor_tensor(h_p[:, ho + ic * 512: ho + (ic + 1) * 512],
                                                         srcm, u_ps[:, sl * 512:(sl + 1) * 512],
                                                         ALU.mult), "sV")
                                M[("hp", e, it, ic)] = v
                            SC.add("vector", ve_h)
                            last_sil[sl] = M[("hp", e, it, ic)]
                            last_u[sl] = M[("hp", e, it, ic)]
                            if e < 8:
                                last_sp[sl] = M[("hp", e, it, ic)]
                                last_bc[par] = M[("sp", e, it, ic)]

                    def emit_dn(g, h2):
                        e, it = g // NTE2, g % NTE2
                        t0 = it * 512
                        so = we_slot(e)

                        def pe_dn(pe, e=e, it=it, so=so, h2=h2, _la=last_acc):
                            if h2 == 0:
                                wt(pe, "sV", _la)
                            else:
                                wt(pe, "sV", M[("acc", e, it, 0)])
                            ho = (it % 2) * 2048
                            for ic in range(4):
                                if h2 == 0:
                                    wt(pe, "sV", M[("hp", e, it, ic)])
                                for j in range(4):
                                    hc = h2 * 4 + j
                                    mm = pe.matmul(
                                        down_ps[:, j * 512:(j + 1) * 512],
                                        wbuf[:, so + 8192 + ic * DM + hc * P: so + 8192 + ic * DM + (hc + 1) * P],
                                        h_p[:, ho + ic * 512: ho + (ic + 1) * 512],
                                        start=(ic == 0), stop=(ic == 3))
                            M[("dn", e, it, h2)] = inc(mm, "sP")
                        SC.add("tensor", pe_dn)
                        if h2 == 1:
                            for ic in range(4):
                                last_dn_ic[ic] = M[("dn", e, it, 1)]

                        def ve_acc(ve, e=e, it=it, t0=t0, h2=h2):
                            wt(ve, "sP", M[("dn", e, it, h2)])
                            a3 = acc[:, :].rearrange("p (c n) -> p c n", c=NCH)[
                                :, h2 * 4:(h2 + 1) * 4, t0:t0 + 512]
                            d3 = down_ps[:, :].rearrange("p (c n) -> p c n", c=4)
                            M[("acc", e, it, h2)] = inc(
                                ve.tensor_tensor(a3, a3, d3, ALU.add), "sV")
                        SC.add("vector", ve_acc)

                    def emit_wn(e):
                        emit_wexp(SC, e + 2, "sP", M[("dn", e, NTE2 - 1, 1)])

                    emit_bc(0)
                    emit_gu(0)
                    for g in range(NG):
                        if g + 1 < NG:
                            emit_bc(g + 1)
                            emit_gu(g + 1)
                        emit_dn(g, 0)
                        emit_dn(g, 1)
                        last_acc = M[("acc", g // NTE2, g % NTE2, 1)]
                        if g % NTE2 == NTE2 - 1 and g // NTE2 + 2 < NEXP:
                            emit_wn(g // NTE2)

                    def sy_out(sy):
                        for it in range(NTE2):
                            wt(sy, "sV", M[("acc", NEXP - 1, it, 1)])
                        for hc in range(NCH):
                            inc(sy.dma_start(out=y_d[hc * P:(hc + 1) * P, :],
                                             in_=acc[:, hc * TC:(hc + 1) * TC]), "dS", 16)
                        wt(sy, "dS", cnt["dS"])
                    SC.add("sync", sy_out)

                    def gp_endE(gp):
                        wt(gp, "dG", cnt["dG"])
                    SC.add("gpsimd", gp_endE)
                    SC.flush(blk)

    return nc



def _prep_inputs(L0_attn_x, L0_attn_qkv_w, L0_attn_qkv_b, L0_attn_out_w, L0_attn_out_b,
                 L0_ffn_gate_w, L0_ffn_lb_bias, L0_ffn_expert_gate_w, L0_ffn_expert_up_w,
                 L0_ffn_expert_down_w, L0_ffn_shared_gate_w, L0_ffn_shared_up_w,
                 L0_ffn_shared_down_w):
    x = np.asarray(L0_attn_x, dtype=np.float32)
    Bb, N, D = x.shape
    xd = np.ascontiguousarray(x.reshape(Bb, N // 2, 2, D).transpose(0, 2, 1, 3))

    shared = {
        "qkvT": np.ascontiguousarray(np.asarray(L0_attn_qkv_w, np.float32).T),
        "outwT": np.ascontiguousarray(np.asarray(L0_attn_out_w, np.float32).T),
        "qkvb": np.ascontiguousarray(np.asarray(L0_attn_qkv_b, np.float32).reshape(1, 3 * DM)),
        "outb": np.ascontiguousarray(np.asarray(L0_attn_out_b, np.float32).reshape(1, DM)),
        "gwT": np.ascontiguousarray(np.asarray(L0_ffn_gate_w, np.float32).T),
        "lbbc": np.ascontiguousarray(np.broadcast_to(
            np.asarray(L0_ffn_lb_bias, np.float32).reshape(1, 8), (P, 8))),
        "egwT": np.ascontiguousarray(np.asarray(L0_ffn_expert_gate_w, np.float32).transpose(0, 2, 1)),
        "euwT": np.ascontiguousarray(np.asarray(L0_ffn_expert_up_w, np.float32).transpose(0, 2, 1)),
        "edwT": np.ascontiguousarray(np.asarray(L0_ffn_expert_down_w, np.float32).transpose(0, 2, 1)),
        "sgwT": np.ascontiguousarray(np.asarray(L0_ffn_shared_gate_w, np.float32).T),
        "suwT": np.ascontiguousarray(np.asarray(L0_ffn_shared_up_w, np.float32).T),
        "sdwT": np.ascontiguousarray(np.asarray(L0_ffn_shared_down_w, np.float32).T),
        "idbf": np.eye(P, dtype=BFD),
        "idf": np.eye(P, dtype=np.float32),
    }
    selpk = np.zeros((P, P), dtype=BFD)
    for pc in range(8):
        for d in range(P):
            h = 2 * pc + (1 if d >= 64 else 0)
            selpk[d, pc * 16 + h] = 1.0
    shared["selpk"] = selpk

    def band(nq):
        r = np.arange(nq)[:, None]
        j = np.arange(P)[None, :]
        return (j >= r) & (j <= r + 16)

    in_maps = []
    halfN = N // 2
    for c in range(8):
        b = c >> 2
        par = (c >> 1) & 1
        half = c & 1
        start = half * TC
        lo, hi = start - 8, start + 2168
        xs = np.zeros((EXT, DM), dtype=np.float32)
        s0, s1 = max(lo, 0), min(hi, halfN)
        xs[s0 - lo: s1 - lo, :] = xd[b, par, s0:s1, :]

        mfirst = band(112)
        if half == 0:
            mfirst = mfirst & (np.arange(P)[None, :] >= 8)
        mlast = band(32)
        if half == 1:
            mlast = mlast & (np.arange(P)[None, :] <= 39)
        m = dict(shared)
        m["xin"] = np.ascontiguousarray(xs.T)
        m["mfirst"] = mfirst.astype(BFD)
        m["mmid"] = band(112).astype(BFD)
        m["mlast"] = mlast.astype(BFD)
        in_maps.append(m)
    return in_maps, (Bb, N, D)


def kernel(**inputs):
    in_maps, (Bb, N, D) = _prep_inputs(**inputs)
    nc = build_nc("full")
    res = run_bass_kernel_spmd(nc, in_maps, core_ids=list(range(8)))
    out = np.zeros((Bb, 2, N // 2, D), dtype=np.float32)
    for c in range(8):
        b = c >> 2
        par = (c >> 1) & 1
        half = c & 1
        y = np.asarray(res.results[c]["y"])
        out[b, par, half * TC:(half + 1) * TC, :] = y.T
    return np.ascontiguousarray(out.transpose(0, 2, 1, 3).reshape(Bb, N, D))


# version 09
# speedup vs baseline: 1.1122x; 1.1122x over previous
import numpy as np
import ml_dtypes
from contextlib import ExitStack

import concourse.bass as bass
import concourse.mybir as mybir
from concourse.bass_utils import run_bass_kernel_spmd

F32 = mybir.dt.float32
BF16 = mybir.dt.bfloat16
AF = mybir.ActivationFunctionType
ALU = mybir.AluOpType
AX = mybir.AxisListType
BFD = ml_dtypes.bfloat16

P = 128
DM = 1024
NCH = 8
EXT = 2176
TC = 2048
QB = [(i * 112, 112) for i in range(18)] + [(2016, 32)]
NQB = len(QB)
NTD = 16
NTE = 8
NEXP = 10
LN_EPS = 1e-5


def build_nc(stage="full", nqb=NQB, bstop=99, actload=True, bcmemset=False):
    nc = bass.Bass("TRN2", target_bir_lowering=False)

    xin = nc.dram_tensor("xin", [DM, EXT], F32, kind="ExternalInput")
    qkvT_d = nc.dram_tensor("qkvT", [DM, 3 * DM], F32, kind="ExternalInput")
    outwT_d = nc.dram_tensor("outwT", [DM, DM], F32, kind="ExternalInput")
    qkvb_d = nc.dram_tensor("qkvb", [1, 3 * DM], F32, kind="ExternalInput")
    outb_d = nc.dram_tensor("outb", [1, DM], F32, kind="ExternalInput")
    gwT_d = nc.dram_tensor("gwT", [DM, 8], F32, kind="ExternalInput")
    lbbc_d = nc.dram_tensor("lbbc", [P, 8], F32, kind="ExternalInput")
    egwT_d = nc.dram_tensor("egwT", [8, DM, 512], F32, kind="ExternalInput")
    euwT_d = nc.dram_tensor("euwT", [8, DM, 512], F32, kind="ExternalInput")
    edwT_d = nc.dram_tensor("edwT", [8, 512, DM], F32, kind="ExternalInput")
    sgwT_d = nc.dram_tensor("sgwT", [DM, DM], F32, kind="ExternalInput")
    suwT_d = nc.dram_tensor("suwT", [DM, DM], F32, kind="ExternalInput")
    sdwT_d = nc.dram_tensor("sdwT", [DM, DM], F32, kind="ExternalInput")
    idbf_d = nc.dram_tensor("idbf", [P, P], BF16, kind="ExternalInput")
    idf_d = nc.dram_tensor("idf", [P, P], F32, kind="ExternalInput")
    selpk_d = nc.dram_tensor("selpk", [P, P], BF16, kind="ExternalInput")
    mfirst_d = nc.dram_tensor("mfirst", [112, P], BF16, kind="ExternalInput")
    mmid_d = nc.dram_tensor("mmid", [112, P], BF16, kind="ExternalInput")
    mlast_d = nc.dram_tensor("mlast", [32, P], BF16, kind="ExternalInput")

    y_d = nc.dram_tensor("y", [DM, TC], F32, kind="ExternalOutput")
    ydbg_d = nc.dram_tensor("ydbg", [P, 18048], F32, kind="ExternalOutput")
    ydbgb_d = nc.dram_tensor("ydbgb", [P, NCH * EXT], BF16, kind="ExternalOutput")

    attout_d = nc.dram_tensor("attout", [DM, TC], F32, kind="Internal")
    rnscr_d = nc.dram_tensor("rnscr", [4, 16, P], BF16, kind="Internal")
    abscr_d = nc.dram_tensor("abscr", [2, 2, 512], BF16, kind="Internal")
    coefscr_d = nc.dram_tensor("coefscr", [8, TC], BF16, kind="Internal")
    ccin_d = nc.dram_tensor("ccin", [16, 1], F32, kind="Internal")
    ccout_d = nc.dram_tensor("ccout", [16, 1], F32, kind="Internal")

    st0 = ExitStack()
    with st0:
        E0 = st0.enter_context

        sems = {}
        cnt = {}
        for s in ("dS", "dG", "sP", "sV", "sA", "sG", "cc",
                  "xA0", "xA1", "bq", "bk", "xd0", "xd1", "bc0", "bc1", "rw", "wst"):
            sems[s] = E0(nc.semaphore(s))
            cnt[s] = 0

        def inc(instr, s, n=1):
            if instr is not None:
                instr.then_inc(sems[s], n)
            cnt[s] += n
            return cnt[s]

        def wt(eng, s, v):
            if v > 0:
                eng.wait_ge(sems[s], v)

        class _DummyInstr:
            def then_inc(self, *a, **k):
                return self

        class _Dummy:
            def __getattr__(self, name):
                def f(*a, **k):
                    return None
                return f

        _DUM = _Dummy()

        class Sched:

            ORDER = ("sync", "gpsimd", "scalar", "tensor", "vector")

            def __init__(self):
                self.fns = {n: [] for n in self.ORDER}
                self.snap = dict(cnt)

            def add(self, eng, fn):
                self.fns[eng].append(fn)
                fn(_DUM)

            def flush(self, blk):
                end = dict(cnt)
                for k, v in self.snap.items():
                    cnt[k] = v
                for eng in self.ORDER:
                    fns = self.fns[eng]
                    if fns:
                        getattr(blk, eng)(lambda e, fns=fns: [f(e) for f in fns])
                for k, v in end.items():
                    assert cnt[k] == v, (k, cnt[k], v)

        def act_load(se, set_name):
            from concourse.hw_specs import get_activation_tables
            for idx, nm in enumerate(get_activation_tables(nc.m.arch)):
                if nm == set_name:
                    se.add_instruction(mybir.InstLoadActFuncSet(
                        name=nc.get_next_instruction_name(),
                        act_func_set_id=idx, ins=[], outs=[]))
                    return
            raise ValueError(set_name)

        xln = E0(nc.sbuf_tensor("xln", [P, NCH * EXT], BF16))
        wbuf = E0(nc.sbuf_tensor("wbuf", [P, 24576], BF16))
        wstage = E0(nc.sbuf_tensor("wstage", [P, 4096], F32))
        gw_sb = E0(nc.sbuf_tensor("gw_sb", [P, 64], BF16))
        lb_bc = E0(nc.sbuf_tensor("lb_bc", [P, 8], F32))
        id_bf = E0(nc.sbuf_tensor("id_bf", [P, P], BF16))
        id_f = E0(nc.sbuf_tensor("id_f", [P, P], F32))
        selpk = E0(nc.sbuf_tensor("selpk_s", [P, P], BF16))
        m_first = E0(nc.sbuf_tensor("mfirst_s", [112, P], BF16))
        m_mid = E0(nc.sbuf_tensor("mmid_s", [112, P], BF16))
        m_last = E0(nc.sbuf_tensor("mlast_s", [32, P], BF16))
        ones_rb = E0(nc.sbuf_tensor("ones_rb", [1, 512], BF16))
        ones_cb = E0(nc.sbuf_tensor("ones_cb", [P, 1], BF16))
        ones_cf = E0(nc.sbuf_tensor("ones_cf", [P, 1], F32))
        eps_c = E0(nc.sbuf_tensor("eps_c", [P, 1], F32))

        def xslc(kc, t0, n):
            return xln[:, kc * EXT + t0: kc * EXT + t0 + n]

        with ExitStack() as stA:
            EA = stA.enter_context
            xst = EA(nc.sbuf_tensor("xst", [P, 2 * 4096], F32))
            xbt = EA(nc.sbuf_tensor("xbt", [P, 2 * 4096], BF16))
            sqt = EA(nc.sbuf_tensor("sqt", [P, 4096], BF16))
            tmpA = EA(nc.sbuf_tensor("tmpA", [P, 4096], BF16))
            a_bc = EA(nc.sbuf_tensor("a_bc", [P, 2 * 512], BF16))
            b_bc = EA(nc.sbuf_tensor("b_bc", [P, 2 * 512], BF16))
            mrow = EA(nc.sbuf_tensor("mrow", [1, 512], F32))
            msq = EA(nc.sbuf_tensor("msq", [1, 512], F32))
            varr = EA(nc.sbuf_tensor("varr", [1, 512], F32))
            lnr = EA(nc.sbuf_tensor("lnr", [1, 512], F32))
            arow = EA(nc.sbuf_tensor("arow", [1, 512], BF16))
            brow = EA(nc.sbuf_tensor("brow", [1, 512], BF16))
            sum_ps = EA(nc.psum_tensor("sum_ps", [1, 512], F32))
            sq_ps = EA(nc.psum_tensor("sq_ps", [1, 512], F32))
            ab_ps = EA(nc.psum_tensor("ab_ps", [P, 512], F32))
            bb_ps = EA(nc.psum_tensor("bb_ps", [P, 512], F32))

            tilesA = [(0, 512), (512, 512), (1024, 512), (1536, 512), (2048, 128)]
            M = {}

            with nc.Block("phA", no_gpsimd_drain=True) as blk:
                SC = Sched()
                actload and SC.add("scalar", lambda se: act_load(se, "natural_log_exp_and_others"))
                def gp_w(gp):
                    for dst, src in ((id_bf, idbf_d), (id_f, idf_d), (selpk, selpk_d),
                                     (m_first, mfirst_d), (m_mid, mmid_d), (m_last, mlast_d),
                                     (lb_bc, lbbc_d)):
                        inc(gp.dma_start(out=dst[:, :], in_=src[:, :]), "dG", 16)
                    for kc in range(NCH):
                        d = inc(gp.dma_start(out=wstage[:, 0:3072],
                                             in_=qkvT_d[kc * P:(kc + 1) * P, :]), "dG", 16)
                        gp.wait_ge(sems["dG"], d)
                        inc(gp.tensor_copy(wbuf[:, kc * 3072:(kc + 1) * 3072],
                                           wstage[:, 0:3072]), "sG")
                    for kc in range(NCH):
                        d = inc(gp.dma_start(out=wstage[:, kc * 8: kc * 8 + 8],
                                             in_=gwT_d[kc * P:(kc + 1) * P, :]), "dG", 16)
                    gp.wait_ge(sems["dG"], d)
                    M["gwcast"] = inc(gp.tensor_copy(gw_sb[:, :], wstage[:, 0:64]), "sG")
                SC.add("gpsimd", gp_w)

                def ve_c(ve):
                    inc(ve.memset(ones_rb[:, :], 1.0), "sV")
                    inc(ve.memset(ones_cb[:, :], 1.0), "sV")
                    inc(ve.memset(ones_cf[:, :], 1.0), "sV")
                    M["ones"] = inc(ve.memset(eps_c[:, :], LN_EPS), "sV")
                SC.add("vector", ve_c)

                for it, (c0, nw) in enumerate(tilesA):
                    par = it % 2
                    po = par * 4096
                    ao = par * 512

                    def sy_x(sy, it=it, c0=c0, nw=nw, po=po):
                        if it >= 2:
                            wt(sy, "sV", M[("xln", it - 2)])
                        sx = "xA%d" % (it % 2)
                        for kc in range(NCH):
                            inc(sy.dma_start(out=xst[:, po + kc * 512: po + kc * 512 + nw],
                                             in_=xin[kc * P:(kc + 1) * P, c0:c0 + nw]), sx, 16)
                        M[("x", it)] = cnt[sx]
                    SC.add("sync", sy_x)

                    def ve_cast(ve, it=it, nw=nw, po=po):
                        wt(ve, "xA%d" % (it % 2), M[("x", it)])
                        if it >= 2:
                            wt(ve, "sP", M[("stat", it - 2)])
                        src = xst[:, po:po + 4096].rearrange("p (c n) -> p c n", c=NCH)[:, :, 0:nw]
                        dst = xbt[:, po:po + 4096].rearrange("p (c n) -> p c n", c=NCH)[:, :, 0:nw]
                        M[("cast", it)] = inc(ve.tensor_copy(dst, src), "sV")
                    SC.add("vector", ve_cast)

                    def se_sq(se, it=it, nw=nw, po=po):
                        wt(se, "sV", M[("cast", it)])
                        if it > 0:
                            wt(se, "sP", M[("stat", it - 1)])
                        src = xbt[:, po:po + 4096].rearrange("p (c n) -> p c n", c=NCH)[:, :, 0:nw]
                        dst = sqt[:, :].rearrange("p (c n) -> p c n", c=NCH)[:, :, 0:nw]
                        M[("sq", it)] = inc(se.activation(dst, src, AF.Square), "sA")
                    SC.add("scalar", se_sq)

                    def pe_stat(pe, it=it, nw=nw, po=po):
                        wt(pe, "sV", max(M[("cast", it)], M["ones"]))
                        wt(pe, "sA", M[("sq", it)])
                        if it > 0:
                            wt(pe, "sA", M[("rowm", it - 1)])
                            wt(pe, "sV", M[("rowv", it - 1)])
                        for kc in range(NCH):
                            pe.matmul(sum_ps[0:1, 0:nw], ones_cb[:, :],
                                      xbt[:, po + kc * 512: po + kc * 512 + nw],
                                      start=(kc == 0), stop=(kc == NCH - 1))
                        for kc in range(NCH):
                            mm = pe.matmul(sq_ps[0:1, 0:nw], ones_cb[:, :],
                                           sqt[:, kc * 512: kc * 512 + nw],
                                           start=(kc == 0), stop=(kc == NCH - 1))
                        M[("stat", it)] = inc(mm, "sP")
                    SC.add("tensor", pe_stat)

                    def se_row(se, it=it, nw=nw):
                        wt(se, "sP", M[("stat", it)])
                        if it > 0:
                            wt(se, "sP", M[("abmm", it - 1)])
                            wt(se, "sV", M[("rowb", it - 1)])
                        a1 = inc(se.activation(mrow[0:1, 0:nw], sum_ps[0:1, 0:nw],
                                               AF.Copy, bias=0.0, scale=1.0 / DM), "sA")
                        se.wait_ge(sems["sA"], a1)
                        M[("rowm", it)] = inc(se.activation(msq[0:1, 0:nw], mrow[0:1, 0:nw],
                                                            AF.Square), "sA")
                    SC.add("scalar", se_row)

                    def ve_row(ve, it=it, nw=nw):
                        wt(ve, "sA", M[("rowm", it)])
                        M[("rowv", it)] = inc(
                            ve.scalar_tensor_tensor(varr[0:1, 0:nw], sq_ps[0:1, 0:nw],
                                                    1.0 / DM, msq[0:1, 0:nw],
                                                    ALU.mult, ALU.subtract), "sV")
                    SC.add("vector", ve_row)

                    def se_row2(se, it=it, nw=nw):
                        wt(se, "sV", M[("rowv", it)])
                        a1 = inc(se.activation(lnr[0:1, 0:nw], varr[0:1, 0:nw],
                                               AF.Ln, bias=eps_c[0:1, 0:1], scale=1.0), "sA")
                        se.wait_ge(sems["sA"], a1)
                        M[("rowa", it)] = inc(se.activation(arow[0:1, 0:nw], lnr[0:1, 0:nw],
                                                            AF.Exp, bias=0.0, scale=-0.5), "sA")
                    SC.add("scalar", se_row2)

                    def ve_row2(ve, it=it, nw=nw):
                        wt(ve, "sA", M[("rowa", it)])
                        M[("rowb", it)] = inc(
                            ve.scalar_tensor_tensor(brow[0:1, 0:nw], mrow[0:1, 0:nw],
                                                    -1.0, arow[0:1, 0:nw],
                                                    ALU.mult, ALU.mult), "sV")
                    SC.add("vector", ve_row2)

                    def pe_ab(pe, it=it, nw=nw, par=par):
                        wt(pe, "sA", M[("rowa", it)])
                        wt(pe, "sV", M[("rowb", it)])
                        if it > 0:
                            wt(pe, "sA", M[("abev", it - 1)])
                        pe.matmul(ab_ps[:, 0:nw], ones_rb[0:1, 0:P], arow[0:1, 0:nw],
                                  start=True, stop=True)
                        mm = pe.matmul(bb_ps[:, 0:nw], ones_rb[0:1, 0:P], brow[0:1, 0:nw],
                                       start=True, stop=True)
                        M[("abmm", it)] = inc(mm, "sP")
                    SC.add("tensor", pe_ab)

                    def se_abev(se, it=it, nw=nw, par=par):
                        wt(se, "sP", M[("abmm", it)])
                        if it >= 2:
                            wt(se, "sV", M[("xln", it - 2)])
                        se.copy(a_bc[:, par * 512: par * 512 + nw], ab_ps[:, 0:nw])
                        M[("abev", it)] = inc(
                            se.copy(b_bc[:, par * 512: par * 512 + nw], bb_ps[:, 0:nw]), "sA")
                    SC.add("scalar", se_abev)

                    def ve_xln(ve, it=it, c0=c0, nw=nw, po=po, par=par):
                        wt(ve, "sA", M[("abev", it)])
                        xb3 = xbt[:, po:po + 4096].rearrange("p (c n) -> p c n", c=NCH)[:, :, 0:nw]
                        a3 = a_bc[:, par * 512: par * 512 + nw][:, None, :].broadcast_to([P, NCH, nw])
                        t3 = tmpA[:, :].rearrange("p (c n) -> p c n", c=NCH)[:, :, 0:nw]
                        v1 = inc(ve.tensor_tensor(t3, xb3, a3, ALU.mult), "sV")
                        ve.wait_ge(sems["sV"], v1)
                        o3 = xln[:, :].rearrange("p (c n) -> p c n", c=NCH)[:, :, c0:c0 + nw]
                        b3 = b_bc[:, par * 512: par * 512 + nw][:, None, :].broadcast_to([P, NCH, nw])
                        M[("xln", it)] = inc(ve.tensor_tensor(o3, t3, b3, ALU.add), "sV")
                    SC.add("vector", ve_xln)

                def sy_endA(sy):
                    wt(sy, "dS", cnt["dS"])
                SC.add("sync", sy_endA)

                def gp_endA(gp):
                    wt(gp, "dG", cnt["dG"])
                SC.add("gpsimd", gp_endA)
                SC.flush(blk)

        if stage == "xln":
            with ExitStack() as stX:
                xf = stX.enter_context(nc.sbuf_tensor("xf_dbg", [P, NCH * EXT], F32))
                with nc.Block("phDbg") as blk:
                    def ve_d(ve):
                        inc(ve.tensor_copy(xf[:, :], xln[:, :]), "sV")
                    SC.add("vector", ve_d)
                    def sy_d(sy):
                        wt(sy, "sV", cnt["sV"])
                        d = inc(sy.dma_start(out=ydbg_d[:, 0:NCH * EXT], in_=xf[:, :]), "dS", 16)
                        sy.wait_ge(sems["dS"], d)
                    SC.add("sync", sy_d)
            return nc

        if stage == "ln":
            with nc.Block("phOutL") as blk:
                def sy_l(sy):
                    d = inc(sy.dma_start(out=y_d[:, :], in_=attout_d[:, :]), "dS", 16)
                    sy.wait_ge(sems["dS"], d)
                SC.add("sync", sy_l)
            return nc

        with ExitStack() as stB:
            EB = stB.enter_context
            outwT_bf = EB(nc.sbuf_tensor("outwT_bf", [P, 8192], BF16))
            qkvb_bf = EB(nc.sbuf_tensor("qkvb_bf", [1, 3 * DM], BF16))
            outb_bf = EB(nc.sbuf_tensor("outb_bf", [1, DM], BF16))
            q_t = EB(nc.sbuf_tensor("q_t", [64, 2 * 1792], BF16))
            k_t = EB(nc.sbuf_tensor("k_t", [64, 2 * 2048], BF16))
            v_sb = EB(nc.sbuf_tensor("v_sb", [P, 2 * 1024], BF16))
            sqb = EB(nc.sbuf_tensor("sqb", [P, 1024], BF16))
            rn_ln = EB(nc.sbuf_tensor("rn_ln", [16, P], F32))
            rn_sb = EB(nc.sbuf_tensor("rn_sb", [16, P], BF16))
            bcrn_q = EB(nc.sbuf_tensor("bcrn_q", [64, 1792], BF16))
            bcrn_k = EB(nc.sbuf_tensor("bcrn_k", [64, 2048], BF16))
            e_p = EB(nc.sbuf_tensor("e_p", [112, 1024], BF16))
            p_m = EB(nc.sbuf_tensor("p_m", [112, 1024], BF16))
            p_n = EB(nc.sbuf_tensor("p_n", [112, 1024], BF16))
            sums = EB(nc.sbuf_tensor("sums", [112, 16], F32))
            rinv = EB(nc.sbuf_tensor("rinv", [112, 16], F32))
            pT_sb = EB(nc.sbuf_tensor("pT_sb", [P, 1792], BF16))
            att = EB(nc.sbuf_tensor("att", [P, 896], BF16))
            opj = EB(nc.sbuf_tensor("opj", [P, 1024], F32))
            s_av = EB(nc.psum_tensor("s_av", [P, 1024], F32))
            psB = EB(nc.psum_tensor("psB", [P, 1024], F32))
            pT_ps = EB(nc.psum_tensor("pT_ps", [P, 1024], BF16))
            av_ps = EB(nc.psum_tensor("av_ps", [64, 1024], F32))
            op_ps = EB(nc.psum_tensor("op_ps", [P, 512], F32))

            M = {("unit", -1, 0): 0, ("unit", -1, 1): 0}

            with nc.Block("phB", no_gpsimd_drain=True) as blk:
                SC = Sched()
                actload and SC.add("scalar", lambda se: act_load(se, "natural_log_exp_and_others"))
                def gp_ow(gp):
                    d = inc(gp.dma_start(out=wstage[0:1, 0:3 * DM], in_=qkvb_d[:, :]), "dG", 16)
                    gp.wait_ge(sems["dG"], d)
                    M["qkvb"] = inc(gp.tensor_copy(qkvb_bf[:, :], wstage[0:1, 0:3 * DM]), "sG")
                    d = inc(gp.dma_start(out=wstage[0:1, 0:DM], in_=outb_d[:, :]), "dG", 16)
                    gp.wait_ge(sems["dG"], d)
                    M["outbb"] = inc(gp.tensor_copy(outb_bf[:, :], wstage[0:1, 0:DM]), "sG")
                    for kc in range(NCH):
                        d = inc(gp.dma_start(out=wstage[:, 0:DM],
                                             in_=outwT_d[kc * P:(kc + 1) * P, :]), "dG", 16)
                        gp.wait_ge(sems["dG"], d)
                        inc(gp.tensor_copy(outwT_bf[:, kc * DM:(kc + 1) * DM],
                                           wstage[:, 0:DM]), "sG")
                SC.add("gpsimd", gp_ow)
                M["outw"] = cnt["sG"]

                def emit_build(qb, bsub=9):
                    s_k, nq = QB[qb]
                    qc0 = 8 + s_k
                    sl = qb % 2
                    qo = sl * 1792
                    ko = sl * 2048

                    def pe_q(pe):
                        if qb > 0:
                            wt(pe, "sV", M[("vev", qb - 1)])
                        else:
                            wt(pe, "sG", M["outbb"])
                        for pc in range(NCH):
                            for kc in range(NCH):
                                pe.matmul(psB[:, pc * P: pc * P + nq],
                                          wbuf[:, kc * 3072 + pc * P: kc * 3072 + (pc + 1) * P],
                                          xslc(kc, qc0, nq), start=(kc == 0), stop=False)
                            mm = pe.matmul(psB[:, pc * P: pc * P + nq],
                                           qkvb_bf[0:1, pc * P:(pc + 1) * P],
                                           ones_rb[0:1, 0:nq], start=False, stop=True)
                        M[("qmm", qb)] = inc(mm, "sP")
                    SC.add("tensor", pe_q)

                    def se_qsq(se):
                        wt(se, "sP", M[("qmm", qb)])
                        if qb > 0:
                            wt(se, "sP", M[("kssq", qb - 1)])
                        M[("qsq", qb)] = inc(
                            se.activation(sqb[:, 0:1024], psB[:, 0:1024], AF.Square), "sA")
                    SC.add("scalar", se_qsq)

                    def ve_qev(ve):
                        wt(ve, "sP", M[("qmm", qb)])
                        wt(ve, "sA", M[("qsq", qb)])
                        if qb > 1:
                            wt(ve, "sP", M[("s", qb - 2, 1)])
                        src0 = psB[0:64, 0:1024].rearrange("p (c n) -> p c n", c=NCH)[:, :, 0:nq]
                        src1 = psB[64:128, 0:1024].rearrange("p (c n) -> p c n", c=NCH)[:, :, 0:nq]
                        dst0 = q_t[0:64, qo:qo + 896].rearrange("p (c n) -> p c n", c=NCH)[:, :, 0:nq]
                        dst1 = q_t[0:64, qo + 896:qo + 1792].rearrange("p (c n) -> p c n", c=NCH)[:, :, 0:nq]
                        inc(ve.tensor_copy(dst0, src0), "sV")
                        M[("qev", qb)] = inc(ve.tensor_copy(dst1, src1), "sV")
                    SC.add("vector", ve_qev)
                    if bsub < 2:
                        return

                    def pe_qssq(pe):
                        wt(pe, "sA", M[("qsq", qb)])
                        wt(pe, "sV", M[("qev", qb)])
                        for pc in range(NCH):
                            mm = pe.matmul(psB[0:16, 0:nq], selpk[:, pc * 16:(pc + 1) * 16],
                                           sqb[:, pc * P: pc * P + nq],
                                           start=(pc == 0), stop=(pc == NCH - 1))
                        M[("qssq", qb)] = inc(mm, "sP")
                    SC.add("tensor", pe_qssq)

                    def se_qrn(se):
                        wt(se, "sP", M[("qssq", qb)])
                        if qb > 0:
                            wt(se, "rw", M[("krnw", qb - 1)])
                        a1 = inc(se.activation(rn_ln[0:16, 0:nq], psB[0:16, 0:nq], AF.Ln), "sA")
                        se.wait_ge(sems["sA"], a1)
                        M[("qrn", qb)] = inc(
                            se.activation(rn_sb[0:16, 0:nq], rn_ln[0:16, 0:nq],
                                          AF.Exp, bias=0.0, scale=-0.5), "sA")
                    SC.add("scalar", se_qrn)

                    def sy_qrn(sy):
                        idx = sl * 2
                        wt(sy, "sA", M[("qrn", qb)])
                        if qb > 0:
                            wt(sy, "sV", M[("qn", qb - 1)])
                        if qb > 1:
                            wt(sy, "bq", M[("qrnb", qb - 2)])
                        d1 = inc(sy.dma_start(out=rnscr_d[idx, 0:16, 0:nq],
                                              in_=rn_sb[0:16, 0:nq]), "rw", 16)
                        M[("qrnw", qb)] = d1
                        sy.wait_ge(sems["rw"], d1)
                        srcq = rnscr_d[idx].rearrange("(pc two) k -> two pc k", two=2)
                        for rh in range(2):
                            bqv = bcrn_q[:, rh * 896: rh * 896 + 896].rearrange(
                                "p (c n) -> p c n", c=NCH)[:, :, 0:nq]
                            mq = inc(sy.dma_start(
                                out=bqv, in_=srcq[rh][:, 0:nq][None].broadcast_to([64, NCH, nq])), "bq", 16)
                        M[("qrnb", qb)] = mq
                    SC.add("sync", sy_qrn)

                    def ve_qn(ve):
                        wt(ve, "bq", M[("qrnb", qb)])
                        M[("qn", qb)] = inc(
                            ve.tensor_tensor(q_t[0:64, qo:qo + 1792], q_t[0:64, qo:qo + 1792],
                                             bcrn_q[0:64, :], ALU.mult), "sV")
                    SC.add("vector", ve_qn)
                    if bsub < 3:
                        return

                    def pe_k(pe):
                        wt(pe, "sA", M[("qrn", qb)])
                        wt(pe, "sV", M[("qev", qb)])
                        for pc in range(NCH):
                            for kc in range(NCH):
                                pe.matmul(psB[:, pc * P:(pc + 1) * P],
                                          wbuf[:, kc * 3072 + DM + pc * P: kc * 3072 + DM + (pc + 1) * P],
                                          xslc(kc, s_k, P), start=(kc == 0), stop=False)
                            mm = pe.matmul(psB[:, pc * P:(pc + 1) * P],
                                           qkvb_bf[0:1, DM + pc * P: DM + (pc + 1) * P],
                                           ones_rb[0:1, 0:P], start=False, stop=True)
                        M[("kmm", qb)] = inc(mm, "sP")
                    SC.add("tensor", pe_k)

                    def se_ksq(se):
                        wt(se, "sP", M[("kmm", qb)])
                        wt(se, "sP", M[("qssq", qb)])
                        M[("ksq", qb)] = inc(
                            se.activation(sqb[:, 0:1024], psB[:, 0:1024], AF.Square), "sA")
                    SC.add("scalar", se_ksq)

                    def ve_kev(ve):
                        wt(ve, "sP", M[("kmm", qb)])
                        wt(ve, "sA", M[("ksq", qb)])
                        if qb > 1:
                            wt(ve, "sP", M[("s", qb - 2, 1)])
                        inc(ve.tensor_copy(k_t[0:64, ko:ko + 1024], psB[0:64, 0:1024]), "sV")
                        M[("kev", qb)] = inc(
                            ve.tensor_copy(k_t[0:64, ko + 1024:ko + 2048], psB[64:128, 0:1024]), "sV")
                    SC.add("vector", ve_kev)
                    if bsub < 4:
                        return

                    def pe_kssq(pe):
                        wt(pe, "sA", M[("ksq", qb)])
                        wt(pe, "sV", M[("kev", qb)])
                        for pc in range(NCH):
                            mm = pe.matmul(psB[0:16, 0:P], selpk[:, pc * 16:(pc + 1) * 16],
                                           sqb[:, pc * P:(pc + 1) * P],
                                           start=(pc == 0), stop=(pc == NCH - 1))
                        M[("kssq", qb)] = inc(mm, "sP")
                    SC.add("tensor", pe_kssq)

                    def se_krn(se):
                        wt(se, "sP", M[("kssq", qb)])
                        wt(se, "rw", M[("qrnw", qb)])
                        a1 = inc(se.activation(rn_ln[0:16, 0:P], psB[0:16, 0:P], AF.Ln), "sA")
                        se.wait_ge(sems["sA"], a1)
                        M[("krn", qb)] = inc(
                            se.activation(rn_sb[0:16, 0:P], rn_ln[0:16, 0:P],
                                          AF.Exp, bias=0.0, scale=-0.5), "sA")
                    SC.add("scalar", se_krn)

                    def sy_krn(sy):
                        idx = sl * 2 + 1
                        wt(sy, "sA", M[("krn", qb)])
                        if qb > 0:
                            wt(sy, "sV", M[("kn", qb - 1)])
                        if qb > 1:
                            wt(sy, "bk", M[("krnb", qb - 2)])
                        d1 = inc(sy.dma_start(out=rnscr_d[idx, 0:16, 0:P],
                                              in_=rn_sb[0:16, 0:P]), "rw", 16)
                        M[("krnw", qb)] = d1
                        sy.wait_ge(sems["rw"], d1)
                        srck = rnscr_d[idx].rearrange("(pc two) k -> two pc k", two=2)
                        for rh in range(2):
                            bkv = bcrn_k[:, rh * 1024: rh * 1024 + 1024].rearrange(
                                "p (c n) -> p c n", c=NCH)
                            mk = inc(sy.dma_start(
                                out=bkv, in_=srck[rh][None].broadcast_to([64, NCH, P])), "bk", 16)
                        M[("krnb", qb)] = mk
                    SC.add("sync", sy_krn)

                    def ve_kn(ve):
                        wt(ve, "bk", M[("krnb", qb)])
                        M[("kn", qb)] = inc(
                            ve.tensor_tensor(k_t[0:64, ko:ko + 2048], k_t[0:64, ko:ko + 2048],
                                             bcrn_k[0:64, :], ALU.mult), "sV")
                    SC.add("vector", ve_kn)
                    if bsub < 5:
                        return

                    def pe_v(pe):
                        wt(pe, "sA", M[("krn", qb)])
                        wt(pe, "sV", M[("kev", qb)])
                        if qb > 1:
                            wt(pe, "sP", M[("av", qb - 2, 1)])
                        for vh in range(2):
                            for kc in range(NCH):
                                pe.matmul(psB[:, vh * 512:(vh + 1) * 512],
                                          xslc(kc, s_k, P),
                                          wbuf[:, kc * 3072 + 2048 + vh * 512: kc * 3072 + 2048 + (vh + 1) * 512],
                                          start=(kc == 0), stop=False)
                            mm = pe.matmul(psB[:, vh * 512:(vh + 1) * 512],
                                           ones_rb[0:1, 0:P],
                                           qkvb_bf[0:1, 2048 + vh * 512: 2048 + (vh + 1) * 512],
                                           start=False, stop=True)
                        M[("vmm", qb)] = inc(mm, "sP")
                    SC.add("tensor", pe_v)

                    def ve_vev(ve):
                        vo = sl * 1024
                        wt(ve, "sP", M[("vmm", qb)])
                        M[("vev", qb)] = inc(
                            ve.tensor_copy(v_sb[:, vo:vo + 1024], psB[:, 0:1024]), "sV")
                    SC.add("vector", ve_vev)

                def emit_scores(qb, half):
                    s_k, nq = QB[qb]
                    sl = qb % 2
                    qo = sl * 1792
                    ko = sl * 2048

                    def pe_s(pe):
                        wt(pe, "sV", M[("qn", qb)])
                        wt(pe, "sV", M[("kn", qb)])
                        prev = (qb, 0) if half == 1 else (qb - 1, 1)
                        wt(pe, "sA", M.get(("exp", prev[0], prev[1]), 0))
                        for hh in range(8):
                            h = half * 8 + hh
                            pc, rh = h // 2, h % 2
                            mm = pe.matmul(s_av[0:nq, hh * P:(hh + 1) * P],
                                           q_t[0:64, qo + rh * 896 + pc * 112: qo + rh * 896 + pc * 112 + nq],
                                           k_t[0:64, ko + rh * 1024 + pc * P: ko + rh * 1024 + (pc + 1) * P],
                                           start=True, stop=True)
                        M[("s", qb, half)] = inc(mm, "sP")
                    SC.add("tensor", pe_s)

                def emit_exp(qb, half):
                    s_k, nq = QB[qb]

                    def se_exp(se):
                        wt(se, "sP", M[("s", qb, half)])
                        prev = (qb, 0) if half == 1 else (qb - 1, 1)
                        wt(se, "sV", M.get(("pm", prev[0], prev[1]), 0))
                        M[("exp", qb, half)] = inc(
                            se.activation(e_p[0:nq, 0:1024], s_av[0:nq, 0:1024], AF.Exp), "sA")
                    SC.add("scalar", se_exp)

                def emit_softmax(qb, half, sub=6):
                    s_k, nq = QB[qb]
                    mask = m_first if qb == 0 else (m_last if qb == NQB - 1 else m_mid)
                    if sub < 2:
                        return

                    def ve_soft(ve):
                        wt(ve, "sA", M[("exp", qb, half)])
                        e3 = e_p[0:nq, :].rearrange("p (h j) -> p h j", h=8)
                        mk3 = mask[0:nq, :][:, None, :].broadcast_to([nq, 8, P])
                        pm3 = p_m[0:nq, :].rearrange("p (h j) -> p h j", h=8)
                        v1 = inc(ve.tensor_tensor(pm3, e3, mk3, ALU.mult), "sV")
                        M[("pm", qb, half)] = v1
                        ve.wait_ge(sems["sV"], v1)
                        v2 = inc(ve.tensor_reduce(
                            sums[0:nq, half * 8:(half + 1) * 8][:, :, None],
                            p_m[0:nq, :].rearrange("p (h j) -> p h j", h=8),
                            AX.X, ALU.add), "sV")
                        ve.wait_ge(sems["sV"], v2)
                        v3 = inc(ve.reciprocal(rinv[0:nq, half * 8:(half + 1) * 8],
                                               sums[0:nq, half * 8:(half + 1) * 8]), "sV")
                        ve.wait_ge(sems["sV"], v3)
                        ri3 = rinv[0:nq, half * 8:(half + 1) * 8][:, :, None].broadcast_to([nq, 8, P])
                        pn3 = p_n[0:nq, :].rearrange("p (h j) -> p h j", h=8)
                        M[("pn", qb, half)] = inc(ve.tensor_tensor(pn3, pm3, ri3, ALU.mult), "sV")
                    SC.add("vector", ve_soft)
                    if sub < 3:
                        return

                    def pe_tr(pe):
                        wt(pe, "sV", M[("pn", qb, half)])
                        prevu = (qb, 0) if half == 1 else (qb - 1, 1)
                        wt(pe, "sV", M.get(("pTe", prevu[0], prevu[1]), 0))
                        for hh in range(8):
                            h = half * 8 + hh
                            mm = pe.transpose(pT_ps[0:P, hh * 112: hh * 112 + nq],
                                              p_n[0:nq, hh * P:(hh + 1) * P],
                                              id_bf[0:nq, 0:nq])
                        M[("tr", qb, half)] = inc(mm, "sP")
                    SC.add("tensor", pe_tr)
                    if sub < 4:
                        return

                    def ve_pTe(ve):
                        wt(ve, "sP", M[("tr", qb, half)])
                        wt(ve, "sP", M.get(("av", qb - 1, half), 0))
                        src = pT_ps[:, 0:896].rearrange(
                            "p (h j) -> p h j", h=8)[:, :, 0:nq]
                        dst = pT_sb[:, half * 896: half * 896 + 896].rearrange(
                            "p (h j) -> p h j", h=8)[:, :, 0:nq]
                        M[("pTe", qb, half)] = inc(ve.tensor_copy(dst, src), "sV")
                    SC.add("vector", ve_pTe)
                    if sub < 5:
                        return

                    def pe_av(pe):
                        sl = qb % 2
                        ko = sl * 1024
                        wt(pe, "sV", M[("pTe", qb, half)])
                        prevu = (qb, 0) if half == 1 else (qb - 1, 1)
                        wt(pe, "sV", M[("unit", prevu[0], prevu[1])])
                        for hh in range(8):
                            h = half * 8 + hh
                            mm = pe.matmul(av_ps[0:64, hh * P: hh * P + nq],
                                           v_sb[:, ko + h * 64: ko + (h + 1) * 64],
                                           pT_sb[:, half * 896 + hh * 112: half * 896 + hh * 112 + nq],
                                           start=True, stop=True)
                        M[("av", qb, half)] = inc(mm, "sP")
                    SC.add("tensor", pe_av)
                    if sub < 6:
                        return

                    def ve_att(ve):
                        wt(ve, "sP", M[("av", qb, half)])
                        wt(ve, "sP", M.get(("op", qb - 1), 0))
                        v1 = 0
                        for par_i in range(2):
                            src = av_ps[0:64, 0:1024].rearrange("p (h j) -> p h j", j=128)[:, par_i::2, 0:nq]
                            dst = att[par_i * 64:(par_i + 1) * 64,
                                      half * 448: half * 448 + 448].rearrange(
                                "p (c j) -> p c j", c=4)[:, :, 0:nq]
                            v1 = inc(ve.tensor_copy(dst, src), "sV")
                        M[("unit", qb, half)] = v1
                    SC.add("vector", ve_att)

                def emit_outproj(qb):
                    s_k, nq = QB[qb]

                    for rnd in range(2):
                        def pe_op(pe, rnd=rnd):
                            if rnd == 0:
                                wt(pe, "sV", M[("unit", qb, 1)])
                                wt(pe, "sA", M.get(("ope", qb - 1, 1), 0))
                                wt(pe, "sG", M["outw"])
                            else:
                                wt(pe, "sA", M[("ope", qb, 0)])
                            for oc4 in range(4):
                                oc = rnd * 4 + oc4
                                for pc in range(NCH):
                                    pe.matmul(op_ps[:, oc4 * P: oc4 * P + nq],
                                              outwT_bf[:, pc * DM + oc * P: pc * DM + (oc + 1) * P],
                                              att[:, pc * 112: pc * 112 + nq],
                                              start=(pc == 0), stop=False)
                                mm = pe.matmul(op_ps[:, oc4 * P: oc4 * P + nq],
                                               outb_bf[0:1, oc * P:(oc + 1) * P],
                                               ones_rb[0:1, 0:nq], start=False, stop=True)
                            M[("op", qb, rnd)] = inc(mm, "sP")
                        SC.add("tensor", pe_op)

                        def se_ope(se, rnd=rnd):
                            wt(se, "sP", M[("op", qb, rnd)])
                            if rnd == 0:
                                wt(se, "dS", M.get(("aod", qb - 1), 0))
                            M[("ope", qb, rnd)] = inc(
                                se.copy(opj[:, rnd * 512:(rnd + 1) * 512], op_ps[:, 0:512]), "sA")
                        SC.add("scalar", se_ope)

                    def sy_ao(sy):
                        wt(sy, "sA", M[("ope", qb, 1)])
                        for oc in range(NCH):
                            inc(sy.dma_start(out=attout_d[oc * P:(oc + 1) * P, s_k:s_k + nq],
                                             in_=opj[:, oc * P: oc * P + nq]), "dS", 16)
                        M[("aod", qb)] = cnt["dS"]
                    SC.add("sync", sy_ao)

                if bstop < 1:
                    emit_build(0, bsub=int(round(bstop * 10)))
                else:
                    emit_build(0)
                for qb in range(nqb if bstop >= 1 else 0):
                    if bstop >= 2:
                        emit_scores(qb, 0)
                    if qb + 1 < nqb:
                        emit_build(qb + 1)
                    if bstop >= 3:
                        emit_exp(qb, 0)
                    if bstop >= 9:
                        emit_scores(qb, 1)
                    if bstop >= 3:
                        emit_softmax(qb, 0, sub=bstop - 2)
                    if bstop >= 9:
                        emit_exp(qb, 1)
                        emit_softmax(qb, 1)
                    if bstop >= 10:
                        emit_outproj(qb)

                if stage == "qkv0":
                    qkvdbg = EB(nc.sbuf_tensor("qkvdbg", [P, 4864], F32))

                    def ve_dbg(ve):
                        wt(ve, "sV", M[("qn", 0)])
                        wt(ve, "sV", M[("kn", 0)])
                        wt(ve, "sV", M[("vev", 0)])
                        inc(ve.tensor_copy(qkvdbg[0:64, 0:1792], q_t[0:64, 0:1792]), "sV")
                        inc(ve.tensor_copy(qkvdbg[0:64, 1792:3840], k_t[0:64, 0:2048]), "sV")
                        inc(ve.tensor_copy(qkvdbg[:, 3840:4864], v_sb[:, 0:1024]), "sV")
                    SC.add("vector", ve_dbg)

                    def sy_dbg(sy):
                        wt(sy, "sV", cnt["sV"])
                        d = inc(sy.dma_start(out=ydbg_d[:, 0:4864], in_=qkvdbg[:, :]), "dS", 16)
                        sy.wait_ge(sems["dS"], d)
                    SC.add("sync", sy_dbg)

                def sy_endB(sy):
                    wt(sy, "dS", cnt["dS"])
                SC.add("sync", sy_endB)

                def gp_endB(gp):
                    wt(gp, "dG", cnt["dG"])
                SC.add("gpsimd", gp_endB)
                SC.flush(blk)

        if stage == "attn":
            with nc.Block("phOutA") as blk:
                def sy_o(sy):
                    d = inc(sy.dma_start(out=y_d[:, :], in_=attout_d[:, :]), "dS", 16)
                    sy.wait_ge(sems["dS"], d)
                blk.sync(sy_o)
            return nc

        with ExitStack() as stDE:
            ED = stDE.enter_context
            acc = ED(nc.sbuf_tensor("acc", [P, NCH * TC], F32))

            def acc3(t0, n):
                return acc[:, :].rearrange("p (c n) -> p c n", c=NCH)[:, :, t0:t0 + n]

            def we_slot(e):
                return (e % 2) * 12288

            MW = {}
            _wstfree = [0]

            def emit_wexp(SCref, e, wsem, wval, q="sync"):
                parts = []
                for part in range(3):
                    def srcs(part=part, e=e):
                        out = []
                        if part == 0:
                            for kc in range(NCH):
                                s = egwT_d[e, kc * P:(kc + 1) * P, :] if e < 8 else                                     sgwT_d[kc * P:(kc + 1) * P, (e - 8) * 512:(e - 7) * 512]
                                out.append((wstage[:, kc * 512:(kc + 1) * 512], s))
                        elif part == 1:
                            for kc in range(NCH):
                                s = euwT_d[e, kc * P:(kc + 1) * P, :] if e < 8 else                                     suwT_d[kc * P:(kc + 1) * P, (e - 8) * 512:(e - 7) * 512]
                                out.append((wstage[:, kc * 512:(kc + 1) * 512], s))
                        else:
                            for ic in range(4):
                                s = edwT_d[e, ic * P:(ic + 1) * P, :] if e < 8 else                                     sdwT_d[(e - 8) * 512 + ic * P: (e - 8) * 512 + (ic + 1) * P, :]
                                out.append((wstage[:, ic * DM:(ic + 1) * DM], s))
                        return out

                    def sy_st(sy, part=part, e=e, wsem=wsem, wval=wval, _wf=_wstfree[0]):
                        if part == 0 and wval > 0:
                            wt(sy, wsem, wval)
                        wt(sy, "sG", _wf)
                        for dst, s in srcs(part):
                            inc(sy.dma_start(out=dst, in_=s), "wst", 16)
                        M[("wst", e, part)] = cnt["wst"]
                    SCref.add(q, sy_st)

                    def gp_ct(gp, part=part, e=e):
                        wt(gp, "wst", M[("wst", e, part)])
                        off = we_slot(e) + part * 4096
                        M[("wct", e, part)] = inc(
                            gp.tensor_copy(wbuf[:, off: off + 4096], wstage[:, :]), "sG")
                    SCref.add("gpsimd", gp_ct)
                    _wstfree[0] = M[("wct", e, part)]
                MW[e] = M[("wct", e, 2)]

            with ExitStack() as stD:
                DD = stD.enter_context
                xstD = DD(nc.sbuf_tensor("xstD", [P, 2 * 1024], F32))
                aostD = DD(nc.sbuf_tensor("aostD", [P, 2 * 1024], F32))
                xb2 = DD(nc.sbuf_tensor("xb2", [P, 1024], BF16))
                sq2 = DD(nc.sbuf_tensor("sq2", [P, 1024], BF16))
                tmpD = DD(nc.sbuf_tensor("tmpD", [P, 1024], BF16))
                a_bcD = DD(nc.sbuf_tensor("a_bcD", [P, 2 * P], BF16))
                b_bcD = DD(nc.sbuf_tensor("b_bcD", [P, 2 * P], BF16))
                mrowD = DD(nc.sbuf_tensor("mrowD", [1, P], F32))
                msqD = DD(nc.sbuf_tensor("msqD", [1, P], F32))
                varrD = DD(nc.sbuf_tensor("varrD", [1, P], F32))
                lnrD = DD(nc.sbuf_tensor("lnrD", [1, P], F32))
                arowD = DD(nc.sbuf_tensor("arowD", [1, P], BF16))
                browD = DD(nc.sbuf_tensor("browD", [1, P], BF16))
                sc0 = DD(nc.sbuf_tensor("sc0", [P, 8], F32))
                sce = DD(nc.sbuf_tensor("sce", [P, 8], F32))
                scs = DD(nc.sbuf_tensor("scs", [P, 8], F32))
                gm1 = DD(nc.sbuf_tensor("gm1", [P, 1], F32))
                gm2 = DD(nc.sbuf_tensor("gm2", [P, 1], F32))
                geq = DD(nc.sbuf_tensor("geq", [P, 8], F32))
                gs2 = DD(nc.sbuf_tensor("gs2", [P, 8], F32))
                sel01 = DD(nc.sbuf_tensor("sel01", [P, 8], F32))
                gden = DD(nc.sbuf_tensor("gden", [P, 1], F32))
                gwin = DD(nc.sbuf_tensor("gwin", [P, 1], F32))
                wful = DD(nc.sbuf_tensor("wful", [P, 8], F32))
                selT = DD(nc.sbuf_tensor("selT", [8, TC], F32))
                cwsb = DD(nc.sbuf_tensor("cwsb", [8, 2], F32))
                cntg = DD(nc.sbuf_tensor("cntg", [8, 1], F32))
                wsumg = DD(nc.sbuf_tensor("wsumg", [8, 1], F32))
                wmaxg = DD(nc.sbuf_tensor("wmaxg", [8, 1], F32))
                wmean = DD(nc.sbuf_tensor("wmean", [8, 1], F32))
                coefT = DD(nc.sbuf_tensor("coefT", [8, TC], BF16))
                sumD = DD(nc.psum_tensor("sumD", [1, P], F32))
                sqD = DD(nc.psum_tensor("sqD", [1, P], F32))
                abD_ps = DD(nc.psum_tensor("abD_ps", [P, P], F32))
                bbD_ps = DD(nc.psum_tensor("bbD_ps", [P, P], F32))
                glog = DD(nc.psum_tensor("glog", [P, 8], F32))
                selT_ps = DD(nc.psum_tensor("selT_ps", [8, P], F32))
                cnt_ps = DD(nc.psum_tensor("cnt_ps", [8, 1], F32))
                wsum_ps = DD(nc.psum_tensor("wsum_ps", [8, 1], F32))

                M = {}

                with nc.Block("phD", no_gpsimd_drain=True) as blk:
                    SC = Sched()
                    actload and SC.add("scalar", lambda se: act_load(se, "natural_log_exp_and_others"))
                    emit_wexp(SC, 0, "dG", 0, q="gpsimd")
                    emit_wexp(SC, 1, "dG", 0, q="gpsimd")

                    for it in range(NTD):
                        t0 = it * P
                        par = it % 2
                        po = par * 1024
                        ao = par * P

                        def sy_xd(sy, it=it, t0=t0, po=po):
                            if it >= 2:
                                wt(sy, "sV", M[("xb2c", it - 2)])
                            sx = "xd%d" % (it % 2)
                            for kc in range(NCH):
                                inc(sy.dma_start(out=xstD[:, po + kc * P: po + (kc + 1) * P],
                                                 in_=xin[kc * P:(kc + 1) * P, 8 + t0: 8 + t0 + P]), sx, 16)
                            for kc in range(NCH):
                                inc(sy.dma_start(out=aostD[:, po + kc * P: po + (kc + 1) * P],
                                                 in_=attout_d[kc * P:(kc + 1) * P, t0: t0 + P]), sx, 16)
                            M[("xd", it)] = cnt[sx]
                        SC.add("sync", sy_xd)

                        def ve_x2(ve, it=it, t0=t0, po=po):
                            wt(ve, "xd%d" % (it % 2), M[("xd", it)])
                            x3 = xstD[:, po:po + 1024].rearrange("p (c n) -> p c n", c=NCH)
                            a3 = aostD[:, po:po + 1024].rearrange("p (c n) -> p c n", c=NCH)
                            v1 = inc(ve.tensor_tensor(acc3(t0, P), x3, a3, ALU.add), "sV")
                            ve.wait_ge(sems["sV"], v1)
                            if it > 0:
                                wt(ve, "sP", M[("statD", it - 1)])
                                wt(ve, "sP", M[("gl", it - 1)])
                            v2 = inc(ve.tensor_copy(
                                xb2[:, :].rearrange("p (c n) -> p c n", c=NCH), acc3(t0, P)), "sV")
                            M[("xb2c", it)] = v2
                        SC.add("vector", ve_x2)

                        def se_sq2(se, it=it):
                            wt(se, "sV", M[("xb2c", it)])
                            if it > 0:
                                wt(se, "sP", M[("statD", it - 1)])
                            M[("sq2", it)] = inc(se.activation(sq2[:, :], xb2[:, :], AF.Square), "sA")
                        SC.add("scalar", se_sq2)

                        def pe_statD(pe, it=it):
                            wt(pe, "sA", M[("sq2", it)])
                            if it > 0:
                                wt(pe, "sA", M[("rowmD", it - 1)])
                                wt(pe, "sV", M[("rowvD", it - 1)])
                            for kc in range(NCH):
                                pe.matmul(sumD[0:1, :], ones_cb[:, :],
                                          xb2[:, kc * P:(kc + 1) * P],
                                          start=(kc == 0), stop=(kc == NCH - 1))
                            for kc in range(NCH):
                                mm = pe.matmul(sqD[0:1, :], ones_cb[:, :],
                                               sq2[:, kc * P:(kc + 1) * P],
                                               start=(kc == 0), stop=(kc == NCH - 1))
                            M[("statD", it)] = inc(mm, "sP")
                        SC.add("tensor", pe_statD)

                        def se_rowD(se, it=it):
                            wt(se, "sP", M[("statD", it)])
                            if it > 0:
                                wt(se, "sP", M[("abmmD", it - 1)])
                                wt(se, "sV", M[("rowbD", it - 1)])
                            a1 = inc(se.activation(mrowD[0:1, :], sumD[0:1, :],
                                                   AF.Copy, bias=0.0, scale=1.0 / DM), "sA")
                            se.wait_ge(sems["sA"], a1)
                            M[("rowmD", it)] = inc(se.activation(msqD[0:1, :], mrowD[0:1, :],
                                                                 AF.Square), "sA")
                        SC.add("scalar", se_rowD)

                        def ve_rowD(ve, it=it):
                            wt(ve, "sA", M[("rowmD", it)])
                            M[("rowvD", it)] = inc(
                                ve.scalar_tensor_tensor(varrD[0:1, :], sqD[0:1, :], 1.0 / DM,
                                                        msqD[0:1, :], ALU.mult, ALU.subtract), "sV")
                        SC.add("vector", ve_rowD)

                        def se_rowD2(se, it=it):
                            wt(se, "sV", M[("rowvD", it)])
                            a1 = inc(se.activation(lnrD[0:1, :], varrD[0:1, :],
                                                   AF.Ln, bias=eps_c[0:1, 0:1], scale=1.0), "sA")
                            se.wait_ge(sems["sA"], a1)
                            M[("rowaD", it)] = inc(se.activation(arowD[0:1, :], lnrD[0:1, :],
                                                                 AF.Exp, bias=0.0, scale=-0.5), "sA")
                        SC.add("scalar", se_rowD2)

                        def ve_rowD2(ve, it=it):
                            wt(ve, "sA", M[("rowaD", it)])
                            M[("rowbD", it)] = inc(
                                ve.scalar_tensor_tensor(browD[0:1, :], mrowD[0:1, :], -1.0,
                                                        arowD[0:1, :], ALU.mult, ALU.mult), "sV")
                        SC.add("vector", ve_rowD2)

                        def pe_abD(pe, it=it, par=par):
                            wt(pe, "sA", M[("rowaD", it)])
                            wt(pe, "sV", M[("rowbD", it)])
                            if it > 0:
                                wt(pe, "sA", M[("abevD", it - 1)])
                            pe.matmul(abD_ps[:, :], ones_rb[0:1, 0:P], arowD[0:1, :],
                                      start=True, stop=True)
                            mm = pe.matmul(bbD_ps[:, :], ones_rb[0:1, 0:P], browD[0:1, :],
                                           start=True, stop=True)
                            M[("abmmD", it)] = inc(mm, "sP")
                        SC.add("tensor", pe_abD)

                        def se_abevD(se, it=it, ao=ao):
                            wt(se, "sP", M[("abmmD", it)])
                            if it >= 2:
                                wt(se, "sV", M[("xln2", it - 2)])
                            se.copy(a_bcD[:, ao:ao + P], abD_ps[:, :])
                            M[("abevD", it)] = inc(se.copy(b_bcD[:, ao:ao + P], bbD_ps[:, :]), "sA")
                        SC.add("scalar", se_abevD)

                        def ve_xln2(ve, it=it, t0=t0, ao=ao):
                            wt(ve, "sA", M[("abevD", it)])
                            xb3 = xb2[:, :].rearrange("p (c n) -> p c n", c=NCH)
                            a3 = a_bcD[:, ao:ao + P][:, None, :].broadcast_to([P, NCH, P])
                            t3 = tmpD[:, :].rearrange("p (c n) -> p c n", c=NCH)
                            v1 = inc(ve.tensor_tensor(t3, xb3, a3, ALU.mult), "sV")
                            ve.wait_ge(sems["sV"], v1)
                            o3 = xln[:, :].rearrange("p (c n) -> p c n", c=NCH)[:, :, t0:t0 + P]
                            b3 = b_bcD[:, ao:ao + P][:, None, :].broadcast_to([P, NCH, P])
                            M[("xln2", it)] = inc(ve.tensor_tensor(o3, t3, b3, ALU.add), "sV")
                        SC.add("vector", ve_xln2)

                        def pe_gl(pe, it=it, t0=t0):
                            wt(pe, "sV", M[("xln2", it)])
                            if it > 0:
                                wt(pe, "sV", M[("gl_read", it - 1)])
                            for kc in range(NCH):
                                mm = pe.matmul(glog[:, :], xslc(kc, t0, P),
                                               gw_sb[:, kc * 8:(kc + 1) * 8],
                                               start=(kc == 0), stop=(kc == NCH - 1))
                            M[("gl", it)] = inc(mm, "sP")
                        SC.add("tensor", pe_gl)

                        def ve_glread(ve, it=it):
                            wt(ve, "sP", M[("gl", it)])
                            M[("gl_read", it)] = inc(
                                ve.tensor_tensor(sc0[:, :], glog[:, :], lb_bc[:, :], ALU.add), "sV")
                        SC.add("vector", ve_glread)

                        def se_sig(se, it=it):
                            wt(se, "sV", M[("gl_read", it)])
                            M[("sce", it)] = inc(se.activation(sce[:, :], sc0[:, :],
                                                               AF.Exp, bias=0.0, scale=-1.0), "sA")
                        SC.add("scalar", se_sig)

                        def ve_top2(ve, it=it):
                            wt(ve, "sA", M[("sce", it)])
                            v = inc(ve.tensor_scalar_add(sce[:, :], sce[:, :], 1.0), "sV")
                            ve.wait_ge(sems["sV"], v)
                            v = inc(ve.reciprocal(scs[:, :], sce[:, :]), "sV")
                            ve.wait_ge(sems["sV"], v)
                            v = inc(ve.reduce_max(gm1[:, :], scs[:, :], AX.X), "sV")
                            ve.wait_ge(sems["sV"], v)
                            v = inc(ve.tensor_single_scalar(geq[:, :], scs[:, :], gm1[:, :],
                                                            ALU.is_equal), "sV")
                            ve.wait_ge(sems["sV"], v)
                            v = inc(ve.scalar_tensor_tensor(gs2[:, :], geq[:, :], 1e9, scs[:, :],
                                                            ALU.mult, ALU.subtract), "sV")
                            ve.wait_ge(sems["sV"], v)
                            v = inc(ve.tensor_reduce(gm2[:, :], gs2[:, :], AX.X, ALU.min,
                                                     negate=True), "sV")
                            ve.wait_ge(sems["sV"], v)
                            v = inc(ve.tensor_single_scalar(sel01[:, :], scs[:, :], gm2[:, :],
                                                            ALU.is_ge), "sV")
                            ve.wait_ge(sems["sV"], v)
                            v = inc(ve.tensor_scalar(gden[:, :], gm1[:, :], gm2[:, :], 1e-8,
                                                     ALU.add, ALU.add), "sV")
                            ve.wait_ge(sems["sV"], v)
                            v = inc(ve.reciprocal(gwin[:, :], gden[:, :]), "sV")
                            ve.wait_ge(sems["sV"], v)
                            v = inc(ve.scalar_tensor_tensor(wful[:, :], scs[:, :], gwin[:, :],
                                                            sel01[:, :], ALU.mult, ALU.mult), "sV")
                            M[("top2", it)] = v
                        SC.add("vector", ve_top2)

                        def pe_gstat(pe, it=it):
                            wt(pe, "sV", M[("top2", it)])
                            pe.matmul(cnt_ps[0:8, 0:1], sel01[:, :], ones_cf[:, :],
                                      start=(it == 0), stop=(it == NTD - 1))
                            pe.matmul(wsum_ps[0:8, 0:1], wful[:, :], ones_cf[:, :],
                                      start=(it == 0), stop=(it == NTD - 1))
                            if it > 0:
                                wt(pe, "sA", M[("selTe", it - 1)])
                            mm = pe.transpose(selT_ps[0:8, 0:P], sel01[0:P, 0:8], id_f[0:P, 0:P])
                            M[("gstat", it)] = inc(mm, "sP")
                        SC.add("tensor", pe_gstat)

                        def se_selT(se, it=it):
                            wt(se, "sP", M[("gstat", it)])
                            M[("selTe", it)] = inc(
                                se.copy(selT[:, it * P:(it + 1) * P], selT_ps[0:8, :]), "sA")
                        SC.add("scalar", se_selT)

                    def se_cw(se):
                        wt(se, "sP", M[("gstat", NTD - 1)])
                        inc(se.copy(cwsb[:, 0:1], cnt_ps[0:8, :]), "sA")
                        M["cw"] = inc(se.copy(cwsb[:, 1:2], wsum_ps[0:8, :]), "sA")
                    SC.add("scalar", se_cw)

                    def gp_cc(gp):
                        wt(gp, "sA", M["cw"])
                        inc(gp.dma_start(out=ccin_d[0:8, :], in_=cwsb[:, 0:1]), "dG", 16)
                        d = inc(gp.dma_start(out=ccin_d[8:16, :], in_=cwsb[:, 1:2]), "dG", 16)
                        gp.wait_ge(sems["dG"], d)
                        c = inc(gp.collective_compute(
                            "AllReduce", ALU.add, replica_groups=[list(range(8))],
                            ins=[ccin_d[:, :].opt()], outs=[ccout_d[:, :].opt()]), "cc")
                        gp.wait_ge(sems["cc"], c)
                        inc(gp.dma_start(out=cntg[:, :], in_=ccout_d[0:8, :]), "dG", 16)
                        M["ccd"] = inc(gp.dma_start(out=wsumg[:, :], in_=ccout_d[8:16, :]), "dG", 16)
                    SC.add("gpsimd", gp_cc)

                    def ve_wm(ve):
                        wt(ve, "dG", M["ccd"])
                        v = inc(ve.tensor_scalar_max(wmaxg[:, :], cntg[:, :], 1.0), "sV")
                        ve.wait_ge(sems["sV"], v)
                        v = inc(ve.reciprocal(wmaxg[:, :], wmaxg[:, :]), "sV")
                        ve.wait_ge(sems["sV"], v)
                        v = inc(ve.tensor_tensor(wmean[:, :], wsumg[:, :], wmaxg[:, :], ALU.mult), "sV")
                        ve.wait_ge(sems["sV"], v)
                        wt(ve, "sA", M[("selTe", NTD - 1)])
                        M["coef"] = inc(ve.tensor_scalar_mul(coefT[:, :], selT[:, :],
                                                             wmean[:, :]), "sV")
                    SC.add("vector", ve_wm)

                    def gp_coef(gp):
                        wt(gp, "sV", M["coef"])
                        MW["coefd"] = inc(gp.dma_start(out=coefscr_d[:, :], in_=coefT[:, :]), "dG", 16)
                    SC.add("gpsimd", gp_coef)

                    def sy_endD(sy):
                        wt(sy, "dS", cnt["dS"])
                    SC.add("sync", sy_endD)

                    def gp_endD(gp):
                        wt(gp, "dG", cnt["dG"])
                    SC.add("gpsimd", gp_endD)
                    SC.flush(blk)

            if stage == "xln2":
                with nc.Block("phDbg2") as blk:
                    def sy_d3(sy):
                        d = inc(sy.dma_start(out=ydbgb_d[:, :], in_=xln[:, :]), "dS", 16)
                        sy.wait_ge(sems["dS"], d)
                    blk.sync(sy_d3)
                return nc

            if stage == "d":
                with nc.Block("phOutD") as blk:
                    def sy_d2(sy):
                        for hc in range(NCH):
                            inc(sy.dma_start(out=y_d[hc * P:(hc + 1) * P, :],
                                             in_=acc[:, hc * TC:(hc + 1) * TC]), "dS", 16)
                        wt(sy, "dS", cnt["dS"])
                    blk.sync(sy_d2)
                return nc

            with ExitStack() as stE:
                EE = stE.enter_context
                NTE2 = 4
                h_p = EE(nc.sbuf_tensor("h_p", [P, 2 * 4 * 512], BF16))
                sil = EE(nc.sbuf_tensor("sil", [P, 2 * 512], BF16))
                sp_ = EE(nc.sbuf_tensor("sp_", [P, 2 * 512], BF16))
                bc = EE(nc.sbuf_tensor("bc", [P, 2 * 512], BF16))
                g_ps = EE(nc.psum_tensor("g_ps", [P, 1024], F32))
                u_ps = EE(nc.psum_tensor("u_ps", [P, 1024], F32))
                down_ps = EE(nc.psum_tensor("down_ps", [P, 2048], F32))

                M = {}
                last_g = [0, 0]
                last_u = [0, 0]
                last_sil = [0, 0]
                last_sp = [0, 0]
                last_bc = [0, 0]
                last_dn_ic = [0, 0, 0, 0]
                last_acc = 0

                with nc.Block("phE", no_gpsimd_drain=True) as blk:
                    SC = Sched()
                    actload and SC.add("scalar", lambda se: act_load(se, "silu_and_others"))

                    NG = NEXP * NTE2

                    def emit_bc(g):
                        e, it = g // NTE2, g % NTE2
                        t0 = it * 512
                        par = it % 2
                        if e >= 8:
                            return

                        def se_bc(se, e=e, it=it, t0=t0, par=par, _lbc=last_bc[par]):
                            wt(se, "dG", MW["coefd"])
                            wt(se, "sV", _lbc)
                            sb = "bc%d" % par
                            inc(se.dma_start(
                                out=bc[:, par * 512:(par + 1) * 512],
                                in_=coefscr_d[e, t0:t0 + 512][None, :].broadcast_to([P, 512])), sb, 16)
                            M[("bcd", e, it)] = cnt[sb]
                        SC.add("scalar", se_bc)

                    def emit_gu(g):
                        e, it = g // NTE2, g % NTE2
                        t0 = it * 512
                        par = it % 2
                        so = we_slot(e)
                        for ic in range(4):
                            sl = ic % 2

                            def pe_gu(pe, e=e, it=it, t0=t0, ic=ic, so=so, sl=sl,
                                      _lg=last_g[sl], _lu=last_u[sl]):
                                if it == 0 and ic == 0:
                                    wt(pe, "sG", MW[e])
                                wt(pe, "sA", _lg)
                                wt(pe, "sV", _lu)
                                for kc in range(NCH):
                                    pe.matmul(g_ps[:, sl * 512:(sl + 1) * 512],
                                              wbuf[:, so + kc * 512 + ic * P: so + kc * 512 + (ic + 1) * P],
                                              xslc(kc, t0, 512), start=(kc == 0), stop=(kc == NCH - 1))
                                for kc in range(NCH):
                                    mm = pe.matmul(u_ps[:, sl * 512:(sl + 1) * 512],
                                                   wbuf[:, so + 4096 + kc * 512 + ic * P: so + 4096 + kc * 512 + (ic + 1) * P],
                                                   xslc(kc, t0, 512), start=(kc == 0), stop=(kc == NCH - 1))
                                M[("gu", e, it, ic)] = inc(mm, "sP")
                            SC.add("tensor", pe_gu)

                            def se_sil(se, e=e, it=it, ic=ic, sl=sl, _ls=last_sil[sl]):
                                wt(se, "sP", M[("gu", e, it, ic)])
                                wt(se, "sV", _ls)
                                M[("sil", e, it, ic)] = inc(
                                    se.activation(sil[:, sl * 512:(sl + 1) * 512],
                                                  g_ps[:, sl * 512:(sl + 1) * 512], AF.Silu), "sA")
                            SC.add("scalar", se_sil)
                            last_g[sl] = M[("sil", e, it, ic)]

                            def ve_h(ve, e=e, it=it, ic=ic, par=par, sl=sl,
                                     _lsp=last_sp[sl], _ldn=last_dn_ic[ic]):
                                wt(ve, "sA", M[("sil", e, it, ic)])
                                if e < 8:
                                    wt(ve, "bc%d" % par, M[("bcd", e, it)])
                                    wt(ve, "sV", _lsp)
                                    v = inc(ve.tensor_tensor(sp_[:, sl * 512:(sl + 1) * 512],
                                                             sil[:, sl * 512:(sl + 1) * 512],
                                                             bc[:, par * 512:(par + 1) * 512],
                                                             ALU.mult), "sV")
                                    M[("sp", e, it, ic)] = v
                                    ve.wait_ge(sems["sV"], v)
                                    srcm = sp_[:, sl * 512:(sl + 1) * 512]
                                else:
                                    srcm = sil[:, sl * 512:(sl + 1) * 512]
                                wt(ve, "sP", _ldn)
                                ho = (it % 2) * 2048
                                v = inc(ve.tensor_tensor(h_p[:, ho + ic * 512: ho + (ic + 1) * 512],
                                                         srcm, u_ps[:, sl * 512:(sl + 1) * 512],
                                                         ALU.mult), "sV")
                                M[("hp", e, it, ic)] = v
                            SC.add("vector", ve_h)
                            last_sil[sl] = M[("hp", e, it, ic)]
                            last_u[sl] = M[("hp", e, it, ic)]
                            if e < 8:
                                last_sp[sl] = M[("hp", e, it, ic)]
                                last_bc[par] = M[("sp", e, it, ic)]

                    def emit_dn(g, h2):
                        e, it = g // NTE2, g % NTE2
                        t0 = it * 512
                        so = we_slot(e)

                        def pe_dn(pe, e=e, it=it, so=so, h2=h2, _la=last_acc):
                            if h2 == 0:
                                wt(pe, "sV", _la)
                            else:
                                wt(pe, "sV", M[("acc", e, it, 0)])
                            ho = (it % 2) * 2048
                            for ic in range(4):
                                if h2 == 0:
                                    wt(pe, "sV", M[("hp", e, it, ic)])
                                for j in range(4):
                                    hc = h2 * 4 + j
                                    mm = pe.matmul(
                                        down_ps[:, j * 512:(j + 1) * 512],
                                        wbuf[:, so + 8192 + ic * DM + hc * P: so + 8192 + ic * DM + (hc + 1) * P],
                                        h_p[:, ho + ic * 512: ho + (ic + 1) * 512],
                                        start=(ic == 0), stop=(ic == 3))
                            M[("dn", e, it, h2)] = inc(mm, "sP")
                        SC.add("tensor", pe_dn)
                        if h2 == 1:
                            for ic in range(4):
                                last_dn_ic[ic] = M[("dn", e, it, 1)]

                        def ve_acc(ve, e=e, it=it, t0=t0, h2=h2):
                            wt(ve, "sP", M[("dn", e, it, h2)])
                            a3 = acc[:, :].rearrange("p (c n) -> p c n", c=NCH)[
                                :, h2 * 4:(h2 + 1) * 4, t0:t0 + 512]
                            d3 = down_ps[:, :].rearrange("p (c n) -> p c n", c=4)
                            M[("acc", e, it, h2)] = inc(
                                ve.tensor_tensor(a3, a3, d3, ALU.add), "sV")
                        SC.add("vector", ve_acc)

                    def emit_wn(e):
                        emit_wexp(SC, e + 2, "sP", M[("dn", e, NTE2 - 1, 1)])

                    emit_bc(0)
                    emit_gu(0)
                    for g in range(NG):
                        if g + 1 < NG:
                            emit_bc(g + 1)
                            emit_gu(g + 1)
                        emit_dn(g, 0)
                        emit_dn(g, 1)
                        last_acc = M[("acc", g // NTE2, g % NTE2, 1)]
                        if g % NTE2 == NTE2 - 1 and g // NTE2 + 2 < NEXP:
                            emit_wn(g // NTE2)

                    def sy_out(sy):
                        for it in range(NTE2):
                            wt(sy, "sV", M[("acc", NEXP - 1, it, 1)])
                        for hc in range(NCH):
                            inc(sy.dma_start(out=y_d[hc * P:(hc + 1) * P, :],
                                             in_=acc[:, hc * TC:(hc + 1) * TC]), "dS", 16)
                        wt(sy, "dS", cnt["dS"])
                    SC.add("sync", sy_out)

                    def gp_endE(gp):
                        wt(gp, "dG", cnt["dG"])
                    SC.add("gpsimd", gp_endE)
                    SC.flush(blk)

    return nc



def _prep_inputs(L0_attn_x, L0_attn_qkv_w, L0_attn_qkv_b, L0_attn_out_w, L0_attn_out_b,
                 L0_ffn_gate_w, L0_ffn_lb_bias, L0_ffn_expert_gate_w, L0_ffn_expert_up_w,
                 L0_ffn_expert_down_w, L0_ffn_shared_gate_w, L0_ffn_shared_up_w,
                 L0_ffn_shared_down_w):
    x = np.asarray(L0_attn_x, dtype=np.float32)
    Bb, N, D = x.shape
    xd = np.ascontiguousarray(x.reshape(Bb, N // 2, 2, D).transpose(0, 2, 1, 3))

    shared = {
        "qkvT": np.ascontiguousarray(np.asarray(L0_attn_qkv_w, np.float32).T),
        "outwT": np.ascontiguousarray(np.asarray(L0_attn_out_w, np.float32).T),
        "qkvb": np.ascontiguousarray(np.asarray(L0_attn_qkv_b, np.float32).reshape(1, 3 * DM)),
        "outb": np.ascontiguousarray(np.asarray(L0_attn_out_b, np.float32).reshape(1, DM)),
        "gwT": np.ascontiguousarray(np.asarray(L0_ffn_gate_w, np.float32).T),
        "lbbc": np.ascontiguousarray(np.broadcast_to(
            np.asarray(L0_ffn_lb_bias, np.float32).reshape(1, 8), (P, 8))),
        "egwT": np.ascontiguousarray(np.asarray(L0_ffn_expert_gate_w, np.float32).transpose(0, 2, 1)),
        "euwT": np.ascontiguousarray(np.asarray(L0_ffn_expert_up_w, np.float32).transpose(0, 2, 1)),
        "edwT": np.ascontiguousarray(np.asarray(L0_ffn_expert_down_w, np.float32).transpose(0, 2, 1)),
        "sgwT": np.ascontiguousarray(np.asarray(L0_ffn_shared_gate_w, np.float32).T),
        "suwT": np.ascontiguousarray(np.asarray(L0_ffn_shared_up_w, np.float32).T),
        "sdwT": np.ascontiguousarray(np.asarray(L0_ffn_shared_down_w, np.float32).T),
        "idbf": np.eye(P, dtype=BFD),
        "idf": np.eye(P, dtype=np.float32),
    }
    selpk = np.zeros((P, P), dtype=BFD)
    for pc in range(8):
        for d in range(P):
            h = 2 * pc + (1 if d >= 64 else 0)
            selpk[d, pc * 16 + h] = 1.0
    shared["selpk"] = selpk

    def band(nq):
        r = np.arange(nq)[:, None]
        j = np.arange(P)[None, :]
        return (j >= r) & (j <= r + 16)

    in_maps = []
    halfN = N // 2
    for c in range(8):
        b = c >> 2
        par = (c >> 1) & 1
        half = c & 1
        start = half * TC
        lo, hi = start - 8, start + 2168
        xs = np.zeros((EXT, DM), dtype=np.float32)
        s0, s1 = max(lo, 0), min(hi, halfN)
        xs[s0 - lo: s1 - lo, :] = xd[b, par, s0:s1, :]

        mfirst = band(112)
        if half == 0:
            mfirst = mfirst & (np.arange(P)[None, :] >= 8)
        mlast = band(32)
        if half == 1:
            mlast = mlast & (np.arange(P)[None, :] <= 39)
        m = dict(shared)
        m["xin"] = np.ascontiguousarray(xs.T)
        m["mfirst"] = mfirst.astype(BFD)
        m["mmid"] = band(112).astype(BFD)
        m["mlast"] = mlast.astype(BFD)
        in_maps.append(m)
    return in_maps, (Bb, N, D)


def kernel(**inputs):
    in_maps, (Bb, N, D) = _prep_inputs(**inputs)
    nc = build_nc("full")
    res = run_bass_kernel_spmd(nc, in_maps, core_ids=list(range(8)))
    out = np.zeros((Bb, 2, N // 2, D), dtype=np.float32)
    for c in range(8):
        b = c >> 2
        par = (c >> 1) & 1
        half = c & 1
        y = np.asarray(res.results[c]["y"])
        out[b, par, half * TC:(half + 1) * TC, :] = y.T
    return np.ascontiguousarray(out.transpose(0, 2, 1, 3).reshape(Bb, N, D))


# version 10
# speedup vs baseline: 1.2027x; 1.0813x over previous
import numpy as np
import ml_dtypes
from contextlib import ExitStack

import concourse.bass as bass
import concourse.mybir as mybir
from concourse.bass_utils import run_bass_kernel_spmd

F32 = mybir.dt.float32
BF16 = mybir.dt.bfloat16
AF = mybir.ActivationFunctionType
ALU = mybir.AluOpType
AX = mybir.AxisListType
BFD = ml_dtypes.bfloat16

P = 128
DM = 1024
NCH = 8
EXT = 2176
TC = 2048
QB = [(i * 112, 112) for i in range(18)] + [(2016, 32)]
NQB = len(QB)
NTD = 16
NTE = 8
NEXP = 10
LN_EPS = 1e-5


def build_nc(stage="full", nqb=NQB, bstop=99, actload=True, bcmemset=False):
    nc = bass.Bass("TRN2", target_bir_lowering=False)

    xin = nc.dram_tensor("xin", [DM, EXT], F32, kind="ExternalInput")
    qkvT_d = nc.dram_tensor("qkvT", [DM, 3 * DM], F32, kind="ExternalInput")
    outwT_d = nc.dram_tensor("outwT", [DM, DM], F32, kind="ExternalInput")
    qkvb_d = nc.dram_tensor("qkvb", [1, 3 * DM], F32, kind="ExternalInput")
    outb_d = nc.dram_tensor("outb", [1, DM], F32, kind="ExternalInput")
    gwT_d = nc.dram_tensor("gwT", [DM, 8], F32, kind="ExternalInput")
    lbbc_d = nc.dram_tensor("lbbc", [P, 8], F32, kind="ExternalInput")
    egwT_d = nc.dram_tensor("egwT", [8, DM, 512], F32, kind="ExternalInput")
    euwT_d = nc.dram_tensor("euwT", [8, DM, 512], F32, kind="ExternalInput")
    edwT_d = nc.dram_tensor("edwT", [8, 512, DM], F32, kind="ExternalInput")
    sgwT_d = nc.dram_tensor("sgwT", [DM, DM], F32, kind="ExternalInput")
    suwT_d = nc.dram_tensor("suwT", [DM, DM], F32, kind="ExternalInput")
    sdwT_d = nc.dram_tensor("sdwT", [DM, DM], F32, kind="ExternalInput")
    idbf_d = nc.dram_tensor("idbf", [P, P], BF16, kind="ExternalInput")
    idf_d = nc.dram_tensor("idf", [P, P], F32, kind="ExternalInput")
    selpk_d = nc.dram_tensor("selpk", [P, P], BF16, kind="ExternalInput")
    mfirst_d = nc.dram_tensor("mfirst", [112, P], BF16, kind="ExternalInput")
    mmid_d = nc.dram_tensor("mmid", [112, P], BF16, kind="ExternalInput")
    mlast_d = nc.dram_tensor("mlast", [32, P], BF16, kind="ExternalInput")

    y_d = nc.dram_tensor("y", [DM, TC], F32, kind="ExternalOutput")
    ydbg_d = nc.dram_tensor("ydbg", [P, 18048], F32, kind="ExternalOutput")
    ydbgb_d = nc.dram_tensor("ydbgb", [P, NCH * EXT], BF16, kind="ExternalOutput")

    attout_d = nc.dram_tensor("attout", [DM, TC], F32, kind="Internal")
    rnscr_d = nc.dram_tensor("rnscr", [4, 16, P], BF16, kind="Internal")
    abscr_d = nc.dram_tensor("abscr", [2, 2, 512], BF16, kind="Internal")
    coefscr_d = nc.dram_tensor("coefscr", [8, TC], BF16, kind="Internal")
    ccin_d = nc.dram_tensor("ccin", [16, 1], F32, kind="Internal")
    ccout_d = nc.dram_tensor("ccout", [16, 1], F32, kind="Internal")

    st0 = ExitStack()
    with st0:
        E0 = st0.enter_context

        sems = {}
        cnt = {}
        for s in ("dS", "dG", "sP", "sV", "sA", "sG", "cc",
                  "xA0", "xA1", "bq", "bk", "xd0", "xd1", "bc0", "bc1", "rw", "wst"):
            sems[s] = E0(nc.semaphore(s))
            cnt[s] = 0

        def inc(instr, s, n=1):
            if instr is not None:
                instr.then_inc(sems[s], n)
            cnt[s] += n
            return cnt[s]

        def wt(eng, s, v):
            if v > 0:
                eng.wait_ge(sems[s], v)

        class _DummyInstr:
            def then_inc(self, *a, **k):
                return self

        class _Dummy:
            def __getattr__(self, name):
                def f(*a, **k):
                    return None
                return f

        _DUM = _Dummy()

        class Sched:

            ORDER = ("sync", "gpsimd", "scalar", "tensor", "vector")

            def __init__(self):
                self.fns = {n: [] for n in self.ORDER}
                self.snap = dict(cnt)

            def add(self, eng, fn):
                self.fns[eng].append(fn)
                fn(_DUM)

            def flush(self, blk):
                end = dict(cnt)
                for k, v in self.snap.items():
                    cnt[k] = v
                for eng in self.ORDER:
                    fns = self.fns[eng]
                    if fns:
                        getattr(blk, eng)(lambda e, fns=fns: [f(e) for f in fns])
                for k, v in end.items():
                    assert cnt[k] == v, (k, cnt[k], v)

        def act_load(se, set_name):
            from concourse.hw_specs import get_activation_tables
            for idx, nm in enumerate(get_activation_tables(nc.m.arch)):
                if nm == set_name:
                    se.add_instruction(mybir.InstLoadActFuncSet(
                        name=nc.get_next_instruction_name(),
                        act_func_set_id=idx, ins=[], outs=[]))
                    return
            raise ValueError(set_name)

        xln = E0(nc.sbuf_tensor("xln", [P, NCH * EXT], BF16))
        wbuf = E0(nc.sbuf_tensor("wbuf", [P, 24576], BF16))
        wstage = E0(nc.sbuf_tensor("wstage", [P, 4096], F32))
        gw_sb = E0(nc.sbuf_tensor("gw_sb", [P, 64], BF16))
        lb_bc = E0(nc.sbuf_tensor("lb_bc", [P, 8], F32))
        id_bf = E0(nc.sbuf_tensor("id_bf", [P, P], BF16))
        id_f = E0(nc.sbuf_tensor("id_f", [P, P], F32))
        selpk = E0(nc.sbuf_tensor("selpk_s", [P, P], BF16))
        m_first = E0(nc.sbuf_tensor("mfirst_s", [112, P], BF16))
        m_mid = E0(nc.sbuf_tensor("mmid_s", [112, P], BF16))
        m_last = E0(nc.sbuf_tensor("mlast_s", [32, P], BF16))
        ones_rb = E0(nc.sbuf_tensor("ones_rb", [1, 512], BF16))
        ones_cb = E0(nc.sbuf_tensor("ones_cb", [P, 1], BF16))
        ones_cf = E0(nc.sbuf_tensor("ones_cf", [P, 1], F32))
        eps_c = E0(nc.sbuf_tensor("eps_c", [P, 1], F32))

        def xslc(kc, t0, n):
            return xln[:, kc * EXT + t0: kc * EXT + t0 + n]

        with ExitStack() as stA:
            EA = stA.enter_context
            xst = EA(nc.sbuf_tensor("xst", [P, 2 * 4096], F32))
            xbt = EA(nc.sbuf_tensor("xbt", [P, 2 * 4096], BF16))
            sqt = EA(nc.sbuf_tensor("sqt", [P, 4096], BF16))
            tmpA = EA(nc.sbuf_tensor("tmpA", [P, 4096], BF16))
            a_bc = EA(nc.sbuf_tensor("a_bc", [P, 2 * 512], BF16))
            b_bc = EA(nc.sbuf_tensor("b_bc", [P, 2 * 512], BF16))
            mrow = EA(nc.sbuf_tensor("mrow", [1, 512], F32))
            msq = EA(nc.sbuf_tensor("msq", [1, 512], F32))
            varr = EA(nc.sbuf_tensor("varr", [1, 512], F32))
            lnr = EA(nc.sbuf_tensor("lnr", [1, 512], F32))
            arow = EA(nc.sbuf_tensor("arow", [1, 512], BF16))
            brow = EA(nc.sbuf_tensor("brow", [1, 512], BF16))
            sum_ps = EA(nc.psum_tensor("sum_ps", [1, 512], F32))
            sq_ps = EA(nc.psum_tensor("sq_ps", [1, 512], F32))
            ab_ps = EA(nc.psum_tensor("ab_ps", [P, 512], F32))
            bb_ps = EA(nc.psum_tensor("bb_ps", [P, 512], F32))

            tilesA = [(0, 512), (512, 512), (1024, 512), (1536, 512), (2048, 128)]
            M = {}

            with nc.Block("phA", no_gpsimd_drain=True) as blk:
                SC = Sched()
                actload and SC.add("scalar", lambda se: act_load(se, "natural_log_exp_and_others"))
                def gp_w(gp):
                    for dst, src in ((id_bf, idbf_d), (id_f, idf_d), (selpk, selpk_d),
                                     (m_first, mfirst_d), (m_mid, mmid_d), (m_last, mlast_d),
                                     (lb_bc, lbbc_d)):
                        inc(gp.dma_start(out=dst[:, :], in_=src[:, :]), "dG", 16)
                    for kc in range(NCH):
                        d = inc(gp.dma_start(out=wstage[:, 0:3072],
                                             in_=qkvT_d[kc * P:(kc + 1) * P, :]), "dG", 16)
                        gp.wait_ge(sems["dG"], d)
                        inc(gp.tensor_copy(wbuf[:, kc * 3072:(kc + 1) * 3072],
                                           wstage[:, 0:3072]), "sG")
                    for kc in range(NCH):
                        d = inc(gp.dma_start(out=wstage[:, kc * 8: kc * 8 + 8],
                                             in_=gwT_d[kc * P:(kc + 1) * P, :]), "dG", 16)
                    gp.wait_ge(sems["dG"], d)
                    M["gwcast"] = inc(gp.tensor_copy(gw_sb[:, :], wstage[:, 0:64]), "sG")
                SC.add("gpsimd", gp_w)

                def ve_c(ve):
                    inc(ve.memset(ones_rb[:, :], 1.0), "sV")
                    inc(ve.memset(ones_cb[:, :], 1.0), "sV")
                    inc(ve.memset(ones_cf[:, :], 1.0), "sV")
                    M["ones"] = inc(ve.memset(eps_c[:, :], LN_EPS), "sV")
                SC.add("vector", ve_c)

                for it, (c0, nw) in enumerate(tilesA):
                    par = it % 2
                    po = par * 4096
                    ao = par * 512

                    def sy_x(sy, it=it, c0=c0, nw=nw, po=po):
                        if it >= 2:
                            wt(sy, "sV", M[("xln", it - 2)])
                        sx = "xA%d" % (it % 2)
                        for kc in range(NCH):
                            inc(sy.dma_start(out=xst[:, po + kc * 512: po + kc * 512 + nw],
                                             in_=xin[kc * P:(kc + 1) * P, c0:c0 + nw]), sx, 16)
                        M[("x", it)] = cnt[sx]
                    SC.add("sync", sy_x)

                    def ve_cast(ve, it=it, nw=nw, po=po):
                        wt(ve, "xA%d" % (it % 2), M[("x", it)])
                        if it >= 2:
                            wt(ve, "sP", M[("stat", it - 2)])
                        src = xst[:, po:po + 4096].rearrange("p (c n) -> p c n", c=NCH)[:, :, 0:nw]
                        dst = xbt[:, po:po + 4096].rearrange("p (c n) -> p c n", c=NCH)[:, :, 0:nw]
                        M[("cast", it)] = inc(ve.tensor_copy(dst, src), "sV")
                    SC.add("vector", ve_cast)

                    def se_sq(se, it=it, nw=nw, po=po):
                        wt(se, "sV", M[("cast", it)])
                        if it > 0:
                            wt(se, "sP", M[("stat", it - 1)])
                        src = xbt[:, po:po + 4096].rearrange("p (c n) -> p c n", c=NCH)[:, :, 0:nw]
                        dst = sqt[:, :].rearrange("p (c n) -> p c n", c=NCH)[:, :, 0:nw]
                        M[("sq", it)] = inc(se.activation(dst, src, AF.Square), "sA")
                    SC.add("scalar", se_sq)

                    def pe_stat(pe, it=it, nw=nw, po=po):
                        wt(pe, "sV", max(M[("cast", it)], M["ones"]))
                        wt(pe, "sA", M[("sq", it)])
                        if it > 0:
                            wt(pe, "sA", M[("rowm", it - 1)])
                            wt(pe, "sV", M[("rowv", it - 1)])
                        for kc in range(NCH):
                            pe.matmul(sum_ps[0:1, 0:nw], ones_cb[:, :],
                                      xbt[:, po + kc * 512: po + kc * 512 + nw],
                                      start=(kc == 0), stop=(kc == NCH - 1))
                        for kc in range(NCH):
                            mm = pe.matmul(sq_ps[0:1, 0:nw], ones_cb[:, :],
                                           sqt[:, kc * 512: kc * 512 + nw],
                                           start=(kc == 0), stop=(kc == NCH - 1))
                        M[("stat", it)] = inc(mm, "sP")
                    SC.add("tensor", pe_stat)

                    def se_row(se, it=it, nw=nw):
                        wt(se, "sP", M[("stat", it)])
                        if it > 0:
                            wt(se, "sP", M[("abmm", it - 1)])
                            wt(se, "sV", M[("rowb", it - 1)])
                        a1 = inc(se.activation(mrow[0:1, 0:nw], sum_ps[0:1, 0:nw],
                                               AF.Copy, bias=0.0, scale=1.0 / DM), "sA")
                        se.wait_ge(sems["sA"], a1)
                        M[("rowm", it)] = inc(se.activation(msq[0:1, 0:nw], mrow[0:1, 0:nw],
                                                            AF.Square), "sA")
                    SC.add("scalar", se_row)

                    def ve_row(ve, it=it, nw=nw):
                        wt(ve, "sA", M[("rowm", it)])
                        M[("rowv", it)] = inc(
                            ve.scalar_tensor_tensor(varr[0:1, 0:nw], sq_ps[0:1, 0:nw],
                                                    1.0 / DM, msq[0:1, 0:nw],
                                                    ALU.mult, ALU.subtract), "sV")
                    SC.add("vector", ve_row)

                    def se_row2(se, it=it, nw=nw):
                        wt(se, "sV", M[("rowv", it)])
                        a1 = inc(se.activation(lnr[0:1, 0:nw], varr[0:1, 0:nw],
                                               AF.Ln, bias=eps_c[0:1, 0:1], scale=1.0), "sA")
                        se.wait_ge(sems["sA"], a1)
                        M[("rowa", it)] = inc(se.activation(arow[0:1, 0:nw], lnr[0:1, 0:nw],
                                                            AF.Exp, bias=0.0, scale=-0.5), "sA")
                    SC.add("scalar", se_row2)

                    def ve_row2(ve, it=it, nw=nw):
                        wt(ve, "sA", M[("rowa", it)])
                        M[("rowb", it)] = inc(
                            ve.scalar_tensor_tensor(brow[0:1, 0:nw], mrow[0:1, 0:nw],
                                                    -1.0, arow[0:1, 0:nw],
                                                    ALU.mult, ALU.mult), "sV")
                    SC.add("vector", ve_row2)

                    def pe_ab(pe, it=it, nw=nw, par=par):
                        wt(pe, "sA", M[("rowa", it)])
                        wt(pe, "sV", M[("rowb", it)])
                        if it > 0:
                            wt(pe, "sA", M[("abev", it - 1)])
                        pe.matmul(ab_ps[:, 0:nw], ones_rb[0:1, 0:P], arow[0:1, 0:nw],
                                  start=True, stop=True)
                        mm = pe.matmul(bb_ps[:, 0:nw], ones_rb[0:1, 0:P], brow[0:1, 0:nw],
                                       start=True, stop=True)
                        M[("abmm", it)] = inc(mm, "sP")
                    SC.add("tensor", pe_ab)

                    def se_abev(se, it=it, nw=nw, par=par):
                        wt(se, "sP", M[("abmm", it)])
                        if it >= 2:
                            wt(se, "sV", M[("xln", it - 2)])
                        se.copy(a_bc[:, par * 512: par * 512 + nw], ab_ps[:, 0:nw])
                        M[("abev", it)] = inc(
                            se.copy(b_bc[:, par * 512: par * 512 + nw], bb_ps[:, 0:nw]), "sA")
                    SC.add("scalar", se_abev)

                    def ve_xln(ve, it=it, c0=c0, nw=nw, po=po, par=par):
                        wt(ve, "sA", M[("abev", it)])
                        xb3 = xbt[:, po:po + 4096].rearrange("p (c n) -> p c n", c=NCH)[:, :, 0:nw]
                        a3 = a_bc[:, par * 512: par * 512 + nw][:, None, :].broadcast_to([P, NCH, nw])
                        t3 = tmpA[:, :].rearrange("p (c n) -> p c n", c=NCH)[:, :, 0:nw]
                        v1 = inc(ve.tensor_tensor(t3, xb3, a3, ALU.mult), "sV")
                        ve.wait_ge(sems["sV"], v1)
                        o3 = xln[:, :].rearrange("p (c n) -> p c n", c=NCH)[:, :, c0:c0 + nw]
                        b3 = b_bc[:, par * 512: par * 512 + nw][:, None, :].broadcast_to([P, NCH, nw])
                        M[("xln", it)] = inc(ve.tensor_tensor(o3, t3, b3, ALU.add), "sV")
                    SC.add("vector", ve_xln)

                def sy_endA(sy):
                    wt(sy, "dS", cnt["dS"])
                SC.add("sync", sy_endA)

                def gp_endA(gp):
                    wt(gp, "dG", cnt["dG"])
                SC.add("gpsimd", gp_endA)
                SC.flush(blk)

        if stage == "xln":
            with ExitStack() as stX:
                xf = stX.enter_context(nc.sbuf_tensor("xf_dbg", [P, NCH * EXT], F32))
                with nc.Block("phDbg") as blk:
                    def ve_d(ve):
                        inc(ve.tensor_copy(xf[:, :], xln[:, :]), "sV")
                    SC.add("vector", ve_d)
                    def sy_d(sy):
                        wt(sy, "sV", cnt["sV"])
                        d = inc(sy.dma_start(out=ydbg_d[:, 0:NCH * EXT], in_=xf[:, :]), "dS", 16)
                        sy.wait_ge(sems["dS"], d)
                    SC.add("sync", sy_d)
            return nc

        if stage == "ln":
            with nc.Block("phOutL") as blk:
                def sy_l(sy):
                    d = inc(sy.dma_start(out=y_d[:, :], in_=attout_d[:, :]), "dS", 16)
                    sy.wait_ge(sems["dS"], d)
                SC.add("sync", sy_l)
            return nc

        with ExitStack() as stB:
            EB = stB.enter_context
            outwT_bf = EB(nc.sbuf_tensor("outwT_bf", [P, 8192], BF16))
            qkvb_bf = EB(nc.sbuf_tensor("qkvb_bf", [1, 3 * DM], BF16))
            outb_bf = EB(nc.sbuf_tensor("outb_bf", [1, DM], BF16))
            q_t = EB(nc.sbuf_tensor("q_t", [64, 2 * 1792], BF16))
            k_t = EB(nc.sbuf_tensor("k_t", [64, 2 * 2048], BF16))
            v_sb = EB(nc.sbuf_tensor("v_sb", [P, 2 * 1024], BF16))
            sqb = EB(nc.sbuf_tensor("sqb", [P, 1024], BF16))
            rn_ln = EB(nc.sbuf_tensor("rn_ln", [16, P], F32))
            rn_sb = EB(nc.sbuf_tensor("rn_sb", [16, P], BF16))
            bcrn_q = EB(nc.sbuf_tensor("bcrn_q", [64, 1792], BF16))
            bcrn_k = EB(nc.sbuf_tensor("bcrn_k", [64, 2048], BF16))
            e_p = EB(nc.sbuf_tensor("e_p", [112, 1024], BF16))
            p_m = EB(nc.sbuf_tensor("p_m", [112, 1024], BF16))
            p_n = EB(nc.sbuf_tensor("p_n", [112, 1024], BF16))
            sums = EB(nc.sbuf_tensor("sums", [112, 16], F32))
            rinv = EB(nc.sbuf_tensor("rinv", [112, 16], F32))
            pT_sb = EB(nc.sbuf_tensor("pT_sb", [P, 1792], BF16))
            att = EB(nc.sbuf_tensor("att", [P, 896], BF16))
            opj = EB(nc.sbuf_tensor("opj", [P, 1024], F32))
            s_av = EB(nc.psum_tensor("s_av", [P, 1024], F32))
            psB = EB(nc.psum_tensor("psB", [P, 1024], F32))
            pT_ps = EB(nc.psum_tensor("pT_ps", [P, 1024], BF16))
            av_ps = EB(nc.psum_tensor("av_ps", [64, 1024], F32))
            op_ps = EB(nc.psum_tensor("op_ps", [P, 512], F32))

            M = {("unit", -1, 0): 0, ("unit", -1, 1): 0}

            with nc.Block("phB", no_gpsimd_drain=True) as blk:
                SC = Sched()
                actload and SC.add("scalar", lambda se: act_load(se, "natural_log_exp_and_others"))
                def gp_ow(gp):
                    d = inc(gp.dma_start(out=wstage[0:1, 0:3 * DM], in_=qkvb_d[:, :]), "dG", 16)
                    gp.wait_ge(sems["dG"], d)
                    M["qkvb"] = inc(gp.tensor_copy(qkvb_bf[:, :], wstage[0:1, 0:3 * DM]), "sG")
                    d = inc(gp.dma_start(out=wstage[0:1, 0:DM], in_=outb_d[:, :]), "dG", 16)
                    gp.wait_ge(sems["dG"], d)
                    M["outbb"] = inc(gp.tensor_copy(outb_bf[:, :], wstage[0:1, 0:DM]), "sG")
                    for kc in range(NCH):
                        d = inc(gp.dma_start(out=wstage[:, 0:DM],
                                             in_=outwT_d[kc * P:(kc + 1) * P, :]), "dG", 16)
                        gp.wait_ge(sems["dG"], d)
                        inc(gp.tensor_copy(outwT_bf[:, kc * DM:(kc + 1) * DM],
                                           wstage[:, 0:DM]), "sG")
                SC.add("gpsimd", gp_ow)
                M["outw"] = cnt["sG"]

                def emit_build(qb, bsub=9):
                    s_k, nq = QB[qb]
                    qc0 = 8 + s_k
                    sl = qb % 2
                    qo = sl * 1792
                    ko = sl * 2048

                    def pe_q(pe):
                        if qb > 0:
                            wt(pe, "sV", M[("vev", qb - 1)])
                        else:
                            wt(pe, "sG", M["outbb"])
                        for pc in range(NCH):
                            for kc in range(NCH):
                                pe.matmul(psB[:, pc * P: pc * P + nq],
                                          wbuf[:, kc * 3072 + pc * P: kc * 3072 + (pc + 1) * P],
                                          xslc(kc, qc0, nq), start=(kc == 0), stop=False)
                            mm = pe.matmul(psB[:, pc * P: pc * P + nq],
                                           qkvb_bf[0:1, pc * P:(pc + 1) * P],
                                           ones_rb[0:1, 0:nq], start=False, stop=True)
                        M[("qmm", qb)] = inc(mm, "sP")
                    SC.add("tensor", pe_q)

                    def se_qsq(se):
                        wt(se, "sP", M[("qmm", qb)])
                        if qb > 0:
                            wt(se, "sP", M[("kssq", qb - 1)])
                        M[("qsq", qb)] = inc(
                            se.activation(sqb[:, 0:1024], psB[:, 0:1024], AF.Square), "sA")
                    SC.add("scalar", se_qsq)

                    def ve_qev(ve):
                        wt(ve, "sP", M[("qmm", qb)])
                        wt(ve, "sA", M[("qsq", qb)])
                        if qb > 1:
                            wt(ve, "sP", M[("s", qb - 2, 1)])
                        src0 = psB[0:64, 0:1024].rearrange("p (c n) -> p c n", c=NCH)[:, :, 0:nq]
                        src1 = psB[64:128, 0:1024].rearrange("p (c n) -> p c n", c=NCH)[:, :, 0:nq]
                        dst0 = q_t[0:64, qo:qo + 896].rearrange("p (c n) -> p c n", c=NCH)[:, :, 0:nq]
                        dst1 = q_t[0:64, qo + 896:qo + 1792].rearrange("p (c n) -> p c n", c=NCH)[:, :, 0:nq]
                        inc(ve.tensor_copy(dst0, src0), "sV")
                        M[("qev", qb)] = inc(ve.tensor_copy(dst1, src1), "sV")
                    SC.add("vector", ve_qev)
                    if bsub < 2:
                        return

                    def pe_qssq(pe):
                        wt(pe, "sA", M[("qsq", qb)])
                        wt(pe, "sV", M[("qev", qb)])
                        for pc in range(NCH):
                            mm = pe.matmul(psB[0:16, 0:nq], selpk[:, pc * 16:(pc + 1) * 16],
                                           sqb[:, pc * P: pc * P + nq],
                                           start=(pc == 0), stop=(pc == NCH - 1))
                        M[("qssq", qb)] = inc(mm, "sP")
                    SC.add("tensor", pe_qssq)

                    def se_qrn(se):
                        wt(se, "sP", M[("qssq", qb)])
                        if qb > 0:
                            wt(se, "rw", M[("krnw", qb - 1)])
                        a1 = inc(se.activation(rn_ln[0:16, 0:nq], psB[0:16, 0:nq], AF.Ln), "sA")
                        se.wait_ge(sems["sA"], a1)
                        M[("qrn", qb)] = inc(
                            se.activation(rn_sb[0:16, 0:nq], rn_ln[0:16, 0:nq],
                                          AF.Exp, bias=0.0, scale=-0.5), "sA")
                    SC.add("scalar", se_qrn)

                    def sy_qrn(sy):
                        idx = sl * 2
                        wt(sy, "sA", M[("qrn", qb)])
                        if qb > 0:
                            wt(sy, "sV", M[("qn", qb - 1)])
                        if qb > 1:
                            wt(sy, "bq", M[("qrnb", qb - 2)])
                        d1 = inc(sy.dma_start(out=rnscr_d[idx, 0:16, 0:nq],
                                              in_=rn_sb[0:16, 0:nq]), "rw", 16)
                        M[("qrnw", qb)] = d1
                        sy.wait_ge(sems["rw"], d1)
                        srcq = rnscr_d[idx].rearrange("(pc two) k -> two pc k", two=2)
                        for rh in range(2):
                            bqv = bcrn_q[:, rh * 896: rh * 896 + 896].rearrange(
                                "p (c n) -> p c n", c=NCH)[:, :, 0:nq]
                            mq = inc(sy.dma_start(
                                out=bqv, in_=srcq[rh][:, 0:nq][None].broadcast_to([64, NCH, nq])), "bq", 16)
                        M[("qrnb", qb)] = mq
                    SC.add("sync", sy_qrn)

                    def ve_qn(ve):
                        wt(ve, "bq", M[("qrnb", qb)])
                        M[("qn", qb)] = inc(
                            ve.tensor_tensor(q_t[0:64, qo:qo + 1792], q_t[0:64, qo:qo + 1792],
                                             bcrn_q[0:64, :], ALU.mult), "sV")
                    SC.add("vector", ve_qn)
                    if bsub < 3:
                        return

                    def pe_k(pe):
                        wt(pe, "sA", M[("qrn", qb)])
                        wt(pe, "sV", M[("qev", qb)])
                        for pc in range(NCH):
                            for kc in range(NCH):
                                pe.matmul(psB[:, pc * P:(pc + 1) * P],
                                          wbuf[:, kc * 3072 + DM + pc * P: kc * 3072 + DM + (pc + 1) * P],
                                          xslc(kc, s_k, P), start=(kc == 0), stop=False)
                            mm = pe.matmul(psB[:, pc * P:(pc + 1) * P],
                                           qkvb_bf[0:1, DM + pc * P: DM + (pc + 1) * P],
                                           ones_rb[0:1, 0:P], start=False, stop=True)
                        M[("kmm", qb)] = inc(mm, "sP")
                    SC.add("tensor", pe_k)

                    def se_ksq(se):
                        wt(se, "sP", M[("kmm", qb)])
                        wt(se, "sP", M[("qssq", qb)])
                        M[("ksq", qb)] = inc(
                            se.activation(sqb[:, 0:1024], psB[:, 0:1024], AF.Square), "sA")
                    SC.add("scalar", se_ksq)

                    def ve_kev(ve):
                        wt(ve, "sP", M[("kmm", qb)])
                        wt(ve, "sA", M[("ksq", qb)])
                        if qb > 1:
                            wt(ve, "sP", M[("s", qb - 2, 1)])
                        inc(ve.tensor_copy(k_t[0:64, ko:ko + 1024], psB[0:64, 0:1024]), "sV")
                        M[("kev", qb)] = inc(
                            ve.tensor_copy(k_t[0:64, ko + 1024:ko + 2048], psB[64:128, 0:1024]), "sV")
                    SC.add("vector", ve_kev)
                    if bsub < 4:
                        return

                    def pe_kssq(pe):
                        wt(pe, "sA", M[("ksq", qb)])
                        wt(pe, "sV", M[("kev", qb)])
                        for pc in range(NCH):
                            mm = pe.matmul(psB[0:16, 0:P], selpk[:, pc * 16:(pc + 1) * 16],
                                           sqb[:, pc * P:(pc + 1) * P],
                                           start=(pc == 0), stop=(pc == NCH - 1))
                        M[("kssq", qb)] = inc(mm, "sP")
                    SC.add("tensor", pe_kssq)

                    def se_krn(se):
                        wt(se, "sP", M[("kssq", qb)])
                        wt(se, "rw", M[("qrnw", qb)])
                        a1 = inc(se.activation(rn_ln[0:16, 0:P], psB[0:16, 0:P], AF.Ln), "sA")
                        se.wait_ge(sems["sA"], a1)
                        M[("krn", qb)] = inc(
                            se.activation(rn_sb[0:16, 0:P], rn_ln[0:16, 0:P],
                                          AF.Exp, bias=0.0, scale=-0.5), "sA")
                    SC.add("scalar", se_krn)

                    def sy_krn(sy):
                        idx = sl * 2 + 1
                        wt(sy, "sA", M[("krn", qb)])
                        if qb > 0:
                            wt(sy, "sV", M[("kn", qb - 1)])
                        if qb > 1:
                            wt(sy, "bk", M[("krnb", qb - 2)])
                        d1 = inc(sy.dma_start(out=rnscr_d[idx, 0:16, 0:P],
                                              in_=rn_sb[0:16, 0:P]), "rw", 16)
                        M[("krnw", qb)] = d1
                        sy.wait_ge(sems["rw"], d1)
                        srck = rnscr_d[idx].rearrange("(pc two) k -> two pc k", two=2)
                        for rh in range(2):
                            bkv = bcrn_k[:, rh * 1024: rh * 1024 + 1024].rearrange(
                                "p (c n) -> p c n", c=NCH)
                            mk = inc(sy.dma_start(
                                out=bkv, in_=srck[rh][None].broadcast_to([64, NCH, P])), "bk", 16)
                        M[("krnb", qb)] = mk
                    SC.add("sync", sy_krn)

                    def ve_kn(ve):
                        wt(ve, "bk", M[("krnb", qb)])
                        M[("kn", qb)] = inc(
                            ve.tensor_tensor(k_t[0:64, ko:ko + 2048], k_t[0:64, ko:ko + 2048],
                                             bcrn_k[0:64, :], ALU.mult), "sV")
                    SC.add("vector", ve_kn)
                    if bsub < 5:
                        return

                    def pe_v(pe):
                        wt(pe, "sA", M[("krn", qb)])
                        wt(pe, "sV", M[("kev", qb)])
                        if qb > 1:
                            wt(pe, "sP", M[("av", qb - 2, 1)])
                        for vh in range(2):
                            for kc in range(NCH):
                                pe.matmul(psB[:, vh * 512:(vh + 1) * 512],
                                          xslc(kc, s_k, P),
                                          wbuf[:, kc * 3072 + 2048 + vh * 512: kc * 3072 + 2048 + (vh + 1) * 512],
                                          start=(kc == 0), stop=False)
                            mm = pe.matmul(psB[:, vh * 512:(vh + 1) * 512],
                                           ones_rb[0:1, 0:P],
                                           qkvb_bf[0:1, 2048 + vh * 512: 2048 + (vh + 1) * 512],
                                           start=False, stop=True)
                        M[("vmm", qb)] = inc(mm, "sP")
                    SC.add("tensor", pe_v)

                    def ve_vev(ve):
                        vo = sl * 1024
                        wt(ve, "sP", M[("vmm", qb)])
                        M[("vev", qb)] = inc(
                            ve.tensor_copy(v_sb[:, vo:vo + 1024], psB[:, 0:1024]), "sV")
                    SC.add("vector", ve_vev)

                def emit_scores(qb, half):
                    s_k, nq = QB[qb]
                    sl = qb % 2
                    qo = sl * 1792
                    ko = sl * 2048

                    def pe_s(pe):
                        wt(pe, "sV", M[("qn", qb)])
                        wt(pe, "sV", M[("kn", qb)])
                        prev = (qb, 0) if half == 1 else (qb - 1, 1)
                        wt(pe, "sA", M.get(("exp", prev[0], prev[1]), 0))
                        for hh in range(8):
                            h = half * 8 + hh
                            pc, rh = h // 2, h % 2
                            mm = pe.matmul(s_av[0:nq, hh * P:(hh + 1) * P],
                                           q_t[0:64, qo + rh * 896 + pc * 112: qo + rh * 896 + pc * 112 + nq],
                                           k_t[0:64, ko + rh * 1024 + pc * P: ko + rh * 1024 + (pc + 1) * P],
                                           start=True, stop=True)
                        M[("s", qb, half)] = inc(mm, "sP")
                    SC.add("tensor", pe_s)

                def emit_exp(qb, half):
                    s_k, nq = QB[qb]

                    def se_exp(se):
                        wt(se, "sP", M[("s", qb, half)])
                        prev = (qb, 0) if half == 1 else (qb - 1, 1)
                        wt(se, "sV", M.get(("pm", prev[0], prev[1]), 0))
                        M[("exp", qb, half)] = inc(
                            se.activation(e_p[0:nq, 0:1024], s_av[0:nq, 0:1024], AF.Exp), "sA")
                    SC.add("scalar", se_exp)

                def emit_softmax(qb, half, sub=6):
                    s_k, nq = QB[qb]
                    mask = m_first if qb == 0 else (m_last if qb == NQB - 1 else m_mid)
                    if sub < 2:
                        return

                    def ve_soft(ve):
                        wt(ve, "sA", M[("exp", qb, half)])
                        e3 = e_p[0:nq, :].rearrange("p (h j) -> p h j", h=8)
                        mk3 = mask[0:nq, :][:, None, :].broadcast_to([nq, 8, P])
                        pm3 = p_m[0:nq, :].rearrange("p (h j) -> p h j", h=8)
                        v1 = inc(ve.tensor_tensor(pm3, e3, mk3, ALU.mult), "sV")
                        M[("pm", qb, half)] = v1
                        ve.wait_ge(sems["sV"], v1)
                        v2 = inc(ve.tensor_reduce(
                            sums[0:nq, half * 8:(half + 1) * 8][:, :, None],
                            p_m[0:nq, :].rearrange("p (h j) -> p h j", h=8),
                            AX.X, ALU.add), "sV")
                        ve.wait_ge(sems["sV"], v2)
                        v3 = inc(ve.reciprocal(rinv[0:nq, half * 8:(half + 1) * 8],
                                               sums[0:nq, half * 8:(half + 1) * 8]), "sV")
                        ve.wait_ge(sems["sV"], v3)
                        ri3 = rinv[0:nq, half * 8:(half + 1) * 8][:, :, None].broadcast_to([nq, 8, P])
                        pn3 = p_n[0:nq, :].rearrange("p (h j) -> p h j", h=8)
                        M[("pn", qb, half)] = inc(ve.tensor_tensor(pn3, pm3, ri3, ALU.mult), "sV")
                    SC.add("vector", ve_soft)
                    if sub < 3:
                        return

                    def pe_tr(pe):
                        wt(pe, "sV", M[("pn", qb, half)])
                        prevu = (qb, 0) if half == 1 else (qb - 1, 1)
                        wt(pe, "sV", M.get(("pTe", prevu[0], prevu[1]), 0))
                        for hh in range(8):
                            h = half * 8 + hh
                            mm = pe.transpose(pT_ps[0:P, hh * 112: hh * 112 + nq],
                                              p_n[0:nq, hh * P:(hh + 1) * P],
                                              id_bf[0:nq, 0:nq])
                        M[("tr", qb, half)] = inc(mm, "sP")
                    SC.add("tensor", pe_tr)
                    if sub < 4:
                        return

                    def ve_pTe(ve):
                        wt(ve, "sP", M[("tr", qb, half)])
                        wt(ve, "sP", M.get(("av", qb - 1, half), 0))
                        src = pT_ps[:, 0:896].rearrange(
                            "p (h j) -> p h j", h=8)[:, :, 0:nq]
                        dst = pT_sb[:, half * 896: half * 896 + 896].rearrange(
                            "p (h j) -> p h j", h=8)[:, :, 0:nq]
                        M[("pTe", qb, half)] = inc(ve.tensor_copy(dst, src), "sV")
                    SC.add("vector", ve_pTe)
                    if sub < 5:
                        return

                    def pe_av(pe):
                        sl = qb % 2
                        ko = sl * 1024
                        wt(pe, "sV", M[("pTe", qb, half)])
                        prevu = (qb, 0) if half == 1 else (qb - 1, 1)
                        wt(pe, "sV", M[("unit", prevu[0], prevu[1])])
                        for hh in range(8):
                            h = half * 8 + hh
                            mm = pe.matmul(av_ps[0:64, hh * P: hh * P + nq],
                                           v_sb[:, ko + h * 64: ko + (h + 1) * 64],
                                           pT_sb[:, half * 896 + hh * 112: half * 896 + hh * 112 + nq],
                                           start=True, stop=True)
                        M[("av", qb, half)] = inc(mm, "sP")
                    SC.add("tensor", pe_av)
                    if sub < 6:
                        return

                    def ve_att(ve):
                        wt(ve, "sP", M[("av", qb, half)])
                        wt(ve, "sP", M.get(("op", qb - 1), 0))
                        v1 = 0
                        for par_i in range(2):
                            src = av_ps[0:64, 0:1024].rearrange("p (h j) -> p h j", j=128)[:, par_i::2, 0:nq]
                            dst = att[par_i * 64:(par_i + 1) * 64,
                                      half * 448: half * 448 + 448].rearrange(
                                "p (c j) -> p c j", c=4)[:, :, 0:nq]
                            v1 = inc(ve.tensor_copy(dst, src), "sV")
                        M[("unit", qb, half)] = v1
                    SC.add("vector", ve_att)

                def emit_outproj(qb):
                    s_k, nq = QB[qb]

                    for rnd in range(2):
                        def pe_op(pe, rnd=rnd):
                            if rnd == 0:
                                wt(pe, "sV", M[("unit", qb, 1)])
                                wt(pe, "sA", M.get(("ope", qb - 1, 1), 0))
                                wt(pe, "sG", M["outw"])
                            else:
                                wt(pe, "sA", M[("ope", qb, 0)])
                            for oc4 in range(4):
                                oc = rnd * 4 + oc4
                                for pc in range(NCH):
                                    pe.matmul(op_ps[:, oc4 * P: oc4 * P + nq],
                                              outwT_bf[:, pc * DM + oc * P: pc * DM + (oc + 1) * P],
                                              att[:, pc * 112: pc * 112 + nq],
                                              start=(pc == 0), stop=False)
                                mm = pe.matmul(op_ps[:, oc4 * P: oc4 * P + nq],
                                               outb_bf[0:1, oc * P:(oc + 1) * P],
                                               ones_rb[0:1, 0:nq], start=False, stop=True)
                            M[("op", qb, rnd)] = inc(mm, "sP")
                        SC.add("tensor", pe_op)

                        def se_ope(se, rnd=rnd):
                            wt(se, "sP", M[("op", qb, rnd)])
                            if rnd == 0:
                                wt(se, "dS", M.get(("aod", qb - 1), 0))
                            M[("ope", qb, rnd)] = inc(
                                se.copy(opj[:, rnd * 512:(rnd + 1) * 512], op_ps[:, 0:512]), "sA")
                        SC.add("scalar", se_ope)

                    def sy_ao(sy):
                        wt(sy, "sA", M[("ope", qb, 1)])
                        for oc in range(NCH):
                            inc(sy.dma_start(out=attout_d[oc * P:(oc + 1) * P, s_k:s_k + nq],
                                             in_=opj[:, oc * P: oc * P + nq]), "dS", 16)
                        M[("aod", qb)] = cnt["dS"]
                    SC.add("sync", sy_ao)

                if bstop < 1:
                    emit_build(0, bsub=int(round(bstop * 10)))
                else:
                    emit_build(0)
                for qb in range(nqb if bstop >= 1 else 0):
                    if bstop >= 2:
                        emit_scores(qb, 0)
                    if qb + 1 < nqb:
                        emit_build(qb + 1)
                    if bstop >= 3:
                        emit_exp(qb, 0)
                    if bstop >= 9:
                        emit_scores(qb, 1)
                    if bstop >= 3:
                        emit_softmax(qb, 0, sub=bstop - 2)
                    if bstop >= 9:
                        emit_exp(qb, 1)
                        emit_softmax(qb, 1)
                    if bstop >= 10:
                        emit_outproj(qb)

                if stage == "qkv0":
                    qkvdbg = EB(nc.sbuf_tensor("qkvdbg", [P, 4864], F32))

                    def ve_dbg(ve):
                        wt(ve, "sV", M[("qn", 0)])
                        wt(ve, "sV", M[("kn", 0)])
                        wt(ve, "sV", M[("vev", 0)])
                        inc(ve.tensor_copy(qkvdbg[0:64, 0:1792], q_t[0:64, 0:1792]), "sV")
                        inc(ve.tensor_copy(qkvdbg[0:64, 1792:3840], k_t[0:64, 0:2048]), "sV")
                        inc(ve.tensor_copy(qkvdbg[:, 3840:4864], v_sb[:, 0:1024]), "sV")
                    SC.add("vector", ve_dbg)

                    def sy_dbg(sy):
                        wt(sy, "sV", cnt["sV"])
                        d = inc(sy.dma_start(out=ydbg_d[:, 0:4864], in_=qkvdbg[:, :]), "dS", 16)
                        sy.wait_ge(sems["dS"], d)
                    SC.add("sync", sy_dbg)

                def sy_endB(sy):
                    wt(sy, "dS", cnt["dS"])
                SC.add("sync", sy_endB)

                def gp_endB(gp):
                    wt(gp, "dG", cnt["dG"])
                SC.add("gpsimd", gp_endB)
                SC.flush(blk)

        if stage == "attn":
            with nc.Block("phOutA") as blk:
                def sy_o(sy):
                    d = inc(sy.dma_start(out=y_d[:, :], in_=attout_d[:, :]), "dS", 16)
                    sy.wait_ge(sems["dS"], d)
                blk.sync(sy_o)
            return nc

        with ExitStack() as stDE:
            ED = stDE.enter_context
            acc = ED(nc.sbuf_tensor("acc", [P, NCH * TC], F32))

            def acc3(t0, n):
                return acc[:, :].rearrange("p (c n) -> p c n", c=NCH)[:, :, t0:t0 + n]

            def we_slot(e):
                return (e % 2) * 12288

            MW = {}
            _wstfree = [0]

            def emit_wexp(SCref, e, wsem, wval, q="sync"):
                parts = []
                for part in range(3):
                    def srcs(part=part, e=e):
                        out = []
                        if part == 0:
                            for kc in range(NCH):
                                s = egwT_d[e, kc * P:(kc + 1) * P, :] if e < 8 else                                     sgwT_d[kc * P:(kc + 1) * P, (e - 8) * 512:(e - 7) * 512]
                                out.append((wstage[:, kc * 512:(kc + 1) * 512], s))
                        elif part == 1:
                            for kc in range(NCH):
                                s = euwT_d[e, kc * P:(kc + 1) * P, :] if e < 8 else                                     suwT_d[kc * P:(kc + 1) * P, (e - 8) * 512:(e - 7) * 512]
                                out.append((wstage[:, kc * 512:(kc + 1) * 512], s))
                        else:
                            for ic in range(4):
                                s = edwT_d[e, ic * P:(ic + 1) * P, :] if e < 8 else                                     sdwT_d[(e - 8) * 512 + ic * P: (e - 8) * 512 + (ic + 1) * P, :]
                                out.append((wstage[:, ic * DM:(ic + 1) * DM], s))
                        return out

                    def sy_st(sy, part=part, e=e, wsem=wsem, wval=wval, _wf=_wstfree[0]):
                        if part == 0 and wval > 0:
                            wt(sy, wsem, wval)
                        wt(sy, "sG", _wf)
                        for dst, s in srcs(part):
                            inc(sy.dma_start(out=dst, in_=s), "wst", 16)
                        M[("wst", e, part)] = cnt["wst"]
                    SCref.add(q, sy_st)

                    def gp_ct(gp, part=part, e=e):
                        wt(gp, "wst", M[("wst", e, part)])
                        off = we_slot(e) + part * 4096
                        M[("wct", e, part)] = inc(
                            gp.tensor_copy(wbuf[:, off: off + 4096], wstage[:, :]), "sG")
                    SCref.add("gpsimd", gp_ct)
                    _wstfree[0] = M[("wct", e, part)]
                MW[e] = M[("wct", e, 2)]

            with ExitStack() as stD:
                DD = stD.enter_context
                xstD = DD(nc.sbuf_tensor("xstD", [P, 2 * 1024], F32))
                aostD = DD(nc.sbuf_tensor("aostD", [P, 2 * 1024], F32))
                xb2 = DD(nc.sbuf_tensor("xb2", [P, 1024], BF16))
                sq2 = DD(nc.sbuf_tensor("sq2", [P, 1024], BF16))
                tmpD = DD(nc.sbuf_tensor("tmpD", [P, 1024], BF16))
                a_bcD = DD(nc.sbuf_tensor("a_bcD", [P, 2 * P], BF16))
                b_bcD = DD(nc.sbuf_tensor("b_bcD", [P, 2 * P], BF16))
                mrowD = DD(nc.sbuf_tensor("mrowD", [1, P], F32))
                msqD = DD(nc.sbuf_tensor("msqD", [1, P], F32))
                varrD = DD(nc.sbuf_tensor("varrD", [1, P], F32))
                lnrD = DD(nc.sbuf_tensor("lnrD", [1, P], F32))
                arowD = DD(nc.sbuf_tensor("arowD", [1, P], BF16))
                browD = DD(nc.sbuf_tensor("browD", [1, P], BF16))
                sc0 = DD(nc.sbuf_tensor("sc0", [P, 8], F32))
                sce = DD(nc.sbuf_tensor("sce", [P, 8], F32))
                scs = DD(nc.sbuf_tensor("scs", [P, 8], F32))
                gm1 = DD(nc.sbuf_tensor("gm1", [P, 1], F32))
                gm2 = DD(nc.sbuf_tensor("gm2", [P, 1], F32))
                geq = DD(nc.sbuf_tensor("geq", [P, 8], F32))
                gs2 = DD(nc.sbuf_tensor("gs2", [P, 8], F32))
                sel01 = DD(nc.sbuf_tensor("sel01", [P, 8], F32))
                gden = DD(nc.sbuf_tensor("gden", [P, 1], F32))
                gwin = DD(nc.sbuf_tensor("gwin", [P, 1], F32))
                wful = DD(nc.sbuf_tensor("wful", [P, 8], F32))
                selT = DD(nc.sbuf_tensor("selT", [8, TC], F32))
                cwsb = DD(nc.sbuf_tensor("cwsb", [8, 2], F32))
                cntg = DD(nc.sbuf_tensor("cntg", [8, 1], F32))
                wsumg = DD(nc.sbuf_tensor("wsumg", [8, 1], F32))
                wmaxg = DD(nc.sbuf_tensor("wmaxg", [8, 1], F32))
                wmean = DD(nc.sbuf_tensor("wmean", [8, 1], F32))
                coefT = DD(nc.sbuf_tensor("coefT", [8, TC], BF16))
                sumD = DD(nc.psum_tensor("sumD", [1, P], F32))
                sqD = DD(nc.psum_tensor("sqD", [1, P], F32))
                abD_ps = DD(nc.psum_tensor("abD_ps", [P, P], F32))
                bbD_ps = DD(nc.psum_tensor("bbD_ps", [P, P], F32))
                glog = DD(nc.psum_tensor("glog", [P, 8], F32))
                selT_ps = DD(nc.psum_tensor("selT_ps", [8, P], F32))
                cnt_ps = DD(nc.psum_tensor("cnt_ps", [8, 1], F32))
                wsum_ps = DD(nc.psum_tensor("wsum_ps", [8, 1], F32))

                M = {}

                with nc.Block("phD", no_gpsimd_drain=True) as blk:
                    SC = Sched()
                    actload and SC.add("scalar", lambda se: act_load(se, "natural_log_exp_and_others"))
                    emit_wexp(SC, 0, "dG", 0, q="gpsimd")
                    emit_wexp(SC, 1, "dG", 0, q="gpsimd")

                    for it in range(NTD):
                        t0 = it * P
                        par = it % 2
                        po = par * 1024
                        ao = par * P

                        def sy_xd(sy, it=it, t0=t0, po=po):
                            if it >= 2:
                                wt(sy, "sV", M[("xb2c", it - 2)])
                            sx = "xd%d" % (it % 2)
                            for kc in range(NCH):
                                inc(sy.dma_start(out=xstD[:, po + kc * P: po + (kc + 1) * P],
                                                 in_=xin[kc * P:(kc + 1) * P, 8 + t0: 8 + t0 + P]), sx, 16)
                            for kc in range(NCH):
                                inc(sy.dma_start(out=aostD[:, po + kc * P: po + (kc + 1) * P],
                                                 in_=attout_d[kc * P:(kc + 1) * P, t0: t0 + P]), sx, 16)
                            M[("xd", it)] = cnt[sx]
                        SC.add("sync", sy_xd)

                        def ve_x2(ve, it=it, t0=t0, po=po):
                            wt(ve, "xd%d" % (it % 2), M[("xd", it)])
                            x3 = xstD[:, po:po + 1024].rearrange("p (c n) -> p c n", c=NCH)
                            a3 = aostD[:, po:po + 1024].rearrange("p (c n) -> p c n", c=NCH)
                            v1 = inc(ve.tensor_tensor(acc3(t0, P), x3, a3, ALU.add), "sV")
                            ve.wait_ge(sems["sV"], v1)
                            if it > 0:
                                wt(ve, "sP", M[("statD", it - 1)])
                                wt(ve, "sP", M[("gl", it - 1)])
                            v2 = inc(ve.tensor_copy(
                                xb2[:, :].rearrange("p (c n) -> p c n", c=NCH), acc3(t0, P)), "sV")
                            M[("xb2c", it)] = v2
                        SC.add("vector", ve_x2)

                        def se_sq2(se, it=it):
                            wt(se, "sV", M[("xb2c", it)])
                            if it > 0:
                                wt(se, "sP", M[("statD", it - 1)])
                            M[("sq2", it)] = inc(se.activation(sq2[:, :], xb2[:, :], AF.Square), "sA")
                        SC.add("scalar", se_sq2)

                        def pe_statD(pe, it=it):
                            wt(pe, "sA", M[("sq2", it)])
                            if it > 0:
                                wt(pe, "sA", M[("rowmD", it - 1)])
                                wt(pe, "sV", M[("rowvD", it - 1)])
                            for kc in range(NCH):
                                pe.matmul(sumD[0:1, :], ones_cb[:, :],
                                          xb2[:, kc * P:(kc + 1) * P],
                                          start=(kc == 0), stop=(kc == NCH - 1))
                            for kc in range(NCH):
                                mm = pe.matmul(sqD[0:1, :], ones_cb[:, :],
                                               sq2[:, kc * P:(kc + 1) * P],
                                               start=(kc == 0), stop=(kc == NCH - 1))
                            M[("statD", it)] = inc(mm, "sP")
                        SC.add("tensor", pe_statD)

                        def se_rowD(se, it=it):
                            wt(se, "sP", M[("statD", it)])
                            if it > 0:
                                wt(se, "sP", M[("abmmD", it - 1)])
                                wt(se, "sV", M[("rowbD", it - 1)])
                            a1 = inc(se.activation(mrowD[0:1, :], sumD[0:1, :],
                                                   AF.Copy, bias=0.0, scale=1.0 / DM), "sA")
                            se.wait_ge(sems["sA"], a1)
                            M[("rowmD", it)] = inc(se.activation(msqD[0:1, :], mrowD[0:1, :],
                                                                 AF.Square), "sA")
                        SC.add("scalar", se_rowD)

                        def ve_rowD(ve, it=it):
                            wt(ve, "sA", M[("rowmD", it)])
                            M[("rowvD", it)] = inc(
                                ve.scalar_tensor_tensor(varrD[0:1, :], sqD[0:1, :], 1.0 / DM,
                                                        msqD[0:1, :], ALU.mult, ALU.subtract), "sV")
                        SC.add("vector", ve_rowD)

                        def se_rowD2(se, it=it):
                            wt(se, "sV", M[("rowvD", it)])
                            a1 = inc(se.activation(lnrD[0:1, :], varrD[0:1, :],
                                                   AF.Ln, bias=eps_c[0:1, 0:1], scale=1.0), "sA")
                            se.wait_ge(sems["sA"], a1)
                            M[("rowaD", it)] = inc(se.activation(arowD[0:1, :], lnrD[0:1, :],
                                                                 AF.Exp, bias=0.0, scale=-0.5), "sA")
                        SC.add("scalar", se_rowD2)

                        def ve_rowD2(ve, it=it):
                            wt(ve, "sA", M[("rowaD", it)])
                            M[("rowbD", it)] = inc(
                                ve.scalar_tensor_tensor(browD[0:1, :], mrowD[0:1, :], -1.0,
                                                        arowD[0:1, :], ALU.mult, ALU.mult), "sV")
                        SC.add("vector", ve_rowD2)

                        def pe_abD(pe, it=it, par=par):
                            wt(pe, "sA", M[("rowaD", it)])
                            wt(pe, "sV", M[("rowbD", it)])
                            if it > 0:
                                wt(pe, "sA", M[("abevD", it - 1)])
                            pe.matmul(abD_ps[:, :], ones_rb[0:1, 0:P], arowD[0:1, :],
                                      start=True, stop=True)
                            mm = pe.matmul(bbD_ps[:, :], ones_rb[0:1, 0:P], browD[0:1, :],
                                           start=True, stop=True)
                            M[("abmmD", it)] = inc(mm, "sP")
                        SC.add("tensor", pe_abD)

                        def se_abevD(se, it=it, ao=ao):
                            wt(se, "sP", M[("abmmD", it)])
                            if it >= 2:
                                wt(se, "sV", M[("xln2", it - 2)])
                            se.copy(a_bcD[:, ao:ao + P], abD_ps[:, :])
                            M[("abevD", it)] = inc(se.copy(b_bcD[:, ao:ao + P], bbD_ps[:, :]), "sA")
                        SC.add("scalar", se_abevD)

                        def ve_xln2(ve, it=it, t0=t0, ao=ao):
                            wt(ve, "sA", M[("abevD", it)])
                            xb3 = xb2[:, :].rearrange("p (c n) -> p c n", c=NCH)
                            a3 = a_bcD[:, ao:ao + P][:, None, :].broadcast_to([P, NCH, P])
                            t3 = tmpD[:, :].rearrange("p (c n) -> p c n", c=NCH)
                            v1 = inc(ve.tensor_tensor(t3, xb3, a3, ALU.mult), "sV")
                            ve.wait_ge(sems["sV"], v1)
                            o3 = xln[:, :].rearrange("p (c n) -> p c n", c=NCH)[:, :, t0:t0 + P]
                            b3 = b_bcD[:, ao:ao + P][:, None, :].broadcast_to([P, NCH, P])
                            M[("xln2", it)] = inc(ve.tensor_tensor(o3, t3, b3, ALU.add), "sV")
                        SC.add("vector", ve_xln2)

                        def pe_gl(pe, it=it, t0=t0):
                            wt(pe, "sV", M[("xln2", it)])
                            if it > 0:
                                wt(pe, "sV", M[("gl_read", it - 1)])
                            for kc in range(NCH):
                                mm = pe.matmul(glog[:, :], xslc(kc, t0, P),
                                               gw_sb[:, kc * 8:(kc + 1) * 8],
                                               start=(kc == 0), stop=(kc == NCH - 1))
                            M[("gl", it)] = inc(mm, "sP")
                        SC.add("tensor", pe_gl)

                        def ve_glread(ve, it=it):
                            wt(ve, "sP", M[("gl", it)])
                            M[("gl_read", it)] = inc(
                                ve.tensor_tensor(sc0[:, :], glog[:, :], lb_bc[:, :], ALU.add), "sV")
                        SC.add("vector", ve_glread)

                        def se_sig(se, it=it):
                            wt(se, "sV", M[("gl_read", it)])
                            M[("sce", it)] = inc(se.activation(sce[:, :], sc0[:, :],
                                                               AF.Exp, bias=0.0, scale=-1.0), "sA")
                        SC.add("scalar", se_sig)

                        def ve_top2(ve, it=it):
                            wt(ve, "sA", M[("sce", it)])
                            v = inc(ve.tensor_scalar_add(sce[:, :], sce[:, :], 1.0), "sV")
                            ve.wait_ge(sems["sV"], v)
                            v = inc(ve.reciprocal(scs[:, :], sce[:, :]), "sV")
                            ve.wait_ge(sems["sV"], v)
                            v = inc(ve.reduce_max(gm1[:, :], scs[:, :], AX.X), "sV")
                            ve.wait_ge(sems["sV"], v)
                            v = inc(ve.tensor_single_scalar(geq[:, :], scs[:, :], gm1[:, :],
                                                            ALU.is_equal), "sV")
                            ve.wait_ge(sems["sV"], v)
                            v = inc(ve.scalar_tensor_tensor(gs2[:, :], geq[:, :], 1e9, scs[:, :],
                                                            ALU.mult, ALU.subtract), "sV")
                            ve.wait_ge(sems["sV"], v)
                            v = inc(ve.tensor_reduce(gm2[:, :], gs2[:, :], AX.X, ALU.min,
                                                     negate=True), "sV")
                            ve.wait_ge(sems["sV"], v)
                            v = inc(ve.tensor_single_scalar(sel01[:, :], scs[:, :], gm2[:, :],
                                                            ALU.is_ge), "sV")
                            ve.wait_ge(sems["sV"], v)
                            v = inc(ve.tensor_scalar(gden[:, :], gm1[:, :], gm2[:, :], 1e-8,
                                                     ALU.add, ALU.add), "sV")
                            ve.wait_ge(sems["sV"], v)
                            v = inc(ve.reciprocal(gwin[:, :], gden[:, :]), "sV")
                            ve.wait_ge(sems["sV"], v)
                            v = inc(ve.scalar_tensor_tensor(wful[:, :], scs[:, :], gwin[:, :],
                                                            sel01[:, :], ALU.mult, ALU.mult), "sV")
                            M[("top2", it)] = v
                        SC.add("vector", ve_top2)

                        def pe_gstat(pe, it=it):
                            wt(pe, "sV", M[("top2", it)])
                            pe.matmul(cnt_ps[0:8, 0:1], sel01[:, :], ones_cf[:, :],
                                      start=(it == 0), stop=(it == NTD - 1))
                            pe.matmul(wsum_ps[0:8, 0:1], wful[:, :], ones_cf[:, :],
                                      start=(it == 0), stop=(it == NTD - 1))
                            if it > 0:
                                wt(pe, "sA", M[("selTe", it - 1)])
                            mm = pe.transpose(selT_ps[0:8, 0:P], sel01[0:P, 0:8], id_f[0:P, 0:P])
                            M[("gstat", it)] = inc(mm, "sP")
                        SC.add("tensor", pe_gstat)

                        def se_selT(se, it=it):
                            wt(se, "sP", M[("gstat", it)])
                            M[("selTe", it)] = inc(
                                se.copy(selT[:, it * P:(it + 1) * P], selT_ps[0:8, :]), "sA")
                        SC.add("scalar", se_selT)

                    def se_cw(se):
                        wt(se, "sP", M[("gstat", NTD - 1)])
                        inc(se.copy(cwsb[:, 0:1], cnt_ps[0:8, :]), "sA")
                        M["cw"] = inc(se.copy(cwsb[:, 1:2], wsum_ps[0:8, :]), "sA")
                    SC.add("scalar", se_cw)

                    def gp_cc(gp):
                        wt(gp, "sA", M["cw"])
                        inc(gp.dma_start(out=ccin_d[0:8, :], in_=cwsb[:, 0:1]), "dG", 16)
                        d = inc(gp.dma_start(out=ccin_d[8:16, :], in_=cwsb[:, 1:2]), "dG", 16)
                        gp.wait_ge(sems["dG"], d)
                        c = inc(gp.collective_compute(
                            "AllReduce", ALU.add, replica_groups=[list(range(8))],
                            ins=[ccin_d[:, :].opt()], outs=[ccout_d[:, :].opt()]), "cc")
                        gp.wait_ge(sems["cc"], c)
                        inc(gp.dma_start(out=cntg[:, :], in_=ccout_d[0:8, :]), "dG", 16)
                        M["ccd"] = inc(gp.dma_start(out=wsumg[:, :], in_=ccout_d[8:16, :]), "dG", 16)
                    SC.add("gpsimd", gp_cc)

                    def ve_wm(ve):
                        wt(ve, "dG", M["ccd"])
                        v = inc(ve.tensor_scalar_max(wmaxg[:, :], cntg[:, :], 1.0), "sV")
                        ve.wait_ge(sems["sV"], v)
                        v = inc(ve.reciprocal(wmaxg[:, :], wmaxg[:, :]), "sV")
                        ve.wait_ge(sems["sV"], v)
                        v = inc(ve.tensor_tensor(wmean[:, :], wsumg[:, :], wmaxg[:, :], ALU.mult), "sV")
                        ve.wait_ge(sems["sV"], v)
                        wt(ve, "sA", M[("selTe", NTD - 1)])
                        M["coef"] = inc(ve.tensor_scalar_mul(coefT[:, :], selT[:, :],
                                                             wmean[:, :]), "sV")
                    SC.add("vector", ve_wm)

                    def gp_coef(gp):
                        wt(gp, "sV", M["coef"])
                        MW["coefd"] = inc(gp.dma_start(out=coefscr_d[:, :], in_=coefT[:, :]), "dG", 16)
                    SC.add("gpsimd", gp_coef)

                    def sy_endD(sy):
                        wt(sy, "dS", cnt["dS"])
                    SC.add("sync", sy_endD)

                    def gp_endD(gp):
                        wt(gp, "dG", cnt["dG"])
                    SC.add("gpsimd", gp_endD)
                    SC.flush(blk)

            if stage == "xln2":
                with nc.Block("phDbg2") as blk:
                    def sy_d3(sy):
                        d = inc(sy.dma_start(out=ydbgb_d[:, :], in_=xln[:, :]), "dS", 16)
                        sy.wait_ge(sems["dS"], d)
                    blk.sync(sy_d3)
                return nc

            if stage == "d":
                with nc.Block("phOutD") as blk:
                    def sy_d2(sy):
                        for hc in range(NCH):
                            inc(sy.dma_start(out=y_d[hc * P:(hc + 1) * P, :],
                                             in_=acc[:, hc * TC:(hc + 1) * TC]), "dS", 16)
                        wt(sy, "dS", cnt["dS"])
                    blk.sync(sy_d2)
                return nc

            with ExitStack() as stE:
                EE = stE.enter_context
                NTE2 = 4
                h_p = EE(nc.sbuf_tensor("h_p", [P, 2 * 4 * 512], BF16))
                sil = EE(nc.sbuf_tensor("sil", [P, 2 * 512], BF16))
                sp_ = EE(nc.sbuf_tensor("sp_", [P, 2 * 512], BF16))
                bc = EE(nc.sbuf_tensor("bc", [P, 2 * 512], BF16))
                g_ps = EE(nc.psum_tensor("g_ps", [P, 1024], F32))
                u_ps = EE(nc.psum_tensor("u_ps", [P, 1024], F32))
                down_ps = EE(nc.psum_tensor("down_ps", [P, 2048], F32))

                M = {}
                last_g = [0, 0]
                last_u = [0, 0]
                last_sil = [0, 0]
                last_sp = [0, 0]
                last_bc = [0, 0]
                last_dn_ic = [0, 0, 0, 0]
                last_acc = 0

                with nc.Block("phE", no_gpsimd_drain=True) as blk:
                    SC = Sched()
                    actload and SC.add("scalar", lambda se: act_load(se, "silu_and_others"))

                    NG = NEXP * NTE2

                    def emit_bc(g):
                        e, it = g // NTE2, g % NTE2
                        t0 = it * 512
                        par = it % 2
                        if e >= 8:
                            return

                        def se_bc(se, e=e, it=it, t0=t0, par=par, _lbc=last_bc[par]):
                            wt(se, "dG", MW["coefd"])
                            wt(se, "sV", _lbc)
                            sb = "bc%d" % par
                            inc(se.dma_start(
                                out=bc[:, par * 512:(par + 1) * 512],
                                in_=coefscr_d[e, t0:t0 + 512][None, :].broadcast_to([P, 512])), sb, 16)
                            M[("bcd", e, it)] = cnt[sb]
                        SC.add("scalar", se_bc)

                    def emit_gu(g, ics=(0, 1, 2, 3)):
                        e, it = g // NTE2, g % NTE2
                        t0 = it * 512
                        par = it % 2
                        so = we_slot(e)
                        for ic in ics:
                            sl = ic % 2

                            def pe_gu(pe, e=e, it=it, t0=t0, ic=ic, so=so, sl=sl,
                                      _lg=last_g[sl], _lu=last_u[sl]):
                                if it == 0 and ic == 0:
                                    wt(pe, "sG", MW[e])
                                wt(pe, "sA", _lg)
                                wt(pe, "sV", _lu)
                                for kc in range(NCH):
                                    pe.matmul(g_ps[:, sl * 512:(sl + 1) * 512],
                                              wbuf[:, so + kc * 512 + ic * P: so + kc * 512 + (ic + 1) * P],
                                              xslc(kc, t0, 512), start=(kc == 0), stop=(kc == NCH - 1))
                                for kc in range(NCH):
                                    mm = pe.matmul(u_ps[:, sl * 512:(sl + 1) * 512],
                                                   wbuf[:, so + 4096 + kc * 512 + ic * P: so + 4096 + kc * 512 + (ic + 1) * P],
                                                   xslc(kc, t0, 512), start=(kc == 0), stop=(kc == NCH - 1))
                                M[("gu", e, it, ic)] = inc(mm, "sP")
                            SC.add("tensor", pe_gu)

                            def se_sil(se, e=e, it=it, ic=ic, sl=sl, _ls=last_sil[sl]):
                                wt(se, "sP", M[("gu", e, it, ic)])
                                wt(se, "sV", _ls)
                                M[("sil", e, it, ic)] = inc(
                                    se.activation(sil[:, sl * 512:(sl + 1) * 512],
                                                  g_ps[:, sl * 512:(sl + 1) * 512], AF.Silu), "sA")
                            SC.add("scalar", se_sil)
                            last_g[sl] = M[("sil", e, it, ic)]

                            def ve_h(ve, e=e, it=it, ic=ic, par=par, sl=sl,
                                     _lsp=last_sp[sl], _ldn=last_dn_ic[ic]):
                                wt(ve, "sA", M[("sil", e, it, ic)])
                                if e < 8:
                                    wt(ve, "bc%d" % par, M[("bcd", e, it)])
                                    wt(ve, "sV", _lsp)
                                    v = inc(ve.tensor_tensor(sp_[:, sl * 512:(sl + 1) * 512],
                                                             sil[:, sl * 512:(sl + 1) * 512],
                                                             bc[:, par * 512:(par + 1) * 512],
                                                             ALU.mult), "sV")
                                    M[("sp", e, it, ic)] = v
                                    ve.wait_ge(sems["sV"], v)
                                    srcm = sp_[:, sl * 512:(sl + 1) * 512]
                                else:
                                    srcm = sil[:, sl * 512:(sl + 1) * 512]
                                wt(ve, "sP", _ldn)
                                ho = (it % 2) * 2048
                                v = inc(ve.tensor_tensor(h_p[:, ho + ic * 512: ho + (ic + 1) * 512],
                                                         srcm, u_ps[:, sl * 512:(sl + 1) * 512],
                                                         ALU.mult), "sV")
                                M[("hp", e, it, ic)] = v
                            SC.add("vector", ve_h)
                            last_sil[sl] = M[("hp", e, it, ic)]
                            last_u[sl] = M[("hp", e, it, ic)]
                            if e < 8:
                                last_sp[sl] = M[("hp", e, it, ic)]
                                last_bc[par] = M[("sp", e, it, ic)]

                    def emit_dn(g, h2):
                        e, it = g // NTE2, g % NTE2
                        t0 = it * 512
                        so = we_slot(e)

                        def pe_dn(pe, e=e, it=it, so=so, h2=h2, _la=last_acc):
                            if h2 == 0:
                                wt(pe, "sV", _la)
                            else:
                                wt(pe, "sV", M[("acc", e, it, 0)])
                            ho = (it % 2) * 2048
                            for ic in range(4):
                                if h2 == 0:
                                    wt(pe, "sV", M[("hp", e, it, ic)])
                                for j in range(4):
                                    hc = h2 * 4 + j
                                    mm = pe.matmul(
                                        down_ps[:, j * 512:(j + 1) * 512],
                                        wbuf[:, so + 8192 + ic * DM + hc * P: so + 8192 + ic * DM + (hc + 1) * P],
                                        h_p[:, ho + ic * 512: ho + (ic + 1) * 512],
                                        start=(ic == 0), stop=(ic == 3))
                            M[("dn", e, it, h2)] = inc(mm, "sP")
                        SC.add("tensor", pe_dn)
                        if h2 == 1:
                            for ic in range(4):
                                last_dn_ic[ic] = M[("dn", e, it, 1)]

                        def ve_acc(ve, e=e, it=it, t0=t0, h2=h2):
                            wt(ve, "sP", M[("dn", e, it, h2)])
                            a3 = acc[:, :].rearrange("p (c n) -> p c n", c=NCH)[
                                :, h2 * 4:(h2 + 1) * 4, t0:t0 + 512]
                            d3 = down_ps[:, :].rearrange("p (c n) -> p c n", c=4)
                            M[("acc", e, it, h2)] = inc(
                                ve.tensor_tensor(a3, a3, d3, ALU.add), "sV")
                        SC.add("vector", ve_acc)

                    def emit_wn(e):
                        emit_wexp(SC, e + 2, "sP", M[("dn", e, NTE2 - 1, 1)])

                    emit_bc(0)
                    emit_gu(0)
                    for g in range(NG):
                        if g + 1 < NG:
                            emit_bc(g + 1)
                            emit_gu(g + 1, (0, 1))
                        emit_dn(g, 0)
                        if g + 1 < NG:
                            emit_gu(g + 1, (2, 3))
                        emit_dn(g, 1)
                        last_acc = M[("acc", g // NTE2, g % NTE2, 1)]
                        if g % NTE2 == NTE2 - 1 and g // NTE2 + 2 < NEXP:
                            emit_wn(g // NTE2)

                    def sy_out(sy):
                        for it in range(NTE2):
                            wt(sy, "sV", M[("acc", NEXP - 1, it, 1)])
                        for hc in range(NCH):
                            inc(sy.dma_start(out=y_d[hc * P:(hc + 1) * P, :],
                                             in_=acc[:, hc * TC:(hc + 1) * TC]), "dS", 16)
                        wt(sy, "dS", cnt["dS"])
                    SC.add("sync", sy_out)

                    def gp_endE(gp):
                        wt(gp, "dG", cnt["dG"])
                    SC.add("gpsimd", gp_endE)
                    SC.flush(blk)

    return nc



def _prep_inputs(L0_attn_x, L0_attn_qkv_w, L0_attn_qkv_b, L0_attn_out_w, L0_attn_out_b,
                 L0_ffn_gate_w, L0_ffn_lb_bias, L0_ffn_expert_gate_w, L0_ffn_expert_up_w,
                 L0_ffn_expert_down_w, L0_ffn_shared_gate_w, L0_ffn_shared_up_w,
                 L0_ffn_shared_down_w):
    x = np.asarray(L0_attn_x, dtype=np.float32)
    Bb, N, D = x.shape
    xd = np.ascontiguousarray(x.reshape(Bb, N // 2, 2, D).transpose(0, 2, 1, 3))

    shared = {
        "qkvT": np.ascontiguousarray(np.asarray(L0_attn_qkv_w, np.float32).T),
        "outwT": np.ascontiguousarray(np.asarray(L0_attn_out_w, np.float32).T),
        "qkvb": np.ascontiguousarray(np.asarray(L0_attn_qkv_b, np.float32).reshape(1, 3 * DM)),
        "outb": np.ascontiguousarray(np.asarray(L0_attn_out_b, np.float32).reshape(1, DM)),
        "gwT": np.ascontiguousarray(np.asarray(L0_ffn_gate_w, np.float32).T),
        "lbbc": np.ascontiguousarray(np.broadcast_to(
            np.asarray(L0_ffn_lb_bias, np.float32).reshape(1, 8), (P, 8))),
        "egwT": np.ascontiguousarray(np.asarray(L0_ffn_expert_gate_w, np.float32).transpose(0, 2, 1)),
        "euwT": np.ascontiguousarray(np.asarray(L0_ffn_expert_up_w, np.float32).transpose(0, 2, 1)),
        "edwT": np.ascontiguousarray(np.asarray(L0_ffn_expert_down_w, np.float32).transpose(0, 2, 1)),
        "sgwT": np.ascontiguousarray(np.asarray(L0_ffn_shared_gate_w, np.float32).T),
        "suwT": np.ascontiguousarray(np.asarray(L0_ffn_shared_up_w, np.float32).T),
        "sdwT": np.ascontiguousarray(np.asarray(L0_ffn_shared_down_w, np.float32).T),
        "idbf": np.eye(P, dtype=BFD),
        "idf": np.eye(P, dtype=np.float32),
    }
    selpk = np.zeros((P, P), dtype=BFD)
    for pc in range(8):
        for d in range(P):
            h = 2 * pc + (1 if d >= 64 else 0)
            selpk[d, pc * 16 + h] = 1.0
    shared["selpk"] = selpk

    def band(nq):
        r = np.arange(nq)[:, None]
        j = np.arange(P)[None, :]
        return (j >= r) & (j <= r + 16)

    in_maps = []
    halfN = N // 2
    for c in range(8):
        b = c >> 2
        par = (c >> 1) & 1
        half = c & 1
        start = half * TC
        lo, hi = start - 8, start + 2168
        xs = np.zeros((EXT, DM), dtype=np.float32)
        s0, s1 = max(lo, 0), min(hi, halfN)
        xs[s0 - lo: s1 - lo, :] = xd[b, par, s0:s1, :]

        mfirst = band(112)
        if half == 0:
            mfirst = mfirst & (np.arange(P)[None, :] >= 8)
        mlast = band(32)
        if half == 1:
            mlast = mlast & (np.arange(P)[None, :] <= 39)
        m = dict(shared)
        m["xin"] = np.ascontiguousarray(xs.T)
        m["mfirst"] = mfirst.astype(BFD)
        m["mmid"] = band(112).astype(BFD)
        m["mlast"] = mlast.astype(BFD)
        in_maps.append(m)
    return in_maps, (Bb, N, D)


def kernel(**inputs):
    in_maps, (Bb, N, D) = _prep_inputs(**inputs)
    nc = build_nc("full")
    res = run_bass_kernel_spmd(nc, in_maps, core_ids=list(range(8)))
    out = np.zeros((Bb, 2, N // 2, D), dtype=np.float32)
    for c in range(8):
        b = c >> 2
        par = (c >> 1) & 1
        half = c & 1
        y = np.asarray(res.results[c]["y"])
        out[b, par, half * TC:(half + 1) * TC, :] = y.T
    return np.ascontiguousarray(out.transpose(0, 2, 1, 3).reshape(Bb, N, D))


# version 11 (final)
# speedup vs baseline: 1.3024x; 1.0829x over previous
import numpy as np
import ml_dtypes
from contextlib import ExitStack

import concourse.bass as bass
import concourse.mybir as mybir
from concourse.bass_utils import run_bass_kernel_spmd

F32 = mybir.dt.float32
BF16 = mybir.dt.bfloat16
AF = mybir.ActivationFunctionType
ALU = mybir.AluOpType
AX = mybir.AxisListType
BFD = ml_dtypes.bfloat16

P = 128
DM = 1024
NCH = 8
EXT = 2176
TC = 2048
QB = [(i * 112, 112) for i in range(18)] + [(2016, 32)]
NQB = len(QB)
NTD = 16
NTE = 8
NEXP = 10
LN_EPS = 1e-5


def build_nc(stage="full", nqb=NQB, bstop=99, actload=True, bcmemset=False):
    nc = bass.Bass("TRN2", target_bir_lowering=False)

    xin = nc.dram_tensor("xin", [DM, EXT], F32, kind="ExternalInput")
    qkvT_d = nc.dram_tensor("qkvT", [DM, 3 * DM], F32, kind="ExternalInput")
    outwT_d = nc.dram_tensor("outwT", [DM, DM], F32, kind="ExternalInput")
    qkvb_d = nc.dram_tensor("qkvb", [1, 3 * DM], F32, kind="ExternalInput")
    outb_d = nc.dram_tensor("outb", [1, DM], F32, kind="ExternalInput")
    gwT_d = nc.dram_tensor("gwT", [DM, 8], F32, kind="ExternalInput")
    lbbc_d = nc.dram_tensor("lbbc", [P, 8], F32, kind="ExternalInput")
    egwT_d = nc.dram_tensor("egwT", [8, DM, 512], F32, kind="ExternalInput")
    euwT_d = nc.dram_tensor("euwT", [8, DM, 512], F32, kind="ExternalInput")
    edwT_d = nc.dram_tensor("edwT", [8, 512, DM], F32, kind="ExternalInput")
    sgwT_d = nc.dram_tensor("sgwT", [DM, DM], F32, kind="ExternalInput")
    suwT_d = nc.dram_tensor("suwT", [DM, DM], F32, kind="ExternalInput")
    sdwT_d = nc.dram_tensor("sdwT", [DM, DM], F32, kind="ExternalInput")
    idbf_d = nc.dram_tensor("idbf", [P, P], BF16, kind="ExternalInput")
    idf_d = nc.dram_tensor("idf", [P, P], F32, kind="ExternalInput")
    selpk_d = nc.dram_tensor("selpk", [P, P], BF16, kind="ExternalInput")
    mfirst_d = nc.dram_tensor("mfirst", [112, P], BF16, kind="ExternalInput")
    mmid_d = nc.dram_tensor("mmid", [112, P], BF16, kind="ExternalInput")
    mlast_d = nc.dram_tensor("mlast", [32, P], BF16, kind="ExternalInput")

    y_d = nc.dram_tensor("y", [DM, TC], F32, kind="ExternalOutput")
    ydbg_d = nc.dram_tensor("ydbg", [P, 18048], F32, kind="ExternalOutput")
    ydbgb_d = nc.dram_tensor("ydbgb", [P, NCH * EXT], BF16, kind="ExternalOutput")

    attout_d = nc.dram_tensor("attout", [DM, TC], F32, kind="Internal")
    rnscr_d = nc.dram_tensor("rnscr", [4, 16, P], BF16, kind="Internal")
    abscr_d = nc.dram_tensor("abscr", [2, 2, 512], BF16, kind="Internal")
    coefscr_d = nc.dram_tensor("coefscr", [8, TC], BF16, kind="Internal")
    ccin_d = nc.dram_tensor("ccin", [16, 1], F32, kind="Internal")
    ccout_d = nc.dram_tensor("ccout", [16, 1], F32, kind="Internal")

    st0 = ExitStack()
    with st0:
        E0 = st0.enter_context

        sems = {}
        cnt = {}
        for s in ("dS", "dG", "sP", "sV", "sA", "sG", "cc",
                  "xA0", "xA1", "bq", "bk", "xd0", "xd1", "bc0", "bc1", "rw", "wst"):
            sems[s] = E0(nc.semaphore(s))
            cnt[s] = 0

        def inc(instr, s, n=1):
            if instr is not None:
                instr.then_inc(sems[s], n)
            cnt[s] += n
            return cnt[s]

        def wt(eng, s, v):
            if v > 0:
                eng.wait_ge(sems[s], v)

        class _DummyInstr:
            def then_inc(self, *a, **k):
                return self

        class _Dummy:
            def __getattr__(self, name):
                def f(*a, **k):
                    return None
                return f

        _DUM = _Dummy()

        class Sched:

            ORDER = ("sync", "gpsimd", "scalar", "tensor", "vector")

            def __init__(self):
                self.fns = {n: [] for n in self.ORDER}
                self.snap = dict(cnt)

            def add(self, eng, fn):
                self.fns[eng].append(fn)
                fn(_DUM)

            def flush(self, blk):
                end = dict(cnt)
                for k, v in self.snap.items():
                    cnt[k] = v
                for eng in self.ORDER:
                    fns = self.fns[eng]
                    if fns:
                        getattr(blk, eng)(lambda e, fns=fns: [f(e) for f in fns])
                for k, v in end.items():
                    assert cnt[k] == v, (k, cnt[k], v)

        def act_load(se, set_name):
            from concourse.hw_specs import get_activation_tables
            for idx, nm in enumerate(get_activation_tables(nc.m.arch)):
                if nm == set_name:
                    se.add_instruction(mybir.InstLoadActFuncSet(
                        name=nc.get_next_instruction_name(),
                        act_func_set_id=idx, ins=[], outs=[]))
                    return
            raise ValueError(set_name)

        xln = E0(nc.sbuf_tensor("xln", [P, NCH * EXT], BF16))
        wbuf = E0(nc.sbuf_tensor("wbuf", [P, 24576], BF16))
        wstage = E0(nc.sbuf_tensor("wstage", [P, 4096], F32))
        gw_sb = E0(nc.sbuf_tensor("gw_sb", [P, 64], BF16))
        lb_bc = E0(nc.sbuf_tensor("lb_bc", [P, 8], F32))
        id_bf = E0(nc.sbuf_tensor("id_bf", [P, P], BF16))
        id_f = E0(nc.sbuf_tensor("id_f", [P, P], F32))
        selpk = E0(nc.sbuf_tensor("selpk_s", [P, P], BF16))
        m_first = E0(nc.sbuf_tensor("mfirst_s", [112, P], BF16))
        m_mid = E0(nc.sbuf_tensor("mmid_s", [112, P], BF16))
        m_last = E0(nc.sbuf_tensor("mlast_s", [32, P], BF16))
        ones_rb = E0(nc.sbuf_tensor("ones_rb", [1, 512], BF16))
        ones_cb = E0(nc.sbuf_tensor("ones_cb", [P, 1], BF16))
        ones_cf = E0(nc.sbuf_tensor("ones_cf", [P, 1], F32))
        eps_c = E0(nc.sbuf_tensor("eps_c", [P, 1], F32))

        def xslc(kc, t0, n):
            return xln[:, kc * EXT + t0: kc * EXT + t0 + n]

        with ExitStack() as stA:
            EA = stA.enter_context
            xst = EA(nc.sbuf_tensor("xst", [P, 2 * 4096], F32))
            xbt = EA(nc.sbuf_tensor("xbt", [P, 2 * 4096], BF16))
            sqt = EA(nc.sbuf_tensor("sqt", [P, 4096], BF16))
            tmpA = EA(nc.sbuf_tensor("tmpA", [P, 4096], BF16))
            a_bc = EA(nc.sbuf_tensor("a_bc", [P, 2 * 512], BF16))
            b_bc = EA(nc.sbuf_tensor("b_bc", [P, 2 * 512], BF16))
            mrow = EA(nc.sbuf_tensor("mrow", [1, 512], F32))
            msq = EA(nc.sbuf_tensor("msq", [1, 512], F32))
            varr = EA(nc.sbuf_tensor("varr", [1, 512], F32))
            lnr = EA(nc.sbuf_tensor("lnr", [1, 512], F32))
            arow = EA(nc.sbuf_tensor("arow", [1, 512], BF16))
            brow = EA(nc.sbuf_tensor("brow", [1, 512], BF16))
            sum_ps = EA(nc.psum_tensor("sum_ps", [1, 512], F32))
            sq_ps = EA(nc.psum_tensor("sq_ps", [1, 512], F32))
            ab_ps = EA(nc.psum_tensor("ab_ps", [P, 512], F32))
            bb_ps = EA(nc.psum_tensor("bb_ps", [P, 512], F32))

            tilesA = [(0, 512), (512, 512), (1024, 512), (1536, 512), (2048, 128)]
            M = {}

            with nc.Block("phA", no_gpsimd_drain=True) as blk:
                SC = Sched()
                actload and SC.add("scalar", lambda se: act_load(se, "natural_log_exp_and_others"))
                def gp_w(gp):
                    for dst, src in ((id_bf, idbf_d), (id_f, idf_d), (selpk, selpk_d),
                                     (m_first, mfirst_d), (m_mid, mmid_d), (m_last, mlast_d),
                                     (lb_bc, lbbc_d)):
                        inc(gp.dma_start(out=dst[:, :], in_=src[:, :]), "dG", 16)
                    for kc in range(NCH):
                        d = inc(gp.dma_start(out=wstage[:, 0:3072],
                                             in_=qkvT_d[kc * P:(kc + 1) * P, :]), "dG", 16)
                        gp.wait_ge(sems["dG"], d)
                        inc(gp.tensor_copy(wbuf[:, kc * 3072:(kc + 1) * 3072],
                                           wstage[:, 0:3072]), "sG")
                    for kc in range(NCH):
                        d = inc(gp.dma_start(out=wstage[:, kc * 8: kc * 8 + 8],
                                             in_=gwT_d[kc * P:(kc + 1) * P, :]), "dG", 16)
                    gp.wait_ge(sems["dG"], d)
                    M["gwcast"] = inc(gp.tensor_copy(gw_sb[:, :], wstage[:, 0:64]), "sG")
                SC.add("gpsimd", gp_w)

                def ve_c(ve):
                    inc(ve.memset(ones_rb[:, :], 1.0), "sV")
                    inc(ve.memset(ones_cb[:, :], 1.0), "sV")
                    inc(ve.memset(ones_cf[:, :], 1.0), "sV")
                    M["ones"] = inc(ve.memset(eps_c[:, :], LN_EPS), "sV")
                SC.add("vector", ve_c)

                for it, (c0, nw) in enumerate(tilesA):
                    par = it % 2
                    po = par * 4096
                    ao = par * 512

                    def sy_x(sy, it=it, c0=c0, nw=nw, po=po):
                        if it >= 2:
                            wt(sy, "sV", M[("xln", it - 2)])
                        sx = "xA%d" % (it % 2)
                        for kc in range(NCH):
                            inc(sy.dma_start(out=xst[:, po + kc * 512: po + kc * 512 + nw],
                                             in_=xin[kc * P:(kc + 1) * P, c0:c0 + nw]), sx, 16)
                        M[("x", it)] = cnt[sx]
                    SC.add("sync", sy_x)

                    def ve_cast(ve, it=it, nw=nw, po=po):
                        wt(ve, "xA%d" % (it % 2), M[("x", it)])
                        if it >= 2:
                            wt(ve, "sP", M[("stat", it - 2)])
                        src = xst[:, po:po + 4096].rearrange("p (c n) -> p c n", c=NCH)[:, :, 0:nw]
                        dst = xbt[:, po:po + 4096].rearrange("p (c n) -> p c n", c=NCH)[:, :, 0:nw]
                        M[("cast", it)] = inc(ve.tensor_copy(dst, src), "sV")
                    SC.add("vector", ve_cast)

                    def se_sq(se, it=it, nw=nw, po=po):
                        wt(se, "sV", M[("cast", it)])
                        if it > 0:
                            wt(se, "sP", M[("stat", it - 1)])
                        src = xbt[:, po:po + 4096].rearrange("p (c n) -> p c n", c=NCH)[:, :, 0:nw]
                        dst = sqt[:, :].rearrange("p (c n) -> p c n", c=NCH)[:, :, 0:nw]
                        M[("sq", it)] = inc(se.activation(dst, src, AF.Square), "sA")
                    SC.add("scalar", se_sq)

                    def pe_stat(pe, it=it, nw=nw, po=po):
                        wt(pe, "sV", max(M[("cast", it)], M["ones"]))
                        wt(pe, "sA", M[("sq", it)])
                        if it > 0:
                            wt(pe, "sA", M[("rowm", it - 1)])
                            wt(pe, "sV", M[("rowv", it - 1)])
                        for kc in range(NCH):
                            pe.matmul(sum_ps[0:1, 0:nw], ones_cb[:, :],
                                      xbt[:, po + kc * 512: po + kc * 512 + nw],
                                      start=(kc == 0), stop=(kc == NCH - 1))
                        for kc in range(NCH):
                            mm = pe.matmul(sq_ps[0:1, 0:nw], ones_cb[:, :],
                                           sqt[:, kc * 512: kc * 512 + nw],
                                           start=(kc == 0), stop=(kc == NCH - 1))
                        M[("stat", it)] = inc(mm, "sP")
                    SC.add("tensor", pe_stat)

                    def se_row(se, it=it, nw=nw):
                        wt(se, "sP", M[("stat", it)])
                        if it > 0:
                            wt(se, "sP", M[("abmm", it - 1)])
                            wt(se, "sV", M[("rowb", it - 1)])
                        a1 = inc(se.activation(mrow[0:1, 0:nw], sum_ps[0:1, 0:nw],
                                               AF.Copy, bias=0.0, scale=1.0 / DM), "sA")
                        se.wait_ge(sems["sA"], a1)
                        M[("rowm", it)] = inc(se.activation(msq[0:1, 0:nw], mrow[0:1, 0:nw],
                                                            AF.Square), "sA")
                    SC.add("scalar", se_row)

                    def ve_row(ve, it=it, nw=nw):
                        wt(ve, "sA", M[("rowm", it)])
                        M[("rowv", it)] = inc(
                            ve.scalar_tensor_tensor(varr[0:1, 0:nw], sq_ps[0:1, 0:nw],
                                                    1.0 / DM, msq[0:1, 0:nw],
                                                    ALU.mult, ALU.subtract), "sV")
                    SC.add("vector", ve_row)

                    def se_row2(se, it=it, nw=nw):
                        wt(se, "sV", M[("rowv", it)])
                        a1 = inc(se.activation(lnr[0:1, 0:nw], varr[0:1, 0:nw],
                                               AF.Ln, bias=eps_c[0:1, 0:1], scale=1.0), "sA")
                        se.wait_ge(sems["sA"], a1)
                        M[("rowa", it)] = inc(se.activation(arow[0:1, 0:nw], lnr[0:1, 0:nw],
                                                            AF.Exp, bias=0.0, scale=-0.5), "sA")
                    SC.add("scalar", se_row2)

                    def ve_row2(ve, it=it, nw=nw):
                        wt(ve, "sA", M[("rowa", it)])
                        M[("rowb", it)] = inc(
                            ve.scalar_tensor_tensor(brow[0:1, 0:nw], mrow[0:1, 0:nw],
                                                    -1.0, arow[0:1, 0:nw],
                                                    ALU.mult, ALU.mult), "sV")
                    SC.add("vector", ve_row2)

                    def pe_ab(pe, it=it, nw=nw, par=par):
                        wt(pe, "sA", M[("rowa", it)])
                        wt(pe, "sV", M[("rowb", it)])
                        if it > 0:
                            wt(pe, "sA", M[("abev", it - 1)])
                        pe.matmul(ab_ps[:, 0:nw], ones_rb[0:1, 0:P], arow[0:1, 0:nw],
                                  start=True, stop=True)
                        mm = pe.matmul(bb_ps[:, 0:nw], ones_rb[0:1, 0:P], brow[0:1, 0:nw],
                                       start=True, stop=True)
                        M[("abmm", it)] = inc(mm, "sP")
                    SC.add("tensor", pe_ab)

                    def se_abev(se, it=it, nw=nw, par=par):
                        wt(se, "sP", M[("abmm", it)])
                        if it >= 2:
                            wt(se, "sV", M[("xln", it - 2)])
                        se.copy(a_bc[:, par * 512: par * 512 + nw], ab_ps[:, 0:nw])
                        M[("abev", it)] = inc(
                            se.copy(b_bc[:, par * 512: par * 512 + nw], bb_ps[:, 0:nw]), "sA")
                    SC.add("scalar", se_abev)

                    def ve_xln(ve, it=it, c0=c0, nw=nw, po=po, par=par):
                        wt(ve, "sA", M[("abev", it)])
                        xb3 = xbt[:, po:po + 4096].rearrange("p (c n) -> p c n", c=NCH)[:, :, 0:nw]
                        a3 = a_bc[:, par * 512: par * 512 + nw][:, None, :].broadcast_to([P, NCH, nw])
                        t3 = tmpA[:, :].rearrange("p (c n) -> p c n", c=NCH)[:, :, 0:nw]
                        v1 = inc(ve.tensor_tensor(t3, xb3, a3, ALU.mult), "sV")
                        ve.wait_ge(sems["sV"], v1)
                        o3 = xln[:, :].rearrange("p (c n) -> p c n", c=NCH)[:, :, c0:c0 + nw]
                        b3 = b_bc[:, par * 512: par * 512 + nw][:, None, :].broadcast_to([P, NCH, nw])
                        M[("xln", it)] = inc(ve.tensor_tensor(o3, t3, b3, ALU.add), "sV")
                    SC.add("vector", ve_xln)

                def sy_endA(sy):
                    wt(sy, "dS", cnt["dS"])
                SC.add("sync", sy_endA)

                def gp_endA(gp):
                    wt(gp, "dG", cnt["dG"])
                SC.add("gpsimd", gp_endA)
                SC.flush(blk)

        if stage == "xln":
            with ExitStack() as stX:
                xf = stX.enter_context(nc.sbuf_tensor("xf_dbg", [P, NCH * EXT], F32))
                with nc.Block("phDbg") as blk:
                    def ve_d(ve):
                        inc(ve.tensor_copy(xf[:, :], xln[:, :]), "sV")
                    SC.add("vector", ve_d)
                    def sy_d(sy):
                        wt(sy, "sV", cnt["sV"])
                        d = inc(sy.dma_start(out=ydbg_d[:, 0:NCH * EXT], in_=xf[:, :]), "dS", 16)
                        sy.wait_ge(sems["dS"], d)
                    SC.add("sync", sy_d)
            return nc

        if stage == "ln":
            with nc.Block("phOutL") as blk:
                def sy_l(sy):
                    d = inc(sy.dma_start(out=y_d[:, :], in_=attout_d[:, :]), "dS", 16)
                    sy.wait_ge(sems["dS"], d)
                SC.add("sync", sy_l)
            return nc

        with ExitStack() as stB:
            EB = stB.enter_context
            outwT_bf = EB(nc.sbuf_tensor("outwT_bf", [P, 8192], BF16))
            qkvb_bf = EB(nc.sbuf_tensor("qkvb_bf", [1, 3 * DM], BF16))
            outb_bf = EB(nc.sbuf_tensor("outb_bf", [1, DM], BF16))
            q_t = EB(nc.sbuf_tensor("q_t", [64, 2 * 1792], BF16))
            k_t = EB(nc.sbuf_tensor("k_t", [64, 2 * 2048], BF16))
            v_sb = EB(nc.sbuf_tensor("v_sb", [P, 2 * 1024], BF16))
            sqb = EB(nc.sbuf_tensor("sqb", [P, 1024], BF16))
            rn_ln = EB(nc.sbuf_tensor("rn_ln", [16, P], F32))
            rn_sb = EB(nc.sbuf_tensor("rn_sb", [16, P], BF16))
            bcrn_q = EB(nc.sbuf_tensor("bcrn_q", [64, 1792], BF16))
            bcrn_k = EB(nc.sbuf_tensor("bcrn_k", [64, 2048], BF16))
            e_p = EB(nc.sbuf_tensor("e_p", [112, 1024], BF16))
            p_m = EB(nc.sbuf_tensor("p_m", [112, 1024], BF16))
            p_n = EB(nc.sbuf_tensor("p_n", [112, 1024], BF16))
            sums = EB(nc.sbuf_tensor("sums", [112, 16], F32))
            rinv = EB(nc.sbuf_tensor("rinv", [112, 16], F32))
            pT_sb = EB(nc.sbuf_tensor("pT_sb", [P, 1792], BF16))
            att = EB(nc.sbuf_tensor("att", [P, 896], BF16))
            opj = EB(nc.sbuf_tensor("opj", [P, 1024], F32))
            s_av = EB(nc.psum_tensor("s_av", [P, 1024], F32))
            psB = EB(nc.psum_tensor("psB", [P, 1024], F32))
            pT_ps = EB(nc.psum_tensor("pT_ps", [P, 1024], BF16))
            av_ps = EB(nc.psum_tensor("av_ps", [64, 1024], F32))
            op_ps = EB(nc.psum_tensor("op_ps", [P, 512], F32))

            M = {("unit", -1, 0): 0, ("unit", -1, 1): 0}

            with nc.Block("phB", no_gpsimd_drain=True) as blk:
                SC = Sched()
                actload and SC.add("scalar", lambda se: act_load(se, "natural_log_exp_and_others"))
                def gp_ow(gp):
                    d = inc(gp.dma_start(out=wstage[0:1, 0:3 * DM], in_=qkvb_d[:, :]), "dG", 16)
                    gp.wait_ge(sems["dG"], d)
                    M["qkvb"] = inc(gp.tensor_copy(qkvb_bf[:, :], wstage[0:1, 0:3 * DM]), "sG")
                    d = inc(gp.dma_start(out=wstage[0:1, 0:DM], in_=outb_d[:, :]), "dG", 16)
                    gp.wait_ge(sems["dG"], d)
                    M["outbb"] = inc(gp.tensor_copy(outb_bf[:, :], wstage[0:1, 0:DM]), "sG")
                    for kc in range(NCH):
                        d = inc(gp.dma_start(out=wstage[:, 0:DM],
                                             in_=outwT_d[kc * P:(kc + 1) * P, :]), "dG", 16)
                        gp.wait_ge(sems["dG"], d)
                        inc(gp.tensor_copy(outwT_bf[:, kc * DM:(kc + 1) * DM],
                                           wstage[:, 0:DM]), "sG")
                SC.add("gpsimd", gp_ow)
                M["outw"] = cnt["sG"]

                def emit_build(qb, bsub=9):
                    s_k, nq = QB[qb]
                    qc0 = 8 + s_k
                    sl = qb % 2
                    qo = sl * 1792
                    ko = sl * 2048

                    def pe_q(pe):
                        if qb > 0:
                            wt(pe, "sV", M[("vev", qb - 1)])
                        else:
                            wt(pe, "sG", M["outbb"])
                        for pc in range(NCH):
                            for kc in range(NCH):
                                pe.matmul(psB[:, pc * P: pc * P + nq],
                                          wbuf[:, kc * 3072 + pc * P: kc * 3072 + (pc + 1) * P],
                                          xslc(kc, qc0, nq), start=(kc == 0), stop=False)
                            mm = pe.matmul(psB[:, pc * P: pc * P + nq],
                                           qkvb_bf[0:1, pc * P:(pc + 1) * P],
                                           ones_rb[0:1, 0:nq], start=False, stop=True)
                        M[("qmm", qb)] = inc(mm, "sP")
                    SC.add("tensor", pe_q)

                    def se_qsq(se):
                        wt(se, "sP", M[("qmm", qb)])
                        if qb > 0:
                            wt(se, "sP", M[("kssq", qb - 1)])
                        M[("qsq", qb)] = inc(
                            se.activation(sqb[:, 0:1024], psB[:, 0:1024], AF.Square), "sA")
                    SC.add("scalar", se_qsq)

                    def ve_qev(ve):
                        wt(ve, "sP", M[("qmm", qb)])
                        wt(ve, "sA", M[("qsq", qb)])
                        if qb > 1:
                            wt(ve, "sP", M[("s", qb - 2, 1)])
                        src0 = psB[0:64, 0:1024].rearrange("p (c n) -> p c n", c=NCH)[:, :, 0:nq]
                        src1 = psB[64:128, 0:1024].rearrange("p (c n) -> p c n", c=NCH)[:, :, 0:nq]
                        dst0 = q_t[0:64, qo:qo + 896].rearrange("p (c n) -> p c n", c=NCH)[:, :, 0:nq]
                        dst1 = q_t[0:64, qo + 896:qo + 1792].rearrange("p (c n) -> p c n", c=NCH)[:, :, 0:nq]
                        inc(ve.tensor_copy(dst0, src0), "sV")
                        M[("qev", qb)] = inc(ve.tensor_copy(dst1, src1), "sV")
                    SC.add("vector", ve_qev)
                    if bsub < 2:
                        return

                    def pe_qssq(pe):
                        wt(pe, "sA", M[("qsq", qb)])
                        wt(pe, "sV", M[("qev", qb)])
                        for pc in range(NCH):
                            mm = pe.matmul(psB[0:16, 0:nq], selpk[:, pc * 16:(pc + 1) * 16],
                                           sqb[:, pc * P: pc * P + nq],
                                           start=(pc == 0), stop=(pc == NCH - 1))
                        M[("qssq", qb)] = inc(mm, "sP")
                    SC.add("tensor", pe_qssq)

                    def se_qrn(se):
                        wt(se, "sP", M[("qssq", qb)])
                        if qb > 0:
                            wt(se, "rw", M[("krnw", qb - 1)])
                        a1 = inc(se.activation(rn_ln[0:16, 0:nq], psB[0:16, 0:nq], AF.Ln), "sA")
                        se.wait_ge(sems["sA"], a1)
                        M[("qrn", qb)] = inc(
                            se.activation(rn_sb[0:16, 0:nq], rn_ln[0:16, 0:nq],
                                          AF.Exp, bias=0.0, scale=-0.5), "sA")
                    SC.add("scalar", se_qrn)

                    def sy_qrn(sy):
                        idx = sl * 2
                        wt(sy, "sA", M[("qrn", qb)])
                        if qb > 0:
                            wt(sy, "sV", M[("qn", qb - 1)])
                        if qb > 1:
                            wt(sy, "bq", M[("qrnb", qb - 2)])
                        d1 = inc(sy.dma_start(out=rnscr_d[idx, 0:16, 0:nq],
                                              in_=rn_sb[0:16, 0:nq]), "rw", 16)
                        M[("qrnw", qb)] = d1
                        sy.wait_ge(sems["rw"], d1)
                        srcq = rnscr_d[idx].rearrange("(pc two) k -> two pc k", two=2)
                        for rh in range(2):
                            bqv = bcrn_q[:, rh * 896: rh * 896 + 896].rearrange(
                                "p (c n) -> p c n", c=NCH)[:, :, 0:nq]
                            mq = inc(sy.dma_start(
                                out=bqv, in_=srcq[rh][:, 0:nq][None].broadcast_to([64, NCH, nq])), "bq", 16)
                        M[("qrnb", qb)] = mq
                    SC.add("sync", sy_qrn)

                    def ve_qn(ve):
                        wt(ve, "bq", M[("qrnb", qb)])
                        M[("qn", qb)] = inc(
                            ve.tensor_tensor(q_t[0:64, qo:qo + 1792], q_t[0:64, qo:qo + 1792],
                                             bcrn_q[0:64, :], ALU.mult), "sV")
                    SC.add("vector", ve_qn)
                    if bsub < 3:
                        return

                    def pe_k(pe):
                        wt(pe, "sA", M[("qrn", qb)])
                        wt(pe, "sV", M[("qev", qb)])
                        for pc in range(NCH):
                            for kc in range(NCH):
                                pe.matmul(psB[:, pc * P:(pc + 1) * P],
                                          wbuf[:, kc * 3072 + DM + pc * P: kc * 3072 + DM + (pc + 1) * P],
                                          xslc(kc, s_k, P), start=(kc == 0), stop=False)
                            mm = pe.matmul(psB[:, pc * P:(pc + 1) * P],
                                           qkvb_bf[0:1, DM + pc * P: DM + (pc + 1) * P],
                                           ones_rb[0:1, 0:P], start=False, stop=True)
                        M[("kmm", qb)] = inc(mm, "sP")
                    SC.add("tensor", pe_k)

                    def se_ksq(se):
                        wt(se, "sP", M[("kmm", qb)])
                        wt(se, "sP", M[("qssq", qb)])
                        M[("ksq", qb)] = inc(
                            se.activation(sqb[:, 0:1024], psB[:, 0:1024], AF.Square), "sA")
                    SC.add("scalar", se_ksq)

                    def ve_kev(ve):
                        wt(ve, "sP", M[("kmm", qb)])
                        wt(ve, "sA", M[("ksq", qb)])
                        if qb > 1:
                            wt(ve, "sP", M[("s", qb - 2, 1)])
                        inc(ve.tensor_copy(k_t[0:64, ko:ko + 1024], psB[0:64, 0:1024]), "sV")
                        M[("kev", qb)] = inc(
                            ve.tensor_copy(k_t[0:64, ko + 1024:ko + 2048], psB[64:128, 0:1024]), "sV")
                    SC.add("vector", ve_kev)
                    if bsub < 4:
                        return

                    def pe_kssq(pe):
                        wt(pe, "sA", M[("ksq", qb)])
                        wt(pe, "sV", M[("kev", qb)])
                        for pc in range(NCH):
                            mm = pe.matmul(psB[0:16, 0:P], selpk[:, pc * 16:(pc + 1) * 16],
                                           sqb[:, pc * P:(pc + 1) * P],
                                           start=(pc == 0), stop=(pc == NCH - 1))
                        M[("kssq", qb)] = inc(mm, "sP")
                    SC.add("tensor", pe_kssq)

                    def se_krn(se):
                        wt(se, "sP", M[("kssq", qb)])
                        wt(se, "rw", M[("qrnw", qb)])
                        a1 = inc(se.activation(rn_ln[0:16, 0:P], psB[0:16, 0:P], AF.Ln), "sA")
                        se.wait_ge(sems["sA"], a1)
                        M[("krn", qb)] = inc(
                            se.activation(rn_sb[0:16, 0:P], rn_ln[0:16, 0:P],
                                          AF.Exp, bias=0.0, scale=-0.5), "sA")
                    SC.add("scalar", se_krn)

                    def sy_krn(sy):
                        idx = sl * 2 + 1
                        wt(sy, "sA", M[("krn", qb)])
                        if qb > 0:
                            wt(sy, "sV", M[("kn", qb - 1)])
                        if qb > 1:
                            wt(sy, "bk", M[("krnb", qb - 2)])
                        d1 = inc(sy.dma_start(out=rnscr_d[idx, 0:16, 0:P],
                                              in_=rn_sb[0:16, 0:P]), "rw", 16)
                        M[("krnw", qb)] = d1
                        sy.wait_ge(sems["rw"], d1)
                        srck = rnscr_d[idx].rearrange("(pc two) k -> two pc k", two=2)
                        for rh in range(2):
                            bkv = bcrn_k[:, rh * 1024: rh * 1024 + 1024].rearrange(
                                "p (c n) -> p c n", c=NCH)
                            mk = inc(sy.dma_start(
                                out=bkv, in_=srck[rh][None].broadcast_to([64, NCH, P])), "bk", 16)
                        M[("krnb", qb)] = mk
                    SC.add("sync", sy_krn)

                    def ve_kn(ve):
                        wt(ve, "bk", M[("krnb", qb)])
                        M[("kn", qb)] = inc(
                            ve.tensor_tensor(k_t[0:64, ko:ko + 2048], k_t[0:64, ko:ko + 2048],
                                             bcrn_k[0:64, :], ALU.mult), "sV")
                    SC.add("vector", ve_kn)
                    if bsub < 5:
                        return

                    def pe_v(pe):
                        wt(pe, "sA", M[("krn", qb)])
                        wt(pe, "sV", M[("kev", qb)])
                        if qb > 1:
                            wt(pe, "sP", M[("av", qb - 2, 1)])
                        for vh in range(2):
                            for kc in range(NCH):
                                pe.matmul(psB[:, vh * 512:(vh + 1) * 512],
                                          xslc(kc, s_k, P),
                                          wbuf[:, kc * 3072 + 2048 + vh * 512: kc * 3072 + 2048 + (vh + 1) * 512],
                                          start=(kc == 0), stop=False)
                            mm = pe.matmul(psB[:, vh * 512:(vh + 1) * 512],
                                           ones_rb[0:1, 0:P],
                                           qkvb_bf[0:1, 2048 + vh * 512: 2048 + (vh + 1) * 512],
                                           start=False, stop=True)
                        M[("vmm", qb)] = inc(mm, "sP")
                    SC.add("tensor", pe_v)

                    def ve_vev(ve):
                        vo = sl * 1024
                        wt(ve, "sP", M[("vmm", qb)])
                        M[("vev", qb)] = inc(
                            ve.tensor_copy(v_sb[:, vo:vo + 1024], psB[:, 0:1024]), "sV")
                    SC.add("vector", ve_vev)

                def emit_scores(qb, half):
                    s_k, nq = QB[qb]
                    sl = qb % 2
                    qo = sl * 1792
                    ko = sl * 2048

                    def pe_s(pe):
                        wt(pe, "sV", M[("qn", qb)])
                        wt(pe, "sV", M[("kn", qb)])
                        prev = (qb, 0) if half == 1 else (qb - 1, 1)
                        wt(pe, "sA", M.get(("exp", prev[0], prev[1]), 0))
                        for hh in range(8):
                            h = half * 8 + hh
                            pc, rh = h // 2, h % 2
                            mm = pe.matmul(s_av[0:nq, hh * P:(hh + 1) * P],
                                           q_t[0:64, qo + rh * 896 + pc * 112: qo + rh * 896 + pc * 112 + nq],
                                           k_t[0:64, ko + rh * 1024 + pc * P: ko + rh * 1024 + (pc + 1) * P],
                                           start=True, stop=True)
                        M[("s", qb, half)] = inc(mm, "sP")
                    SC.add("tensor", pe_s)

                def emit_exp(qb, half):
                    s_k, nq = QB[qb]

                    def se_exp(se):
                        wt(se, "sP", M[("s", qb, half)])
                        prev = (qb, 0) if half == 1 else (qb - 1, 1)
                        wt(se, "sV", M.get(("pm", prev[0], prev[1]), 0))
                        M[("exp", qb, half)] = inc(
                            se.activation(e_p[0:nq, 0:1024], s_av[0:nq, 0:1024], AF.Exp), "sA")
                    SC.add("scalar", se_exp)

                def emit_softmax(qb, half, sub=6):
                    s_k, nq = QB[qb]
                    mask = m_first if qb == 0 else (m_last if qb == NQB - 1 else m_mid)
                    if sub < 2:
                        return

                    def ve_soft(ve):
                        wt(ve, "sA", M[("exp", qb, half)])
                        e3 = e_p[0:nq, :].rearrange("p (h j) -> p h j", h=8)
                        mk3 = mask[0:nq, :][:, None, :].broadcast_to([nq, 8, P])
                        pm3 = p_m[0:nq, :].rearrange("p (h j) -> p h j", h=8)
                        v1 = inc(ve.tensor_tensor(pm3, e3, mk3, ALU.mult), "sV")
                        M[("pm", qb, half)] = v1
                        ve.wait_ge(sems["sV"], v1)
                        v2 = inc(ve.tensor_reduce(
                            sums[0:nq, half * 8:(half + 1) * 8][:, :, None],
                            p_m[0:nq, :].rearrange("p (h j) -> p h j", h=8),
                            AX.X, ALU.add), "sV")
                        ve.wait_ge(sems["sV"], v2)
                        v3 = inc(ve.reciprocal(rinv[0:nq, half * 8:(half + 1) * 8],
                                               sums[0:nq, half * 8:(half + 1) * 8]), "sV")
                        ve.wait_ge(sems["sV"], v3)
                        ri3 = rinv[0:nq, half * 8:(half + 1) * 8][:, :, None].broadcast_to([nq, 8, P])
                        pn3 = p_n[0:nq, :].rearrange("p (h j) -> p h j", h=8)
                        M[("pn", qb, half)] = inc(ve.tensor_tensor(pn3, pm3, ri3, ALU.mult), "sV")
                    SC.add("vector", ve_soft)
                    if sub < 3:
                        return

                    def pe_tr(pe):
                        wt(pe, "sV", M[("pn", qb, half)])
                        prevu = (qb, 0) if half == 1 else (qb - 1, 1)
                        wt(pe, "sV", M.get(("pTe", prevu[0], prevu[1]), 0))
                        for hh in range(8):
                            h = half * 8 + hh
                            mm = pe.transpose(pT_ps[0:P, hh * 112: hh * 112 + nq],
                                              p_n[0:nq, hh * P:(hh + 1) * P],
                                              id_bf[0:nq, 0:nq])
                        M[("tr", qb, half)] = inc(mm, "sP")
                    SC.add("tensor", pe_tr)
                    if sub < 4:
                        return

                    def ve_pTe(ve):
                        wt(ve, "sP", M[("tr", qb, half)])
                        wt(ve, "sP", M.get(("av", qb - 1, half), 0))
                        src = pT_ps[:, 0:896].rearrange(
                            "p (h j) -> p h j", h=8)[:, :, 0:nq]
                        dst = pT_sb[:, half * 896: half * 896 + 896].rearrange(
                            "p (h j) -> p h j", h=8)[:, :, 0:nq]
                        M[("pTe", qb, half)] = inc(ve.tensor_copy(dst, src), "sV")
                    SC.add("vector", ve_pTe)
                    if sub < 5:
                        return

                    def pe_av(pe):
                        sl = qb % 2
                        ko = sl * 1024
                        wt(pe, "sV", M[("pTe", qb, half)])
                        prevu = (qb, 0) if half == 1 else (qb - 1, 1)
                        wt(pe, "sV", M[("unit", prevu[0], prevu[1])])
                        for hh in range(8):
                            h = half * 8 + hh
                            mm = pe.matmul(av_ps[0:64, hh * P: hh * P + nq],
                                           v_sb[:, ko + h * 64: ko + (h + 1) * 64],
                                           pT_sb[:, half * 896 + hh * 112: half * 896 + hh * 112 + nq],
                                           start=True, stop=True)
                        M[("av", qb, half)] = inc(mm, "sP")
                    SC.add("tensor", pe_av)
                    if sub < 6:
                        return

                    def ve_att(ve):
                        wt(ve, "sP", M[("av", qb, half)])
                        wt(ve, "sP", M.get(("op", qb - 1), 0))
                        v1 = 0
                        for par_i in range(2):
                            src = av_ps[0:64, 0:1024].rearrange("p (h j) -> p h j", j=128)[:, par_i::2, 0:nq]
                            dst = att[par_i * 64:(par_i + 1) * 64,
                                      half * 448: half * 448 + 448].rearrange(
                                "p (c j) -> p c j", c=4)[:, :, 0:nq]
                            v1 = inc(ve.tensor_copy(dst, src), "sV")
                        M[("unit", qb, half)] = v1
                    SC.add("vector", ve_att)

                def emit_outproj(qb):
                    s_k, nq = QB[qb]

                    for rnd in range(2):
                        def pe_op(pe, rnd=rnd):
                            if rnd == 0:
                                wt(pe, "sV", M[("unit", qb, 1)])
                                wt(pe, "sA", M.get(("ope", qb - 1, 1), 0))
                                wt(pe, "sG", M["outw"])
                            else:
                                wt(pe, "sA", M[("ope", qb, 0)])
                            for oc4 in range(4):
                                oc = rnd * 4 + oc4
                                for pc in range(NCH):
                                    pe.matmul(op_ps[:, oc4 * P: oc4 * P + nq],
                                              outwT_bf[:, pc * DM + oc * P: pc * DM + (oc + 1) * P],
                                              att[:, pc * 112: pc * 112 + nq],
                                              start=(pc == 0), stop=False)
                                mm = pe.matmul(op_ps[:, oc4 * P: oc4 * P + nq],
                                               outb_bf[0:1, oc * P:(oc + 1) * P],
                                               ones_rb[0:1, 0:nq], start=False, stop=True)
                            M[("op", qb, rnd)] = inc(mm, "sP")
                        SC.add("tensor", pe_op)

                        def se_ope(se, rnd=rnd):
                            wt(se, "sP", M[("op", qb, rnd)])
                            if rnd == 0:
                                wt(se, "dS", M.get(("aod", qb - 1), 0))
                            M[("ope", qb, rnd)] = inc(
                                se.copy(opj[:, rnd * 512:(rnd + 1) * 512], op_ps[:, 0:512]), "sA")
                        SC.add("scalar", se_ope)

                    def sy_ao(sy):
                        wt(sy, "sA", M[("ope", qb, 1)])
                        for oc in range(NCH):
                            inc(sy.dma_start(out=attout_d[oc * P:(oc + 1) * P, s_k:s_k + nq],
                                             in_=opj[:, oc * P: oc * P + nq]), "dS", 16)
                        M[("aod", qb)] = cnt["dS"]
                    SC.add("sync", sy_ao)

                if bstop < 1:
                    emit_build(0, bsub=int(round(bstop * 10)))
                else:
                    emit_build(0)
                for qb in range(nqb if bstop >= 1 else 0):
                    if bstop >= 2:
                        emit_scores(qb, 0)
                    if qb + 1 < nqb:
                        emit_build(qb + 1)
                    if bstop >= 3:
                        emit_exp(qb, 0)
                    if bstop >= 9:
                        emit_scores(qb, 1)
                    if bstop >= 10 and qb > 0:
                        emit_outproj(qb - 1)
                    if bstop >= 3:
                        emit_softmax(qb, 0, sub=bstop - 2)
                    if bstop >= 9:
                        emit_exp(qb, 1)
                        emit_softmax(qb, 1)
                if bstop >= 10 and nqb > 0:
                    emit_outproj(nqb - 1)

                if stage == "qkv0":
                    qkvdbg = EB(nc.sbuf_tensor("qkvdbg", [P, 4864], F32))

                    def ve_dbg(ve):
                        wt(ve, "sV", M[("qn", 0)])
                        wt(ve, "sV", M[("kn", 0)])
                        wt(ve, "sV", M[("vev", 0)])
                        inc(ve.tensor_copy(qkvdbg[0:64, 0:1792], q_t[0:64, 0:1792]), "sV")
                        inc(ve.tensor_copy(qkvdbg[0:64, 1792:3840], k_t[0:64, 0:2048]), "sV")
                        inc(ve.tensor_copy(qkvdbg[:, 3840:4864], v_sb[:, 0:1024]), "sV")
                    SC.add("vector", ve_dbg)

                    def sy_dbg(sy):
                        wt(sy, "sV", cnt["sV"])
                        d = inc(sy.dma_start(out=ydbg_d[:, 0:4864], in_=qkvdbg[:, :]), "dS", 16)
                        sy.wait_ge(sems["dS"], d)
                    SC.add("sync", sy_dbg)

                def sy_endB(sy):
                    wt(sy, "dS", cnt["dS"])
                SC.add("sync", sy_endB)

                def gp_endB(gp):
                    wt(gp, "dG", cnt["dG"])
                SC.add("gpsimd", gp_endB)
                SC.flush(blk)

        if stage == "attn":
            with nc.Block("phOutA") as blk:
                def sy_o(sy):
                    d = inc(sy.dma_start(out=y_d[:, :], in_=attout_d[:, :]), "dS", 16)
                    sy.wait_ge(sems["dS"], d)
                blk.sync(sy_o)
            return nc

        with ExitStack() as stDE:
            ED = stDE.enter_context
            acc = ED(nc.sbuf_tensor("acc", [P, NCH * TC], F32))

            def acc3(t0, n):
                return acc[:, :].rearrange("p (c n) -> p c n", c=NCH)[:, :, t0:t0 + n]

            def we_slot(e):
                return (e % 2) * 12288

            MW = {}
            _wstfree = [0]

            def emit_wexp(SCref, e, wsem, wval, q="sync"):
                parts = []
                for part in range(3):
                    def srcs(part=part, e=e):
                        out = []
                        if part == 0:
                            for kc in range(NCH):
                                s = egwT_d[e, kc * P:(kc + 1) * P, :] if e < 8 else                                     sgwT_d[kc * P:(kc + 1) * P, (e - 8) * 512:(e - 7) * 512]
                                out.append((wstage[:, kc * 512:(kc + 1) * 512], s))
                        elif part == 1:
                            for kc in range(NCH):
                                s = euwT_d[e, kc * P:(kc + 1) * P, :] if e < 8 else                                     suwT_d[kc * P:(kc + 1) * P, (e - 8) * 512:(e - 7) * 512]
                                out.append((wstage[:, kc * 512:(kc + 1) * 512], s))
                        else:
                            for ic in range(4):
                                s = edwT_d[e, ic * P:(ic + 1) * P, :] if e < 8 else                                     sdwT_d[(e - 8) * 512 + ic * P: (e - 8) * 512 + (ic + 1) * P, :]
                                out.append((wstage[:, ic * DM:(ic + 1) * DM], s))
                        return out

                    def sy_st(sy, part=part, e=e, wsem=wsem, wval=wval, _wf=_wstfree[0]):
                        if part == 0 and wval > 0:
                            wt(sy, wsem, wval)
                        wt(sy, "sG", _wf)
                        for dst, s in srcs(part):
                            inc(sy.dma_start(out=dst, in_=s), "wst", 16)
                        M[("wst", e, part)] = cnt["wst"]
                    SCref.add(q, sy_st)

                    def gp_ct(gp, part=part, e=e):
                        wt(gp, "wst", M[("wst", e, part)])
                        off = we_slot(e) + part * 4096
                        M[("wct", e, part)] = inc(
                            gp.tensor_copy(wbuf[:, off: off + 4096], wstage[:, :]), "sG")
                    SCref.add("gpsimd", gp_ct)
                    _wstfree[0] = M[("wct", e, part)]
                MW[e] = M[("wct", e, 2)]

            with ExitStack() as stD:
                DD = stD.enter_context
                xstD = DD(nc.sbuf_tensor("xstD", [P, 2 * 1024], F32))
                aostD = DD(nc.sbuf_tensor("aostD", [P, 2 * 1024], F32))
                xb2 = DD(nc.sbuf_tensor("xb2", [P, 1024], BF16))
                sq2 = DD(nc.sbuf_tensor("sq2", [P, 1024], BF16))
                tmpD = DD(nc.sbuf_tensor("tmpD", [P, 1024], BF16))
                a_bcD = DD(nc.sbuf_tensor("a_bcD", [P, 2 * P], BF16))
                b_bcD = DD(nc.sbuf_tensor("b_bcD", [P, 2 * P], BF16))
                mrowD = DD(nc.sbuf_tensor("mrowD", [1, P], F32))
                msqD = DD(nc.sbuf_tensor("msqD", [1, P], F32))
                varrD = DD(nc.sbuf_tensor("varrD", [1, P], F32))
                lnrD = DD(nc.sbuf_tensor("lnrD", [1, P], F32))
                arowD = DD(nc.sbuf_tensor("arowD", [1, P], BF16))
                browD = DD(nc.sbuf_tensor("browD", [1, P], BF16))
                sc0 = DD(nc.sbuf_tensor("sc0", [P, 8], F32))
                sce = DD(nc.sbuf_tensor("sce", [P, 8], F32))
                scs = DD(nc.sbuf_tensor("scs", [P, 8], F32))
                gm1 = DD(nc.sbuf_tensor("gm1", [P, 1], F32))
                gm2 = DD(nc.sbuf_tensor("gm2", [P, 1], F32))
                geq = DD(nc.sbuf_tensor("geq", [P, 8], F32))
                gs2 = DD(nc.sbuf_tensor("gs2", [P, 8], F32))
                sel01 = DD(nc.sbuf_tensor("sel01", [P, 8], F32))
                gden = DD(nc.sbuf_tensor("gden", [P, 1], F32))
                gwin = DD(nc.sbuf_tensor("gwin", [P, 1], F32))
                wful = DD(nc.sbuf_tensor("wful", [P, 8], F32))
                selT = DD(nc.sbuf_tensor("selT", [8, TC], F32))
                cwsb = DD(nc.sbuf_tensor("cwsb", [8, 2], F32))
                cntg = DD(nc.sbuf_tensor("cntg", [8, 1], F32))
                wsumg = DD(nc.sbuf_tensor("wsumg", [8, 1], F32))
                wmaxg = DD(nc.sbuf_tensor("wmaxg", [8, 1], F32))
                wmean = DD(nc.sbuf_tensor("wmean", [8, 1], F32))
                coefT = DD(nc.sbuf_tensor("coefT", [8, TC], BF16))
                sumD = DD(nc.psum_tensor("sumD", [1, P], F32))
                sqD = DD(nc.psum_tensor("sqD", [1, P], F32))
                abD_ps = DD(nc.psum_tensor("abD_ps", [P, P], F32))
                bbD_ps = DD(nc.psum_tensor("bbD_ps", [P, P], F32))
                glog = DD(nc.psum_tensor("glog", [P, 8], F32))
                selT_ps = DD(nc.psum_tensor("selT_ps", [8, P], F32))
                cnt_ps = DD(nc.psum_tensor("cnt_ps", [8, 1], F32))
                wsum_ps = DD(nc.psum_tensor("wsum_ps", [8, 1], F32))

                M = {}

                with nc.Block("phD", no_gpsimd_drain=True) as blk:
                    SC = Sched()
                    actload and SC.add("scalar", lambda se: act_load(se, "natural_log_exp_and_others"))
                    emit_wexp(SC, 0, "dG", 0, q="gpsimd")
                    emit_wexp(SC, 1, "dG", 0, q="gpsimd")

                    for it in range(NTD):
                        t0 = it * P
                        par = it % 2
                        po = par * 1024
                        ao = par * P

                        def sy_xd(sy, it=it, t0=t0, po=po):
                            if it >= 2:
                                wt(sy, "sV", M[("xb2c", it - 2)])
                            sx = "xd%d" % (it % 2)
                            for kc in range(NCH):
                                inc(sy.dma_start(out=xstD[:, po + kc * P: po + (kc + 1) * P],
                                                 in_=xin[kc * P:(kc + 1) * P, 8 + t0: 8 + t0 + P]), sx, 16)
                            for kc in range(NCH):
                                inc(sy.dma_start(out=aostD[:, po + kc * P: po + (kc + 1) * P],
                                                 in_=attout_d[kc * P:(kc + 1) * P, t0: t0 + P]), sx, 16)
                            M[("xd", it)] = cnt[sx]
                        SC.add("sync", sy_xd)

                        def ve_x2(ve, it=it, t0=t0, po=po):
                            wt(ve, "xd%d" % (it % 2), M[("xd", it)])
                            x3 = xstD[:, po:po + 1024].rearrange("p (c n) -> p c n", c=NCH)
                            a3 = aostD[:, po:po + 1024].rearrange("p (c n) -> p c n", c=NCH)
                            v1 = inc(ve.tensor_tensor(acc3(t0, P), x3, a3, ALU.add), "sV")
                            ve.wait_ge(sems["sV"], v1)
                            if it > 0:
                                wt(ve, "sP", M[("statD", it - 1)])
                                wt(ve, "sP", M[("gl", it - 1)])
                            v2 = inc(ve.tensor_copy(
                                xb2[:, :].rearrange("p (c n) -> p c n", c=NCH), acc3(t0, P)), "sV")
                            M[("xb2c", it)] = v2
                        SC.add("vector", ve_x2)

                        def se_sq2(se, it=it):
                            wt(se, "sV", M[("xb2c", it)])
                            if it > 0:
                                wt(se, "sP", M[("statD", it - 1)])
                            M[("sq2", it)] = inc(se.activation(sq2[:, :], xb2[:, :], AF.Square), "sA")
                        SC.add("scalar", se_sq2)

                        def pe_statD(pe, it=it):
                            wt(pe, "sA", M[("sq2", it)])
                            if it > 0:
                                wt(pe, "sA", M[("rowmD", it - 1)])
                                wt(pe, "sV", M[("rowvD", it - 1)])
                            for kc in range(NCH):
                                pe.matmul(sumD[0:1, :], ones_cb[:, :],
                                          xb2[:, kc * P:(kc + 1) * P],
                                          start=(kc == 0), stop=(kc == NCH - 1))
                            for kc in range(NCH):
                                mm = pe.matmul(sqD[0:1, :], ones_cb[:, :],
                                               sq2[:, kc * P:(kc + 1) * P],
                                               start=(kc == 0), stop=(kc == NCH - 1))
                            M[("statD", it)] = inc(mm, "sP")
                        SC.add("tensor", pe_statD)

                        def se_rowD(se, it=it):
                            wt(se, "sP", M[("statD", it)])
                            if it > 0:
                                wt(se, "sP", M[("abmmD", it - 1)])
                                wt(se, "sV", M[("rowbD", it - 1)])
                            a1 = inc(se.activation(mrowD[0:1, :], sumD[0:1, :],
                                                   AF.Copy, bias=0.0, scale=1.0 / DM), "sA")
                            se.wait_ge(sems["sA"], a1)
                            M[("rowmD", it)] = inc(se.activation(msqD[0:1, :], mrowD[0:1, :],
                                                                 AF.Square), "sA")
                        SC.add("scalar", se_rowD)

                        def ve_rowD(ve, it=it):
                            wt(ve, "sA", M[("rowmD", it)])
                            M[("rowvD", it)] = inc(
                                ve.scalar_tensor_tensor(varrD[0:1, :], sqD[0:1, :], 1.0 / DM,
                                                        msqD[0:1, :], ALU.mult, ALU.subtract), "sV")
                        SC.add("vector", ve_rowD)

                        def se_rowD2(se, it=it):
                            wt(se, "sV", M[("rowvD", it)])
                            a1 = inc(se.activation(lnrD[0:1, :], varrD[0:1, :],
                                                   AF.Ln, bias=eps_c[0:1, 0:1], scale=1.0), "sA")
                            se.wait_ge(sems["sA"], a1)
                            M[("rowaD", it)] = inc(se.activation(arowD[0:1, :], lnrD[0:1, :],
                                                                 AF.Exp, bias=0.0, scale=-0.5), "sA")
                        SC.add("scalar", se_rowD2)

                        def ve_rowD2(ve, it=it):
                            wt(ve, "sA", M[("rowaD", it)])
                            M[("rowbD", it)] = inc(
                                ve.scalar_tensor_tensor(browD[0:1, :], mrowD[0:1, :], -1.0,
                                                        arowD[0:1, :], ALU.mult, ALU.mult), "sV")
                        SC.add("vector", ve_rowD2)

                        def pe_abD(pe, it=it, par=par):
                            wt(pe, "sA", M[("rowaD", it)])
                            wt(pe, "sV", M[("rowbD", it)])
                            if it > 0:
                                wt(pe, "sA", M[("abevD", it - 1)])
                            pe.matmul(abD_ps[:, :], ones_rb[0:1, 0:P], arowD[0:1, :],
                                      start=True, stop=True)
                            mm = pe.matmul(bbD_ps[:, :], ones_rb[0:1, 0:P], browD[0:1, :],
                                           start=True, stop=True)
                            M[("abmmD", it)] = inc(mm, "sP")
                        SC.add("tensor", pe_abD)

                        def se_abevD(se, it=it, ao=ao):
                            wt(se, "sP", M[("abmmD", it)])
                            if it >= 2:
                                wt(se, "sV", M[("xln2", it - 2)])
                            se.copy(a_bcD[:, ao:ao + P], abD_ps[:, :])
                            M[("abevD", it)] = inc(se.copy(b_bcD[:, ao:ao + P], bbD_ps[:, :]), "sA")
                        SC.add("scalar", se_abevD)

                        def ve_xln2(ve, it=it, t0=t0, ao=ao):
                            wt(ve, "sA", M[("abevD", it)])
                            xb3 = xb2[:, :].rearrange("p (c n) -> p c n", c=NCH)
                            a3 = a_bcD[:, ao:ao + P][:, None, :].broadcast_to([P, NCH, P])
                            t3 = tmpD[:, :].rearrange("p (c n) -> p c n", c=NCH)
                            v1 = inc(ve.tensor_tensor(t3, xb3, a3, ALU.mult), "sV")
                            ve.wait_ge(sems["sV"], v1)
                            o3 = xln[:, :].rearrange("p (c n) -> p c n", c=NCH)[:, :, t0:t0 + P]
                            b3 = b_bcD[:, ao:ao + P][:, None, :].broadcast_to([P, NCH, P])
                            M[("xln2", it)] = inc(ve.tensor_tensor(o3, t3, b3, ALU.add), "sV")
                        SC.add("vector", ve_xln2)

                        def pe_gl(pe, it=it, t0=t0):
                            wt(pe, "sV", M[("xln2", it)])
                            if it > 0:
                                wt(pe, "sV", M[("gl_read", it - 1)])
                            for kc in range(NCH):
                                mm = pe.matmul(glog[:, :], xslc(kc, t0, P),
                                               gw_sb[:, kc * 8:(kc + 1) * 8],
                                               start=(kc == 0), stop=(kc == NCH - 1))
                            M[("gl", it)] = inc(mm, "sP")
                        SC.add("tensor", pe_gl)

                        def ve_glread(ve, it=it):
                            wt(ve, "sP", M[("gl", it)])
                            M[("gl_read", it)] = inc(
                                ve.tensor_tensor(sc0[:, :], glog[:, :], lb_bc[:, :], ALU.add), "sV")
                        SC.add("vector", ve_glread)

                        def se_sig(se, it=it):
                            wt(se, "sV", M[("gl_read", it)])
                            M[("sce", it)] = inc(se.activation(sce[:, :], sc0[:, :],
                                                               AF.Exp, bias=0.0, scale=-1.0), "sA")
                        SC.add("scalar", se_sig)

                        def ve_top2(ve, it=it):
                            wt(ve, "sA", M[("sce", it)])
                            v = inc(ve.tensor_scalar_add(sce[:, :], sce[:, :], 1.0), "sV")
                            ve.wait_ge(sems["sV"], v)
                            v = inc(ve.reciprocal(scs[:, :], sce[:, :]), "sV")
                            ve.wait_ge(sems["sV"], v)
                            v = inc(ve.reduce_max(gm1[:, :], scs[:, :], AX.X), "sV")
                            ve.wait_ge(sems["sV"], v)
                            v = inc(ve.tensor_single_scalar(geq[:, :], scs[:, :], gm1[:, :],
                                                            ALU.is_equal), "sV")
                            ve.wait_ge(sems["sV"], v)
                            v = inc(ve.scalar_tensor_tensor(gs2[:, :], geq[:, :], 1e9, scs[:, :],
                                                            ALU.mult, ALU.subtract), "sV")
                            ve.wait_ge(sems["sV"], v)
                            v = inc(ve.tensor_reduce(gm2[:, :], gs2[:, :], AX.X, ALU.min,
                                                     negate=True), "sV")
                            ve.wait_ge(sems["sV"], v)
                            v = inc(ve.tensor_single_scalar(sel01[:, :], scs[:, :], gm2[:, :],
                                                            ALU.is_ge), "sV")
                            ve.wait_ge(sems["sV"], v)
                            v = inc(ve.tensor_scalar(gden[:, :], gm1[:, :], gm2[:, :], 1e-8,
                                                     ALU.add, ALU.add), "sV")
                            ve.wait_ge(sems["sV"], v)
                            v = inc(ve.reciprocal(gwin[:, :], gden[:, :]), "sV")
                            ve.wait_ge(sems["sV"], v)
                            v = inc(ve.scalar_tensor_tensor(wful[:, :], scs[:, :], gwin[:, :],
                                                            sel01[:, :], ALU.mult, ALU.mult), "sV")
                            M[("top2", it)] = v
                        SC.add("vector", ve_top2)

                        def pe_gstat(pe, it=it):
                            wt(pe, "sV", M[("top2", it)])
                            pe.matmul(cnt_ps[0:8, 0:1], sel01[:, :], ones_cf[:, :],
                                      start=(it == 0), stop=(it == NTD - 1))
                            pe.matmul(wsum_ps[0:8, 0:1], wful[:, :], ones_cf[:, :],
                                      start=(it == 0), stop=(it == NTD - 1))
                            if it > 0:
                                wt(pe, "sA", M[("selTe", it - 1)])
                            mm = pe.transpose(selT_ps[0:8, 0:P], sel01[0:P, 0:8], id_f[0:P, 0:P])
                            M[("gstat", it)] = inc(mm, "sP")
                        SC.add("tensor", pe_gstat)

                        def se_selT(se, it=it):
                            wt(se, "sP", M[("gstat", it)])
                            M[("selTe", it)] = inc(
                                se.copy(selT[:, it * P:(it + 1) * P], selT_ps[0:8, :]), "sA")
                        SC.add("scalar", se_selT)

                    def se_cw(se):
                        wt(se, "sP", M[("gstat", NTD - 1)])
                        inc(se.copy(cwsb[:, 0:1], cnt_ps[0:8, :]), "sA")
                        M["cw"] = inc(se.copy(cwsb[:, 1:2], wsum_ps[0:8, :]), "sA")
                    SC.add("scalar", se_cw)

                    def gp_cc(gp):
                        wt(gp, "sA", M["cw"])
                        inc(gp.dma_start(out=ccin_d[0:8, :], in_=cwsb[:, 0:1]), "dG", 16)
                        d = inc(gp.dma_start(out=ccin_d[8:16, :], in_=cwsb[:, 1:2]), "dG", 16)
                        gp.wait_ge(sems["dG"], d)
                        c = inc(gp.collective_compute(
                            "AllReduce", ALU.add, replica_groups=[list(range(8))],
                            ins=[ccin_d[:, :].opt()], outs=[ccout_d[:, :].opt()]), "cc")
                        gp.wait_ge(sems["cc"], c)
                        inc(gp.dma_start(out=cntg[:, :], in_=ccout_d[0:8, :]), "dG", 16)
                        M["ccd"] = inc(gp.dma_start(out=wsumg[:, :], in_=ccout_d[8:16, :]), "dG", 16)
                    SC.add("gpsimd", gp_cc)

                    def ve_wm(ve):
                        wt(ve, "dG", M["ccd"])
                        v = inc(ve.tensor_scalar_max(wmaxg[:, :], cntg[:, :], 1.0), "sV")
                        ve.wait_ge(sems["sV"], v)
                        v = inc(ve.reciprocal(wmaxg[:, :], wmaxg[:, :]), "sV")
                        ve.wait_ge(sems["sV"], v)
                        v = inc(ve.tensor_tensor(wmean[:, :], wsumg[:, :], wmaxg[:, :], ALU.mult), "sV")
                        ve.wait_ge(sems["sV"], v)
                        wt(ve, "sA", M[("selTe", NTD - 1)])
                        M["coef"] = inc(ve.tensor_scalar_mul(coefT[:, :], selT[:, :],
                                                             wmean[:, :]), "sV")
                    SC.add("vector", ve_wm)

                    def gp_coef(gp):
                        wt(gp, "sV", M["coef"])
                        MW["coefd"] = inc(gp.dma_start(out=coefscr_d[:, :], in_=coefT[:, :]), "dG", 16)
                    SC.add("gpsimd", gp_coef)

                    def sy_endD(sy):
                        wt(sy, "dS", cnt["dS"])
                    SC.add("sync", sy_endD)

                    def gp_endD(gp):
                        wt(gp, "dG", cnt["dG"])
                    SC.add("gpsimd", gp_endD)
                    SC.flush(blk)

            if stage == "xln2":
                with nc.Block("phDbg2") as blk:
                    def sy_d3(sy):
                        d = inc(sy.dma_start(out=ydbgb_d[:, :], in_=xln[:, :]), "dS", 16)
                        sy.wait_ge(sems["dS"], d)
                    blk.sync(sy_d3)
                return nc

            if stage == "d":
                with nc.Block("phOutD") as blk:
                    def sy_d2(sy):
                        for hc in range(NCH):
                            inc(sy.dma_start(out=y_d[hc * P:(hc + 1) * P, :],
                                             in_=acc[:, hc * TC:(hc + 1) * TC]), "dS", 16)
                        wt(sy, "dS", cnt["dS"])
                    blk.sync(sy_d2)
                return nc

            with ExitStack() as stE:
                EE = stE.enter_context
                NTE2 = 4
                h_p = EE(nc.sbuf_tensor("h_p", [P, 2 * 4 * 512], BF16))
                sil = EE(nc.sbuf_tensor("sil", [P, 2 * 512], BF16))
                sp_ = EE(nc.sbuf_tensor("sp_", [P, 2 * 512], BF16))
                bc = EE(nc.sbuf_tensor("bc", [P, 2 * 512], BF16))
                g_ps = EE(nc.psum_tensor("g_ps", [P, 1024], F32))
                u_ps = EE(nc.psum_tensor("u_ps", [P, 1024], F32))
                down_ps = EE(nc.psum_tensor("down_ps", [P, 2048], F32))

                M = {}
                last_g = [0, 0]
                last_u = [0, 0]
                last_sil = [0, 0]
                last_sp = [0, 0]
                last_bc = [0, 0]
                last_dn_ic = [0, 0, 0, 0]
                last_acc = 0

                with nc.Block("phE", no_gpsimd_drain=True) as blk:
                    SC = Sched()
                    actload and SC.add("scalar", lambda se: act_load(se, "silu_and_others"))

                    NG = NEXP * NTE2

                    def emit_bc(g):
                        e, it = g // NTE2, g % NTE2
                        t0 = it * 512
                        par = it % 2
                        if e >= 8:
                            return

                        def se_bc(se, e=e, it=it, t0=t0, par=par, _lbc=last_bc[par]):
                            wt(se, "dG", MW["coefd"])
                            wt(se, "sV", _lbc)
                            sb = "bc%d" % par
                            inc(se.dma_start(
                                out=bc[:, par * 512:(par + 1) * 512],
                                in_=coefscr_d[e, t0:t0 + 512][None, :].broadcast_to([P, 512])), sb, 16)
                            M[("bcd", e, it)] = cnt[sb]
                        SC.add("scalar", se_bc)

                    def emit_gu(g, ics=(0, 1, 2, 3)):
                        e, it = g // NTE2, g % NTE2
                        t0 = it * 512
                        par = it % 2
                        so = we_slot(e)
                        for ic in ics:
                            sl = ic % 2

                            def pe_gu(pe, e=e, it=it, t0=t0, ic=ic, so=so, sl=sl,
                                      _lg=last_g[sl], _lu=last_u[sl]):
                                if it == 0 and ic == 0:
                                    wt(pe, "sG", MW[e])
                                wt(pe, "sA", _lg)
                                wt(pe, "sV", _lu)
                                for kc in range(NCH):
                                    pe.matmul(g_ps[:, sl * 512:(sl + 1) * 512],
                                              wbuf[:, so + kc * 512 + ic * P: so + kc * 512 + (ic + 1) * P],
                                              xslc(kc, t0, 512), start=(kc == 0), stop=(kc == NCH - 1))
                                for kc in range(NCH):
                                    mm = pe.matmul(u_ps[:, sl * 512:(sl + 1) * 512],
                                                   wbuf[:, so + 4096 + kc * 512 + ic * P: so + 4096 + kc * 512 + (ic + 1) * P],
                                                   xslc(kc, t0, 512), start=(kc == 0), stop=(kc == NCH - 1))
                                M[("gu", e, it, ic)] = inc(mm, "sP")
                            SC.add("tensor", pe_gu)

                            def se_sil(se, e=e, it=it, ic=ic, sl=sl, _ls=last_sil[sl]):
                                wt(se, "sP", M[("gu", e, it, ic)])
                                wt(se, "sV", _ls)
                                M[("sil", e, it, ic)] = inc(
                                    se.activation(sil[:, sl * 512:(sl + 1) * 512],
                                                  g_ps[:, sl * 512:(sl + 1) * 512], AF.Silu), "sA")
                            SC.add("scalar", se_sil)
                            last_g[sl] = M[("sil", e, it, ic)]

                            def ve_h(ve, e=e, it=it, ic=ic, par=par, sl=sl,
                                     _lsp=last_sp[sl], _ldn=last_dn_ic[ic]):
                                wt(ve, "sA", M[("sil", e, it, ic)])
                                if e < 8:
                                    wt(ve, "bc%d" % par, M[("bcd", e, it)])
                                    wt(ve, "sV", _lsp)
                                    v = inc(ve.tensor_tensor(sp_[:, sl * 512:(sl + 1) * 512],
                                                             sil[:, sl * 512:(sl + 1) * 512],
                                                             bc[:, par * 512:(par + 1) * 512],
                                                             ALU.mult), "sV")
                                    M[("sp", e, it, ic)] = v
                                    ve.wait_ge(sems["sV"], v)
                                    srcm = sp_[:, sl * 512:(sl + 1) * 512]
                                else:
                                    srcm = sil[:, sl * 512:(sl + 1) * 512]
                                wt(ve, "sP", _ldn)
                                ho = (it % 2) * 2048
                                v = inc(ve.tensor_tensor(h_p[:, ho + ic * 512: ho + (ic + 1) * 512],
                                                         srcm, u_ps[:, sl * 512:(sl + 1) * 512],
                                                         ALU.mult), "sV")
                                M[("hp", e, it, ic)] = v
                            SC.add("vector", ve_h)
                            last_sil[sl] = M[("hp", e, it, ic)]
                            last_u[sl] = M[("hp", e, it, ic)]
                            if e < 8:
                                last_sp[sl] = M[("hp", e, it, ic)]
                                last_bc[par] = M[("sp", e, it, ic)]

                    def emit_dn(g, h2):
                        e, it = g // NTE2, g % NTE2
                        t0 = it * 512
                        so = we_slot(e)

                        def pe_dn(pe, e=e, it=it, so=so, h2=h2, _la=last_acc):
                            if h2 == 0:
                                wt(pe, "sV", _la)
                            else:
                                wt(pe, "sV", M[("acc", e, it, 0)])
                            ho = (it % 2) * 2048
                            for ic in range(4):
                                if h2 == 0:
                                    wt(pe, "sV", M[("hp", e, it, ic)])
                                for j in range(4):
                                    hc = h2 * 4 + j
                                    mm = pe.matmul(
                                        down_ps[:, j * 512:(j + 1) * 512],
                                        wbuf[:, so + 8192 + ic * DM + hc * P: so + 8192 + ic * DM + (hc + 1) * P],
                                        h_p[:, ho + ic * 512: ho + (ic + 1) * 512],
                                        start=(ic == 0), stop=(ic == 3))
                            M[("dn", e, it, h2)] = inc(mm, "sP")
                        SC.add("tensor", pe_dn)
                        if h2 == 1:
                            for ic in range(4):
                                last_dn_ic[ic] = M[("dn", e, it, 1)]

                        def ve_acc(ve, e=e, it=it, t0=t0, h2=h2):
                            wt(ve, "sP", M[("dn", e, it, h2)])
                            a3 = acc[:, :].rearrange("p (c n) -> p c n", c=NCH)[
                                :, h2 * 4:(h2 + 1) * 4, t0:t0 + 512]
                            d3 = down_ps[:, :].rearrange("p (c n) -> p c n", c=4)
                            M[("acc", e, it, h2)] = inc(
                                ve.tensor_tensor(a3, a3, d3, ALU.add), "sV")
                        SC.add("vector", ve_acc)

                    def emit_wn(e):
                        emit_wexp(SC, e + 2, "sP", M[("dn", e, NTE2 - 1, 1)])

                    emit_bc(0)
                    emit_gu(0)
                    for g in range(NG):
                        if g + 1 < NG:
                            emit_bc(g + 1)
                            emit_gu(g + 1, (0, 1))
                        emit_dn(g, 0)
                        if g + 1 < NG:
                            emit_gu(g + 1, (2, 3))
                        emit_dn(g, 1)
                        last_acc = M[("acc", g // NTE2, g % NTE2, 1)]
                        if g % NTE2 == NTE2 - 1 and g // NTE2 + 2 < NEXP:
                            emit_wn(g // NTE2)

                    def sy_out(sy):
                        for it in range(NTE2):
                            wt(sy, "sV", M[("acc", NEXP - 1, it, 1)])
                        for hc in range(NCH):
                            inc(sy.dma_start(out=y_d[hc * P:(hc + 1) * P, :],
                                             in_=acc[:, hc * TC:(hc + 1) * TC]), "dS", 16)
                        wt(sy, "dS", cnt["dS"])
                    SC.add("sync", sy_out)

                    def gp_endE(gp):
                        wt(gp, "dG", cnt["dG"])
                    SC.add("gpsimd", gp_endE)
                    SC.flush(blk)

    return nc



def _prep_inputs(L0_attn_x, L0_attn_qkv_w, L0_attn_qkv_b, L0_attn_out_w, L0_attn_out_b,
                 L0_ffn_gate_w, L0_ffn_lb_bias, L0_ffn_expert_gate_w, L0_ffn_expert_up_w,
                 L0_ffn_expert_down_w, L0_ffn_shared_gate_w, L0_ffn_shared_up_w,
                 L0_ffn_shared_down_w):
    x = np.asarray(L0_attn_x, dtype=np.float32)
    Bb, N, D = x.shape
    xd = np.ascontiguousarray(x.reshape(Bb, N // 2, 2, D).transpose(0, 2, 1, 3))

    shared = {
        "qkvT": np.ascontiguousarray(np.asarray(L0_attn_qkv_w, np.float32).T),
        "outwT": np.ascontiguousarray(np.asarray(L0_attn_out_w, np.float32).T),
        "qkvb": np.ascontiguousarray(np.asarray(L0_attn_qkv_b, np.float32).reshape(1, 3 * DM)),
        "outb": np.ascontiguousarray(np.asarray(L0_attn_out_b, np.float32).reshape(1, DM)),
        "gwT": np.ascontiguousarray(np.asarray(L0_ffn_gate_w, np.float32).T),
        "lbbc": np.ascontiguousarray(np.broadcast_to(
            np.asarray(L0_ffn_lb_bias, np.float32).reshape(1, 8), (P, 8))),
        "egwT": np.ascontiguousarray(np.asarray(L0_ffn_expert_gate_w, np.float32).transpose(0, 2, 1)),
        "euwT": np.ascontiguousarray(np.asarray(L0_ffn_expert_up_w, np.float32).transpose(0, 2, 1)),
        "edwT": np.ascontiguousarray(np.asarray(L0_ffn_expert_down_w, np.float32).transpose(0, 2, 1)),
        "sgwT": np.ascontiguousarray(np.asarray(L0_ffn_shared_gate_w, np.float32).T),
        "suwT": np.ascontiguousarray(np.asarray(L0_ffn_shared_up_w, np.float32).T),
        "sdwT": np.ascontiguousarray(np.asarray(L0_ffn_shared_down_w, np.float32).T),
        "idbf": np.eye(P, dtype=BFD),
        "idf": np.eye(P, dtype=np.float32),
    }
    selpk = np.zeros((P, P), dtype=BFD)
    for pc in range(8):
        for d in range(P):
            h = 2 * pc + (1 if d >= 64 else 0)
            selpk[d, pc * 16 + h] = 1.0
    shared["selpk"] = selpk

    def band(nq):
        r = np.arange(nq)[:, None]
        j = np.arange(P)[None, :]
        return (j >= r) & (j <= r + 16)

    in_maps = []
    halfN = N // 2
    for c in range(8):
        b = c >> 2
        par = (c >> 1) & 1
        half = c & 1
        start = half * TC
        lo, hi = start - 8, start + 2168
        xs = np.zeros((EXT, DM), dtype=np.float32)
        s0, s1 = max(lo, 0), min(hi, halfN)
        xs[s0 - lo: s1 - lo, :] = xd[b, par, s0:s1, :]

        mfirst = band(112)
        if half == 0:
            mfirst = mfirst & (np.arange(P)[None, :] >= 8)
        mlast = band(32)
        if half == 1:
            mlast = mlast & (np.arange(P)[None, :] <= 39)
        m = dict(shared)
        m["xin"] = np.ascontiguousarray(xs.T)
        m["mfirst"] = mfirst.astype(BFD)
        m["mmid"] = band(112).astype(BFD)
        m["mlast"] = mlast.astype(BFD)
        in_maps.append(m)
    return in_maps, (Bb, N, D)


def kernel(**inputs):
    in_maps, (Bb, N, D) = _prep_inputs(**inputs)
    nc = build_nc("full")
    res = run_bass_kernel_spmd(nc, in_maps, core_ids=list(range(8)))
    out = np.zeros((Bb, 2, N // 2, D), dtype=np.float32)
    for c in range(8):
        b = c >> 2
        par = (c >> 1) & 1
        half = c & 1
        y = np.asarray(res.results[c]["y"])
        out[b, par, half * TC:(half + 1) * TC, :] = y.T
    return np.ascontiguousarray(out.transpose(0, 2, 1, 3).reshape(Bb, N, D))
